# version 16
# baseline (speedup 1.0000x reference)
import numpy as np
import concourse.bass as bass
import concourse.mybir as mybir

F32 = mybir.dt.float32
AF = mybir.ActivationFunctionType
ALU = mybir.AluOpType
AX = mybir.AxisListType

D = 1024
INNER = 512
HID = 1408
E = 8
TOK = 128
T = 2 * TOK
ND = D // 128
NCI = INNER // 128
NH = HID // 128
LC = 16
NJ = TOK // LC
EPS = 1e-5
BIG = 1e9
ENGS = ("te", "dve", "act", "gps", "sync")
SEM_NAMES = ("te", "dve", "act", "gps", "dw", "dx", "dst", "dwa", "dwb", "dwc", "dwd")


class PSem:
    def __init__(self, name):
        self.name = name
        self.h = None
        self.n = 0


class Planner:
    def __init__(self):
        self.ops = {e: [] for e in ENGS}
        self.sems = {k: PSem(k) for k in SEM_NAMES}
        self.waited = {e: {} for e in ENGS}
        self.bank_free = [("te", 0)] * 8
        self.bank_rr = 0
        self.dirty = {e: False for e in ENGS}

    AUTOB = ("dve", "act", "gps")

    def do(self, eng, f, _raw=False):
        if _raw or eng not in self.AUTOB:
            self.ops[eng].append(f)
            return
        if self.dirty[eng]:
            self.dirty[eng] = False
            self.wait(eng, eng)
        s = self.sems[eng]
        s.n += 1
        def g(e, f=f, s=s):
            ins = f(e)
            ins.then_inc(s.h, 1)
            return ins
        self.ops[eng].append(g)
        self.dirty[eng] = True

    def bump(self, eng, sem_name, amt=1):
        if eng in self.AUTOB and sem_name == eng:
            return self.sems[sem_name].n
        s = self.sems[sem_name]
        s.n += amt
        last = self.ops[eng][-1]
        def g(e, last=last, s=s, amt=amt):
            ins = last(e)
            ins.then_inc(s.h, amt)
            return ins
        self.ops[eng][-1] = g
        return s.n

    def wait(self, eng, sem_name, val=None, force=False):
        s = self.sems[sem_name]
        val = s.n if val is None else val
        if val <= 0:
            return
        if not force and self.waited[eng].get(sem_name, -1) >= val:
            return
        assert val <= s.n, f"{eng} waits {sem_name}>={val} > planned {s.n}"
        self.waited[eng][sem_name] = val
        self.do(eng, lambda e, s=s, val=val: e.wait_ge(s.h, val), _raw=True)

    def dma(self, eng, out, in_, sem="dw"):
        if eng == "gps":
            self.wait(eng, sem, self.sems[sem].n)
        self.do(eng, lambda e, out=out, in_=in_: e.dma_start(out=out, in_=in_), _raw=True)
        return self.bump(eng, sem, 16)

    def take_banks(self, n):
        out = []
        tries = 0
        while len(out) < n:
            b = self.bank_rr % 8
            self.bank_rr += 1
            tries += 1
            assert tries < 64, "bank allocator livelock (too many held banks)"
            if b in getattr(self, "held", set()) or b in out:
                continue
            sname, v = self.bank_free[b]
            self.wait("te", sname, v)
            out.append(b)
        return out

    def hold_banks(self, banks):
        if not hasattr(self, "held"):
            self.held = set()
        self.held |= set(banks)

    def unhold_banks(self, banks):
        self.held -= set(banks)

    def release_bank(self, b, sem_name):
        self.bank_free[b] = (sem_name, self.sems[sem_name].n)

    def guard(self, eng):
        self.bump(eng, eng)
        self.wait(eng, eng)


def build(rows_per_core=4096, n_chunks=None, tap_names=(), newton_iters=4):
    p = Planner()
    nc = bass.Bass("TRN2", target_bir_lowering=False, debug=False)
    NCH = (rows_per_core // T) if n_chunks is None else n_chunks
    SEQ = rows_per_core // 2

    def din(name, shape):
        return nc.dram_tensor(name, shape, F32, kind="ExternalInput").ap()
    X = din("x", [rows_per_core, D])
    WA = din("w_a", [2 * INNER, D]); WXm = din("w_x", [2 * INNER, D])
    WOUT = din("w_out", [D, INNER])
    MVR = din("mat_v_real", [INNER, INNER]); MVI = din("mat_v_imag", [INNER, INNER])
    H0R = din("h0_re", [INNER]); H0I = din("h0_im", [INNER])
    GW = din("gate_w", [E, D]); GBB = din("gate_bb", [128, E])
    W1 = din("w1", [E, D, HID]); B1 = din("b1", [E, HID])
    W3 = din("w3", [E, D, HID]); B3 = din("b3", [E, HID])
    W2 = din("w2", [E, HID, D]); B2 = din("b2", [E, D])
    SW1 = din("sw1", [D, HID]); SB1 = din("sb1", [HID])
    SW3 = din("sw3", [D, HID]); SB3 = din("sb3", [HID])
    SW2 = din("sw2", [HID, D]); SB2 = din("sb2", [D])
    EYE = din("eye512", [INNER, INNER])
    I128D = din("i128", [128, 128])
    IOTA8 = din("iota8", [128, E])
    ONES = din("ones", [1, 128])
    OUT = nc.dram_tensor("out", [rows_per_core, D], F32, kind="ExternalOutput").ap()
    BF = mybir.dt.bfloat16
    WAT_HBM = nc.dram_tensor("waT_hbm", [D, 4 * INNER], BF, kind="Internal").ap()
    WXT_HBM = nc.dram_tensor("wxT_hbm", [D, 4 * INNER], BF, kind="Internal").ap()
    W1P = nc.dram_tensor("w1p_hbm", [9, NH, 128, 2 * D], BF, kind="Internal").ap()
    W3P = nc.dram_tensor("w3p_hbm", [9, NH, 128, 2 * D], BF, kind="Internal").ap()
    W2P = nc.dram_tensor("w2p_hbm", [9, NH, 128, 2 * D], BF, kind="Internal").ap()
    WOUTP = nc.dram_tensor("woutp_hbm", [INNER, 2 * D], BF, kind="Internal").ap()

    TAPCOLS = {"Zr": NCI * INNER, "Zi": NCI * INNER, "ln1": 2 * D, "afac": NCI * T,
               "aim": NCI * T, "xcre": NCI * T, "xcim": NCI * T, "ure": NCI * T,
               "uim": NCI * T, "sre": NCI * T, "sim": NCI * T, "hre": NCI * T,
               "x1": 2 * D, "ln2": 2 * D, "rww": 2 * E, "hffn0": NH * T, "xout": 2 * D}
    taps = {}
    for name in tap_names:
        taps[name] = nc.dram_tensor("tap_" + name, [128, TAPCOLS[name]], F32,
                                    kind="ExternalOutput").ap()

    def tapdma(name, src, eng_dep="dve"):
        if name not in taps:
            return
        p.wait("gps", eng_dep, p.sems[eng_dep].n)
        st = p.dma("gps", taps[name], src, sem="dst")
        p.wait("dve", "dst", st)
        p.wait("act", "dst", st)
        p.wait("te", "dst", st)

    def S(name, cols, dtype=F32):
        return nc.alloc_sbuf_tensor(name, [128, cols], dtype).ap()

    i128 = S("c_i128", 128)
    iota8 = S("c_iota8", E)
    gbb = S("c_gbb", E)
    ones = S("c_ones", 128)
    h0 = S("c_h0", 2 * NCI)
    gwT = S("r_gwT", ND * E)
    gwTp = S("r_gwTp", ND * E)
    b1s = S("r_b1", 9 * NH); b3s = S("r_b3", 9 * NH)
    b2sb = S("r_b2sb", D)
    rwf9 = S("r_rwf9", 2 * TOK)
    woutT = S("r_woutT", NCI * D)
    Zr = S("r_Zr", NCI * INNER); Zi = S("r_Zi", NCI * INNER); ZrN = S("r_ZrN", NCI * INNER)
    vT_r = S("r_vTr", NCI * INNER); vT_iN = S("r_vTiN", NCI * INNER)

    pool = S("pool", 11264)
    watb = S("watb", 4096)
    wbuf = S("wbuf", NH * D)
    x_tm = S("x_tm", 2 * D)
    xn_tm = S("xn_tm", 2 * D)
    scr = S("scr", 2 * D)
    scr2 = S("scr2", D)
    tmps = [S(f"tmp{i}", 128) for i in range(6)]
    gtmps = [S(f"gtmp{i}", 128) for i in range(4)]
    small = S("small", 64)
    carry_re = S("carry_re", 2 * 2 * NCI); carry_iN = S("carry_iN", 2 * 2 * NCI)
    scb_re = S("scb_re", 2 * NCI); scb_im = S("scb_im", 2 * NCI)
    scar_r = S("scar_r", NCI * 2 * NJ); scar_i = S("scar_i", NCI * 2 * NJ)
    gate = S("gate", 2 * 48)
    dumm = S("dumm", 64)
    rww = S("rww", 2 * 2 * E)

    def Rg(start, cols, t=pool):
        return t[:, start:start + cols]
    xn1_fm = Rg(0, ND * T)
    xr_re = Rg(0, NCI * T); xr_iN = Rg(NCI * T, NCI * T)
    a_re = scr[:, 0:NCI * T]; a_im = scr[:, NCI * T:2 * NCI * T]
    xn2_fm = Rg(2048, ND * T)
    xc_re = Rg(4096, NCI * T); xc_im = Rg(4096 + NCI * T, NCI * T)
    u_re = Rg(6144, NCI * T); u_im = Rg(6144 + NCI * T, NCI * T)
    P_re = Rg(8192, NCI * T); P_im = Rg(8192 + NCI * T, NCI * T)
    h_re = Rg(10240, NCI * T)
    hffn = woutT[:, 0:NH * T]
    hmul = S("hmul4", 4 * T)
    xn2_b = wbuf[:, 6144:8192]
    x_tm2 = wbuf[:, 8960:11008]
    v_r = Rg(0, 2048); v_iN = Rg(2048, 2048); vT_i = Rg(4096, 2048)
    Xr = Rg(6144, 2048); Xi = Rg(8192, 2048)
    Rr = x_tm[:, 0:2048]; Ri = xn_tm[:, 0:2048]; RiN = Rg(0, 2048, scr)
    eye = Rg(0, 2048, watb)
    BF16 = mybir.dt.bfloat16
    def bview(reg, half):
        return reg.bitcast(BF16)[:, half * 2048:(half + 1) * 2048]
    Xbr, Xbi = bview(Xr, 0), bview(Xr, 1)
    Zbr, Zbi = bview(Xi, 0), bview(Xi, 1)
    Rbr, Rbi = bview(Rr, 0), bview(Rr, 1)
    RbiN = bview(Ri, 0)
    vTbr, vTbi = bview(v_r, 0), bview(v_r, 1)
    vTbiN = bview(v_iN, 0)
    Xr_p = v_iN.bitcast(F32) if False else None
    wstage = watb

    pb = [nc.alloc_psum_tensor(f"pb{i}", [128, 512], F32).ap() for i in range(8)]

    def bpair(reg):
        n = reg.shape[1]
        bc = reg.bitcast(mybir.dt.bfloat16)
        return bc[:, 0:n], bc[:, n:2 * n]

    def hpair(hf):
        bc = hffn.bitcast(mybir.dt.bfloat16)
        return bc[:, hf * 2 * T: hf * 2 * T + T], bc[:, hf * 2 * T + T: hf * 2 * T + 2 * T]

    def mm3(terms):
        out = []
        for (lp, rp) in terms:
            lh, ll = lp
            rh, rl = rp
            out += [(lh, rh), (lh, rl), (ll, rh)]
        return out

    def psub(lhs_terms_p, lhs_terms_q, cols, evac):
        bP, bQ = p.take_banks(2)
        mm_group(bP, cols, lhs_terms_p)
        mm_group(bQ, cols, lhs_terms_q)
        tv = p.bump("te", "te")
        p.wait("dve", "te", tv)
        p.do("dve", lambda v, bP=bP, cols=cols: v.tensor_copy(scr2[:, 0:cols], pb[bP][:, 0:cols]))
        p.do("dve", lambda v, bQ=bQ, cols=cols: evac(v, scr2[:, 0:cols], pb[bQ]))
        p.bump("dve", "dve")
        p.release_bank(bP, "dve"); p.release_bank(bQ, "dve")

    xrh_re, xrl_re = bpair(xr_re)
    xrh_iN, xrl_iN = bpair(xr_iN)

    def prtile(pr, kc, mc):
        h, l = pr
        return (mtile(h, kc, mc), mtile(l, kc, mc))

    def xrpair(pr, kc):
        h, l = pr
        return (h[:, kc * T:(kc + 1) * T], l[:, kc * T:(kc + 1) * T])

    hreh, hrel = bpair(h_re)

    def mtile(msb, kc, mc):
        return msb[:, kc * INNER + mc * 128: kc * INNER + (mc + 1) * 128]

    def q5(ap):
        return ap.rearrange("p (c s j u) -> p c s j u", c=NCI, s=2, j=NJ)

    def q4(ap):
        return ap.rearrange("p (c s u) -> p c s u", c=NCI, s=2)

    def qcs(ap):
        return ap.rearrange("p (c s) -> p c s", c=NCI)

    p.dma("sync", i128, I128D)
    p.dma("sync", iota8, IOTA8)
    p.dma("sync", gbb, GBB)
    p.dma("sync", ones[0:1, :], ONES)
    p.dma("sync", h0[:, 0:NCI] if False else scr2[0:NCI, 0:128],
          H0R.rearrange("(c p) -> c p", p=128))
    p.dma("sync", scr2[32:32 + NCI, 0:128], H0I.rearrange("(c p) -> c p", p=128))
    p.dma("sync", b2sb[0:E, :], B2)
    p.dma("sync", b2sb[E:E + 1, :], SB2.unsqueeze(0))
    p.dma("sync", rwf9[E:E + 1, 0:TOK], ONES)
    p.dma("sync", rwf9[E:E + 1, TOK:2 * TOK], ONES)
    p.dma("sync", v_r.rearrange("p (c j) -> p c j", c=NCI),
          MVR.rearrange("(c p) j -> p c j", p=128))
    p.dma("sync", v_iN.rearrange("p (c j) -> p c j", c=NCI),
          MVI.rearrange("(c p) j -> p c j", p=128))
    p.dma("sync", scr[0:E, 0:D], GW)
    ld0 = p.dma("sync", wstage.rearrange("p (o j) -> p o j", o=ND),
                WOUT.rearrange("(o p) j -> p o j", p=128))

    p.wait("dve", "dw", ld0)
    p.do("dve", lambda v: v.memset(scb_re, 0.0))
    p.do("dve", lambda v: v.memset(scb_im, 0.0))
    p.bump("dve", "dve")

    def transp(src_ap, *evacs):
        kp, m = src_ap.shape[0], src_ap.shape[1]
        bp = src_ap.base_partition()
        b = p.take_banks(1)[0]
        p.do("te", lambda t, s=src_ap, b=b, kp=kp, m=m, bp=bp:
             t.transpose(pb[b][0:m, 0:kp], s, i128[bp:bp + kp, bp:bp + kp]))
        tv = p.bump("te", "te")
        p.wait("dve", "te", tv)
        for ev in evacs:
            p.do("dve", lambda v, b=b, ev=ev: ev(v, pb[b]))
        p.bump("dve", "dve")
        p.release_bank(b, "dve")

    p.wait("te", "dw", ld0)
    for (src, dst) in ((v_r, vT_r), (v_iN, vT_i)):
        for cj in range(NCI):
            for ci in range(NCI):
                transp(src[:, ci * INNER + cj * 128: ci * INNER + (cj + 1) * 128],
                       lambda v, ps, dst=dst, ci=ci, cj=cj:
                       v.tensor_copy(dst[:, cj * INNER + ci * 128: cj * INNER + (ci + 1) * 128],
                                     ps[:, 0:128]))
    p.do("dve", lambda v: v.tensor_scalar_mul(vT_iN, vT_i, -1.0))
    p.do("dve", lambda v: v.tensor_scalar_mul(v_iN, v_iN, -1.0))
    p.bump("dve", "dve")

    for c in range(NCI):
        for o in range(ND):
            transp(wstage[:, o * INNER + c * 128: o * INNER + (c + 1) * 128],
                   lambda v, ps, c=c, o=o:
                   v.tensor_copy(woutT[:, c * D + o * 128: c * D + (o + 1) * 128], ps[:, 0:128]))
    for c in range(ND):
        transp(scr[0:E, c * 128:(c + 1) * 128],
               lambda v, ps, c=c: v.tensor_copy(gwT[:, c * E:(c + 1) * E], ps[:, 0:E]))
    transp(scr2[0:NCI, 0:128],
           lambda v, ps: v.tensor_copy(h0[:, 0:NCI], ps[:, 0:NCI]))
    transp(scr2[32:32 + NCI, 0:128],
           lambda v, ps: v.tensor_copy(h0[:, NCI:2 * NCI], ps[:, 0:NCI]))
    p.do("dve", lambda v: v.tensor_copy(
        qcs(carry_re[:, 0:2 * NCI]), h0[:, 0:NCI].unsqueeze(2).to_broadcast([128, NCI, 2])))
    p.do("dve", lambda v: v.tensor_scalar(
        qcs(carry_iN[:, 0:2 * NCI]), h0[:, NCI:2 * NCI].unsqueeze(2).to_broadcast([128, NCI, 2]),
        -1.0, None, ALU.mult))
    p.bump("dve", "dve")
    for (BB, SBB, dst) in ((B1, SB1, b1s), (B3, SB3, b3s)):
        p.wait("sync", "dve", p.sems["dve"].n)
        p.dma("sync", scr[0:E, 0:HID], BB)
        ldb = p.dma("sync", scr[E:E + 1, 0:HID], SBB.unsqueeze(0))
        p.wait("te", "dw", ldb)
        for hc in range(NH):
            transp(scr[0:9, hc * 128:(hc + 1) * 128],
                   lambda v, ps, hc=hc, dst=dst: v.tensor_copy(
                       dst.rearrange("p (e h) -> p e h", e=9)[:, :, hc], ps[:, 0:9]))

    for (W_in, W_hbm) in ((WA, WAT_HBM), (WXm, WXT_HBM)):
        p.wait("sync", "te", p.sems["te"].n)
        p.wait("sync", "dve", p.sems["dve"].n)
        ld = p.dma("sync", wbuf[:, 0:ND * D].rearrange("p (o d) -> p o d", o=ND),
                   W_in.rearrange("(o p) d -> p o d", p=128))
        p.wait("te", "dw", ld)
        scb = scr.bitcast(mybir.dt.bfloat16)
        for dc in range(ND):
            for o in range(ND):
                b = p.take_banks(1)[0]
                p.do("te", lambda t, o=o, dc=dc, b=b:
                     t.transpose(pb[b][:, 0:128],
                                 wbuf[:, o * D + dc * 128: o * D + (dc + 1) * 128], i128))
                tv = p.bump("te", "te")
                p.wait("dve", "te", tv)
                p.do("dve", lambda v, b=b, o=o:
                     v.tensor_copy(scb[:, o * 64: o * 64 + 64], pb[b][:, 0:128:2]))
                p.do("dve", lambda v, b=b, o=o:
                     v.tensor_copy(scb[:, 512 + o * 64: 512 + o * 64 + 64], pb[b][:, 1:128:2]))
                p.guard("dve")
                p.do("dve", lambda v, b=b, o=o:
                     v.tensor_sub(scb[:, 1024 + o * 64: 1024 + o * 64 + 64], pb[b][:, 0:128:2],
                                  scb[:, o * 64: o * 64 + 64]))
                p.do("dve", lambda v, b=b, o=o:
                     v.tensor_sub(scb[:, 1536 + o * 64: 1536 + o * 64 + 64], pb[b][:, 1:128:2],
                                  scb[:, 512 + o * 64: 512 + o * 64 + 64]))
                p.bump("dve", "dve")
                p.release_bank(b, "dve")
            p.wait("gps", "dve", p.sems["dve"].n)
            st = p.dma("gps", W_hbm[dc * 128:(dc + 1) * 128, :], scb[:, 0:2048], sem="dst")
            p.wait("dve", "dst", st)
    watw_done = p.sems["dst"].n

    def repack(Wsrcs, Wdst, colblocks):
        p.wait("sync", "dw", p.sems["dw"].n)
        stg = watb[:, 2048:4096].bitcast(mybir.dt.bfloat16)
        cdim = ND if colblocks else NH
        for e in range(9):
            Ws = Wsrcs[0][e] if e < E else Wsrcs[1]
            p.wait("sync", "dst", p.sems["dst"].n)
            p.wait("sync", "gps", p.sems["gps"].n)
            ldw = p.dma("sync", wbuf.rearrange("p (c h) -> p c h", c=cdim),
                        Ws.rearrange("(c p) h -> p c h", p=128))
            p.wait("gps", "dw", ldw)
            for hf in range(NH):
                if colblocks:
                    blk = wbuf.rearrange("p (c h) -> p c h", c=ND)[:, :, hf * 128:(hf + 1) * 128]
                    sl = stg[:, 0:2048].rearrange("p (x c q) -> p x c q", x=2, c=ND)
                    hi, lo = sl[:, 0], sl[:, 1]
                else:
                    blk = wbuf[:, hf * D:(hf + 1) * D]
                    hi, lo = stg[:, 0:1024], stg[:, 1024:2048]
                p.wait("gps", "dst", p.sems["dst"].n)
                p.do("gps", lambda g, blk=blk, hi=hi: g.tensor_copy(hi, blk))
                p.guard("gps")
                p.do("gps", lambda g, blk=blk, hi=hi, lo=lo: g.tensor_tensor(lo, blk, hi, ALU.subtract))
                p.bump("gps", "gps")
                p.wait("gps", "gps", p.sems["gps"].n)
                p.dma("gps", Wdst[e, hf], stg[:, 0:2048] if not colblocks else None, sem="dst") \
                    if False else None
                p.dma("gps", Wdst[e, hf], stg[:, 0:2048], sem="dst")

    p.wait("sync", "dst", watw_done)
    ld_eye = p.dma("sync", eye.rearrange("p (c j) -> p c j", c=NCI),
                   EYE.rearrange("(c p) j -> p c j", p=128))
    repack((W1, SW1), W1P, True)
    repack((W3, SW3), W3P, True)
    repack((W2, SW2), W2P, False)
    repack_done = p.sems["dst"].n
    p.wait("dve", "dw", ld_eye)
    p.do("dve", lambda v: v.tensor_mul(scr[:, 0:2048], v_r, v_r))
    p.do("dve", lambda v: v.tensor_mul(scr2, v_iN[:, 0:D], v_iN[:, 0:D]))
    p.do("dve", lambda v: v.tensor_add(scr[:, 0:D], scr[:, 0:D], scr2))
    p.do("dve", lambda v: v.tensor_mul(scr2, v_iN[:, D:2 * D], v_iN[:, D:2 * D]))
    p.do("dve", lambda v: v.tensor_add(scr[:, D:2 * D], scr[:, D:2 * D], scr2))
    p.do("dve", lambda v: v.reduce_sum(small[:, 0:1], scr[:, 0:2048], axis=AX.X))
    p.do("dve", lambda v: v.memset(scr2[:, 0:1], 1.0))
    rsum_done = p.bump("dve", "dve")
    b = p.take_banks(1)[0]
    p.wait("te", "dve", rsum_done)
    p.do("te", lambda t, b=b: t.matmul(pb[b][0:1, 0:1], small[:, 0:1], scr2[:, 0:1],
                                       start=True, stop=True))
    tv = p.bump("te", "te")
    p.wait("dve", "te", tv)
    p.do("dve", lambda v, b=b: v.reciprocal(small[0:1, 2:3], pb[b][0:1, 0:1]))
    p.bump("dve", "dve")
    p.release_bank(b, "dve")
    b = p.take_banks(1)[0]
    p.wait("te", "dve")
    p.do("te", lambda t, b=b: t.matmul(pb[b][:, 0:1], ones[0:1, 0:128], small[0:1, 2:3],
                                       start=True, stop=True))
    tv = p.bump("te", "te")
    p.wait("dve", "te", tv)
    p.do("dve", lambda v, b=b: v.tensor_copy(small[:, 4:5], pb[b][:, 0:1]))
    p.bump("dve", "dve")
    p.release_bank(b, "dve")
    p.guard("dve")
    p.do("dve", lambda v: v.tensor_scalar(Xbr, vT_r, small[:, 4:5], None, ALU.mult))
    p.do("dve", lambda v: v.tensor_scalar(Xbi, vT_iN, small[:, 4:5], None, ALU.mult))
    p.do("dve", lambda v: v.tensor_scalar(Zbr, v_r, small[:, 4:5], None, ALU.mult))
    p.do("dve", lambda v: v.tensor_scalar(Zbi, v_iN, small[:, 4:5], None, ALU.mult))
    p.do("dve", lambda v: v.tensor_copy(vTbr, vT_r))
    p.do("dve", lambda v: v.tensor_copy(vTbi, vT_i))
    p.do("dve", lambda v: v.tensor_copy(vTbiN, vT_iN))
    nv = p.bump("dve", "dve")
    p.wait("te", "dve", nv)

    def mm_group(bank, out_cols, terms):
        n = len(terms)
        for i, (lt, rh) in enumerate(terms):
            p.do("te", lambda t, b=bank, lt=lt, rh=rh, i=i, n=n, oc=out_cols:
                 t.matmul(pb[b][:, 0:oc], lt, rh, start=(i == 0), stop=(i == n - 1)))

    def m512_terms(pairs, mc):
        out = []
        for (lsb, rsb) in pairs:
            for kc in range(NCI):
                out.append((mtile(lsb, kc, mc), rsb[:, kc * INNER:(kc + 1) * INNER]))
        return out

    def newton_iter(xr, xi, zr, zi, rr, ri, riN, vtr, vti, vtiN, cast):
        banks = p.take_banks(8)
        for mc in range(NCI):
            mm_group(banks[mc], INNER, m512_terms(((vtr, xr), (vtiN, xi)), mc))
            mm_group(banks[4 + mc], INNER, m512_terms(((vti, xr), (vtr, xi)), mc))
        t1v = p.bump("te", "te")
        p.wait("dve", "te", t1v)
        for mc in range(NCI):
            sl = slice(mc * INNER, (mc + 1) * INNER)
            p.do("dve", lambda v, b=banks[mc], sl=sl, rr=rr: v.tensor_sub(rr[:, sl], eye[:, sl], pb[b][:, 0:INNER]))
            p.do("dve", lambda v, b=banks[4 + mc], sl=sl, ri=ri: v.tensor_scalar_mul(ri[:, sl], pb[b][:, 0:INNER], -1.0))
            p.do("dve", lambda v, b=banks[4 + mc], sl=sl, riN=riN: v.tensor_copy(riN[:, sl], pb[b][:, 0:INNER]))
            p.bump("dve", "dve")
            p.release_bank(banks[mc], "dve"); p.release_bank(banks[4 + mc], "dve")
        p.wait("te", "dve")
        banks = p.take_banks(8)
        for mc in range(NCI):
            mm_group(banks[mc], INNER, m512_terms(((zr, rr), (zi, riN)), mc))
            mm_group(banks[4 + mc], INNER, m512_terms(((zr, ri), (zi, rr)), mc))
        dv = p.bump("te", "te")
        p.wait("dve", "te", dv)
        for mc in range(NCI):
            sl = slice(mc * INNER, (mc + 1) * INNER)
            p.do("dve", lambda v, b=banks[mc], sl=sl, xr=xr: v.tensor_add(xr[:, sl], xr[:, sl], pb[b][:, 0:INNER]))
            p.do("dve", lambda v, b=banks[4 + mc], sl=sl, xi=xi: v.tensor_add(xi[:, sl], xi[:, sl], pb[b][:, 0:INNER]))
            p.bump("dve", "dve")
            p.release_bank(banks[mc], "dve"); p.release_bank(banks[4 + mc], "dve")
        p.wait("te", "dve")
        banks = p.take_banks(8)
        for mc in range(NCI):
            mm_group(banks[mc], INNER, m512_terms(((rr, zr), (riN, zi)), mc))
            mm_group(banks[4 + mc], INNER, m512_terms(((ri, zr), (rr, zi)), mc))
        ev = p.bump("te", "te")
        p.wait("dve", "te", ev)
        for mc in range(NCI):
            sl = slice(mc * INNER, (mc + 1) * INNER)
            p.do("dve", lambda v, b=banks[mc], sl=sl, zr=zr: v.tensor_add(zr[:, sl], zr[:, sl], pb[b][:, 0:INNER]))
            p.do("dve", lambda v, b=banks[4 + mc], sl=sl, zi=zi: v.tensor_add(zi[:, sl], zi[:, sl], pb[b][:, 0:INNER]))
            p.bump("dve", "dve")
            p.release_bank(banks[mc], "dve"); p.release_bank(banks[4 + mc], "dve")
        p.wait("te", "dve")

    for it in range(34):
        newton_iter(Xbr, Xbi, Zbr, Zbi, Rbr, Rbi, RbiN, vTbr, vTbi, vTbiN, True)
    p.wait("dve", "te", p.sems["te"].n)
    p.do("dve", lambda v: v.tensor_copy(Zr, Zbr))
    p.do("dve", lambda v: v.tensor_copy(Zi, Zbi))
    p.do("dve", lambda v: v.tensor_copy(scr, Xbr))
    p.bump("dve", "dve")
    p.do("dve", lambda v: v.tensor_copy(Xi, Xbi))
    p.do("dve", lambda v: v.tensor_copy(Xr, scr))
    p.bump("dve", "dve")
    p.wait("te", "dve", p.sems["dve"].n)
    for it in range(newton_iters):
        newton_iter(Xr, Xi, Zr, Zi, Rr, Ri, RiN, vT_r, vT_i, vT_iN, False)

    tapdma("Zr", Zr); tapdma("Zi", Zi)
    zr_h, zr_l = bpair(ZrN)
    zi_h, zi_l = bpair(Zr)
    vtr_h, vtr_l = bpair(vT_iN)
    vti_h, vti_l = bpair(vT_r)
    gwh, gwl = bpair(gwTp)
    p.do("dve", lambda v: v.tensor_copy(zr_h, Zr))
    p.guard("dve")
    p.do("dve", lambda v: v.tensor_sub(zr_l, Zr, zr_h))
    p.do("dve", lambda v: v.tensor_copy(zi_h, Zi))
    p.guard("dve")
    p.do("dve", lambda v: v.tensor_sub(zi_l, Zi, zi_h))
    p.do("dve", lambda v: v.tensor_copy(vtr_h, vT_r))
    p.guard("dve")
    p.do("dve", lambda v: v.tensor_sub(vtr_l, vT_r, vtr_h))
    p.do("dve", lambda v: v.tensor_copy(vti_h, vT_i))
    p.guard("dve")
    p.do("dve", lambda v: v.tensor_sub(vti_l, vT_i, vti_h))
    p.do("dve", lambda v: v.tensor_copy(gwh, gwT))
    p.guard("dve")
    p.do("dve", lambda v: v.tensor_sub(gwl, gwT, gwh))
    p.bump("dve", "dve")
    scrb = scr.bitcast(mybir.dt.bfloat16)
    for c in range(NCI):
        p.wait("dve", "dst", p.sems["dst"].n)
        p.do("dve", lambda v, c=c: v.tensor_copy(scrb[:, 0:1024], woutT[:, c * D:(c + 1) * D]))
        p.guard("dve")
        p.do("dve", lambda v, c=c: v.tensor_sub(scrb[:, 1024:2048], woutT[:, c * D:(c + 1) * D],
                                                scrb[:, 0:1024]))
        p.bump("dve", "dve")
        p.wait("gps", "dve", p.sems["dve"].n)
        p.dma("gps", WOUTP[c * 128:(c + 1) * 128, :], scrb[:, 0:2048], sem="dst")
    p.wait("dve", "dst", p.sems["dst"].n)

    def ln(dst_fm, tapname, k, x_tm):
        p.wait("dve", "te", p.sems["te"].n)
        for j in range(2):
            p.do("dve", lambda v, j=j: v.reduce_sum(small[:, 8 + j:9 + j],
                                                    x_tm[:, j * D:(j + 1) * D], axis=AX.X))
        sv = p.bump("dve", "dve")
        p.wait("act", "dve", sv)
        for j in range(2):
            p.do("act", lambda s, j=j: s.activation(scr2, x_tm[:, j * D:(j + 1) * D], AF.Square,
                                                    accum_out=small[:, 10 + j:11 + j]))
        qv = p.bump("act", "act")
        p.wait("dve", "act", qv)
        for j in range(2):
            p.do("dve", lambda v, j=j: v.tensor_scalar_mul(small[:, 12 + j:13 + j],
                                                           small[:, 8 + j:9 + j], 1.0 / D))
            p.do("dve", lambda v, j=j: v.tensor_scalar_mul(small[:, 14 + j:15 + j],
                                                           small[:, 10 + j:11 + j], 1.0 / D))
        p.guard("dve")
        for j in range(2):
            p.do("dve", lambda v, j=j: v.tensor_scalar(small[:, 16 + j:17 + j],
                                                       small[:, 12 + j:13 + j],
                                                       small[:, 12 + j:13 + j], -1.0,
                                                       ALU.mult, ALU.mult))
        p.guard("dve")
        for j in range(2):
            p.do("dve", lambda v, j=j: v.tensor_add(small[:, 14 + j:15 + j],
                                                    small[:, 14 + j:15 + j], small[:, 16 + j:17 + j]))
            p.do("dve", lambda v, j=j: v.tensor_scalar_add(small[:, 14 + j:15 + j],
                                                           small[:, 14 + j:15 + j], EPS))
        p.guard("dve")
        for j in range(2):
            p.do("dve", lambda v, j=j: v.reciprocal(small[:, 18 + j:19 + j], small[:, 14 + j:15 + j]))
        rv = p.bump("dve", "dve")
        p.wait("act", "dve", rv)
        for j in range(2):
            p.do("act", lambda s, j=j: s.activation(small[:, 20 + j:21 + j],
                                                    small[:, 18 + j:19 + j], AF.Sqrt))
        rstdv = p.bump("act", "act")
        p.wait("dve", "act", rstdv)
        for j in range(2):
            p.do("dve", lambda v, j=j:
                 v.tensor_scalar(xn_tm[:, j * D:(j + 1) * D], x_tm[:, j * D:(j + 1) * D],
                                 small[:, 12 + j:13 + j], small[:, 20 + j:21 + j],
                                 ALU.subtract, ALU.mult))
        xnv = p.bump("dve", "dve")
        if k == 0:
            tapdma(tapname, xn_tm)
        return xnv

    def ln_tp(dst_fm, xnv):
        p.wait("te", "dve", xnv)
        dh, dl = bpair(dst_fm)
        for c in range(ND):
            for j in range(2):
                sl = slice(c * T + j * 128, c * T + (j + 1) * 128)
                transp(xn_tm[:, j * D + c * 128: j * D + (c + 1) * 128],
                       lambda v, ps, sl=sl: v.tensor_copy(dh[:, sl], ps[:, 0:128]),
                       lambda v, ps, sl=sl: v.tensor_tensor(dl[:, sl], ps[:, 0:128],
                                                            dh[:, sl], ALU.subtract))
        return p.sems["dve"].n

    def proj_and_factor(W_hbm, dst_re, dst_im, kind, key, xv, nxt):
        p.wait("te", "dve", xv)
        banks = p.take_banks(8)
        wbv = watb.bitcast(mybir.dt.bfloat16)
        x1h, x1l = bpair(xn1_fm)
        for tn in range(4):
            p.wait("te", *ws[(key, tn)])
            for m in range(8):
                for kcl in range(2):
                    kca = 2 * tn + kcl
                    base = (tn % 2) * 4096 + kcl * 2048
                    wh = wbv[:, base + m * 128: base + (m + 1) * 128]
                    wl = wbv[:, base + 1024 + m * 128: base + 1024 + (m + 1) * 128]
                    xh = x1h[:, kca * T:(kca + 1) * T]
                    xl = x1l[:, kca * T:(kca + 1) * T]
                    for ti, (lt, rh) in enumerate(((wh, xh), (wh, xl), (wl, xh))):
                        p.do("te", lambda t, b=banks[m], lt=lt, rh=rh, st=(kca == 0 and ti == 0),
                             sp=(kca == 7 and ti == 2):
                             t.matmul(pb[b][:, 0:T], lt, rh, start=st, stop=sp))
            pv = p.bump("te", "te")
            if nxt[tn] is not None:
                k2, t2, W2 = nxt[tn]
                wissue(k2, t2, W2, pv)
        p.wait("dve", "te", pv)
        fs = xn_tm
        for cc in range(NCI):
            p.do("dve", lambda v, b=banks[cc], cc=cc:
                 v.tensor_copy(fs[:, cc * 512: cc * 512 + T], pb[b][:, 0:T]))
            p.bump("dve", "dve")
            p.release_bank(banks[cc], "dve")
            p.do("dve", lambda v, b=banks[4 + cc], cc=cc:
                 v.tensor_copy(fs[:, cc * 512 + T:(cc + 1) * 512], pb[b][:, 0:T]))
            p.bump("dve", "dve")
            p.release_bank(banks[4 + cc], "dve")
        p.wait("act", "dve", p.sems["dve"].n)
        m2f = pool[:, 8192:10240]
        p.do("act", lambda s: s.activation(m2f, fs, AF.Square))
        sqv = p.bump("act", "act")
        p.wait("dve", "act", sqv)
        m2 = m2f.rearrange("p (c a u) -> p c a u", c=NCI, a=2)
        s2c = scr2.rearrange("p (c u) -> p c u", c=NCI)
        p.do("dve", lambda v: v.tensor_add(s2c, m2[:, :, 0, :], m2[:, :, 1, :]))
        m2v = p.bump("dve", "dve")
        p.wait("act", "dve", m2v)
        if kind == "a":
            p.do("act", lambda s: s.activation(m2f[:, 0:D], scr2, AF.Sqrt))
            rtv = p.bump("act", "act")
            p.wait("dve", "act", rtv)
            p.do("dve", lambda v: v.tensor_scalar_add(scr2, scr2, 1.0))
            p.guard("dve")
            p.do("dve", lambda v: v.reciprocal(scr2, scr2))
            p.guard("dve")
            p.do("dve", lambda v: v.tensor_mul(scr2, scr2, m2f[:, 0:D]))
            p.guard("dve")
        else:
            p.do("act", lambda s: s.activation(scr2, scr2, AF.Sqrt))
            rtv = p.bump("act", "act")
            p.wait("dve", "act", rtv)
        fre = fs.rearrange("p (c a u) -> p c a u", c=NCI, a=2)
        p.do("dve", lambda v: v.tensor_mul(dst_re.rearrange("p (c u) -> p c u", c=NCI),
                                           fre[:, :, 0, :], s2c))
        p.do("dve", lambda v: v.tensor_mul(dst_im.rearrange("p (c u) -> p c u", c=NCI),
                                           fre[:, :, 1, :], s2c))
        p.bump("dve", "dve")

    def scan_inner(ta, tb):
        qa_r, qa_i = q5(a_re), q5(a_im)
        qu_r, qu_i = q5(u_re), q5(u_im)
        qP_r, qP_i = q5(P_re), q5(P_im)
        def t128(i):
            return tmps[i][:, 0:NCI * 2 * NJ].rearrange("p (c s j) -> p c s j", c=NCI, s=2)
        def g128(i):
            return gtmps[i][:, 0:NCI * 2 * NJ].rearrange("p (c s j) -> p c s j", c=NCI, s=2)
        t1_, t2_, t3_, t4_, t5_, t6_ = (t128(i) for i in range(6))
        g1, g2, g3, g4 = (g128(i) for i in range(4))
        t1, t2, t3, t4, t5, t6 = t1_, t2_, t3_, t4_, t5_, t6_
        for t in range(ta, tb):
            ar, ai = qa_r[:, :, :, :, t], qa_i[:, :, :, :, t]
            ur, ui = qu_r[:, :, :, :, t], qu_i[:, :, :, :, t]
            pr, pi = qP_r[:, :, :, :, t], qP_i[:, :, :, :, t]
            if t == 0:
                p.do("dve", lambda v, ar=ar, ur=ur: v.tensor_mul(t1, ar, ur))
                p.do("dve", lambda v, ai=ai, ui=ui: v.tensor_mul(t2, ai, ui))
                p.do("dve", lambda v, ar=ar, ui=ui: v.tensor_mul(t3, ar, ui))
                p.do("dve", lambda v, ai=ai, ur=ur: v.tensor_mul(t4, ai, ur))
                p.do("dve", lambda v, ur=ur: v.tensor_sub(ur, t1, t2))
                p.do("dve", lambda v, ui=ui: v.tensor_add(ui, t3, t4))
                p.do("dve", lambda v, pr=pr, ar=ar: v.tensor_copy(pr, ar))
                p.do("dve", lambda v, pi=pi, ai=ai: v.tensor_copy(pi, ai))
            else:
                urp, uip = qu_r[:, :, :, :, t - 1], qu_i[:, :, :, :, t - 1]
                prp, pip = qP_r[:, :, :, :, t - 1], qP_i[:, :, :, :, t - 1]
                p.do("dve", lambda v, urp=urp, ur=ur: v.tensor_add(t5, urp, ur))
                p.do("dve", lambda v, uip=uip, ui=ui: v.tensor_add(t6, uip, ui))
                p.do("dve", lambda v, ar=ar: v.tensor_mul(t1, ar, t5))
                p.do("dve", lambda v, ai=ai: v.tensor_mul(t2, ai, t6))
                p.do("dve", lambda v, ur=ur: v.tensor_sub(ur, t1, t2))
                p.do("dve", lambda v, ar=ar: v.tensor_mul(t3, ar, t6))
                p.do("dve", lambda v, ai=ai: v.tensor_mul(t4, ai, t5))
                p.do("dve", lambda v, ui=ui: v.tensor_add(ui, t3, t4))
                p.do("dve", lambda v, ar=ar, prp=prp: v.tensor_mul(g1, ar, prp))
                p.do("dve", lambda v, ai=ai, pip=pip: v.tensor_mul(g2, ai, pip))
                p.do("dve", lambda v, pr=pr: v.tensor_sub(pr, g1, g2))
                p.do("dve", lambda v, ar=ar, pip=pip: v.tensor_mul(g3, ar, pip))
                p.do("dve", lambda v, ai=ai, prp=prp: v.tensor_mul(g4, ai, prp))
                p.do("dve", lambda v, pi=pi: v.tensor_add(pi, g3, g4))
        p.bump("dve", "dve")

    def scan_chain(ja, jb):
        qu_r, qu_i = q5(u_re), q5(u_im)
        qP_r, qP_i = q5(P_re), q5(P_im)
        qs_r = scar_r.rearrange("p (c s j) -> p c s j", c=NCI, s=2)
        qs_i = scar_i.rearrange("p (c s j) -> p c s j", c=NCI, s=2)
        def t8(i):
            return tmps[i][:, 0:8].rearrange("p (c s) -> p c s", c=NCI)
        t1b, t2b, t3b, t4b, t5b = (t8(i) for i in range(5))
        if ja == 1:
            p.do("dve", lambda v: v.tensor_copy(qs_r[:, :, :, 0], qcs(scb_re)))
            p.do("dve", lambda v: v.tensor_copy(qs_i[:, :, :, 0], qcs(scb_im)))
        for j in range(ja, jb):
            Ar = qP_r[:, :, :, j - 1, LC - 1]; Ai = qP_i[:, :, :, j - 1, LC - 1]
            Br = qu_r[:, :, :, j - 1, LC - 1]; Bi = qu_i[:, :, :, j - 1, LC - 1]
            Srp = qs_r[:, :, :, j - 1]; Sip = qs_i[:, :, :, j - 1]
            Sr = qs_r[:, :, :, j] if j < NJ else qcs(scb_re)
            Si = qs_i[:, :, :, j] if j < NJ else qcs(scb_im)
            p.do("dve", lambda v, Ar=Ar, Srp=Srp: v.tensor_mul(t1b, Ar, Srp))
            p.do("dve", lambda v, Ai=Ai, Sip=Sip: v.tensor_mul(t2b, Ai, Sip))
            p.do("dve", lambda v, Ar=Ar, Sip=Sip: v.tensor_mul(t3b, Ar, Sip))
            p.do("dve", lambda v, Ai=Ai, Srp=Srp: v.tensor_mul(t4b, Ai, Srp))
            p.do("dve", lambda v: v.tensor_sub(t5b, t1b, t2b))
            p.do("dve", lambda v, Sr=Sr, Br=Br: v.tensor_add(Sr, t5b, Br))
            p.do("dve", lambda v: v.tensor_add(t5b, t3b, t4b))
            p.do("dve", lambda v, Si=Si, Bi=Bi: v.tensor_add(Si, t5b, Bi))
        p.bump("dve", "dve")

    def scan_app():
        qP_r, qP_i = q5(P_re), q5(P_im)
        qs_r = scar_r.rearrange("p (c s j) -> p c s j", c=NCI, s=2)
        qs_i = scar_i.rearrange("p (c s j) -> p c s j", c=NCI, s=2)
        Srb = qs_r.unsqueeze(4).to_broadcast([128, NCI, 2, NJ, LC])
        Sib = qs_i.unsqueeze(4).to_broadcast([128, NCI, 2, NJ, LC])
        sA = scr[:, 0:1024].rearrange("p (c s j u) -> p c s j u", c=NCI, s=2, j=NJ)
        sB = scr[:, 1024:2048].rearrange("p (c s j u) -> p c s j u", c=NCI, s=2, j=NJ)
        p.do("dve", lambda v: v.tensor_mul(sA, qP_r, Srb))
        p.do("dve", lambda v: v.tensor_mul(sB, qP_i, Sib))
        p.do("dve", lambda v: v.tensor_sub(scr[:, 0:1024], scr[:, 0:1024], scr[:, 1024:2048]))
        p.do("dve", lambda v: v.tensor_add(u_re, u_re, scr[:, 0:1024]))
        p.do("dve", lambda v: v.tensor_mul(sA, qP_r, Sib))
        p.do("dve", lambda v: v.tensor_mul(sB, qP_i, Srb))
        p.do("dve", lambda v: v.tensor_add(scr[:, 0:1024], scr[:, 0:1024], scr[:, 1024:2048]))
        p.do("dve", lambda v: v.tensor_add(u_im, u_im, scr[:, 0:1024]))
        p.bump("dve", "dve")

    def gate_topk(k, xn2q, x2v):
        rwo = (k % 2) * 2 * E
        p.wait("te", "dve", x2v)
        p.wait("dve", "te", p.sems["te"].n)
        x2h, x2l = bpair(xn2q)
        for j in range(2):
            b = p.take_banks(1)[0]
            terms = []
            for kc in range(ND):
                lh = x2h[:, kc * T + j * 128: kc * T + (j + 1) * 128]
                ll = x2l[:, kc * T + j * 128: kc * T + (j + 1) * 128]
                rh = gwh[:, kc * E:(kc + 1) * E]
                rl = gwl[:, kc * E:(kc + 1) * E]
                terms += [(lh, rh), (lh, rl), (ll, rh)]
            mm_group(b, E, terms)
            tv = p.bump("te", "te")
            p.wait("dve", "te", tv)
            p.do("dve", lambda v, b=b, j=j: v.tensor_copy(gate[:, j * 48: j * 48 + E], pb[b][:, 0:E]))
            p.bump("dve", "dve")
            p.release_bank(b, "dve")
        def g(j, off, n=E):
            return gate[:, j * 48 + off: j * 48 + off + n]
        for j in range(2):
            p.do("dve", lambda v, j=j: v.reduce_max(small[:, 24 + j:25 + j], g(j, 0), axis=AX.X))
        p.guard("dve")
        for j in range(2):
            p.do("dve", lambda v, j=j: v.tensor_scalar_mul(small[:, 26 + j:27 + j],
                                                           small[:, 24 + j:25 + j], -1.0))
        ngv = p.bump("dve", "dve")
        p.wait("act", "dve", ngv)
        for j in range(2):
            p.do("act", lambda s, j=j: s.activation(g(j, 8), g(j, 0), AF.Exp,
                                                    bias=small[:, 26 + j:27 + j]))
        ev = p.bump("act", "act")
        p.wait("dve", "act", ev)
        for j in range(2):
            p.do("dve", lambda v, j=j: v.reduce_sum(small[:, 28 + j:29 + j], g(j, 8), axis=AX.X))
        p.guard("dve")
        for j in range(2):
            p.do("dve", lambda v, j=j: v.reciprocal(small[:, 30 + j:31 + j], small[:, 28 + j:29 + j]))
        p.guard("dve")
        for j in range(2):
            p.do("dve", lambda v, j=j: v.tensor_scalar(g(j, 8), g(j, 8), small[:, 30 + j:31 + j],
                                                       None, ALU.mult))
            p.do("dve", lambda v, j=j: v.tensor_add(g(j, 16), g(j, 8), gbb))
        p.guard("dve")
        for pass_i in range(2):
            for j in range(2):
                p.do("dve", lambda v, j=j: v.reduce_max(small[:, 32 + j:33 + j], g(j, 16), axis=AX.X))
            p.guard("dve")
            for j in range(2):
                p.do("dve", lambda v, j=j: v.tensor_scalar(g(j, 24), g(j, 16),
                                                           small[:, 32 + j:33 + j], None, ALU.is_ge))
            p.guard("dve")
            for j in range(2):
                p.do("dve", lambda v, j=j: v.tensor_scalar(g(j, 32), g(j, 24), -BIG, BIG,
                                                           ALU.mult, ALU.add))
                p.do("dve", lambda v, j=j: v.tensor_add(g(j, 32), g(j, 32), iota8))
            p.guard("dve")
            for j in range(2):
                p.do("dve", lambda v, j=j: v.tensor_reduce(small[:, 34 + j:35 + j], g(j, 32),
                                                           AX.X, ALU.min))
            p.guard("dve")
            for j in range(2):
                p.do("dve", lambda v, j=j: v.tensor_scalar(g(j, 24), iota8,
                                                           small[:, 34 + j:35 + j], None, ALU.is_equal))
            p.guard("dve")
            for j in range(2):
                p.do("dve", lambda v, j=j: v.tensor_mul(g(j, 40), g(j, 8), g(j, 24)))
            p.guard("dve")
            for j in range(2):
                p.do("dve", lambda v, j=j, pi=pass_i:
                     v.reduce_sum(small[:, 36 + 2 * j + pi:37 + 2 * j + pi], g(j, 40), axis=AX.X))
            p.guard("dve")
            for j in range(2):
                if pass_i == 0:
                    p.do("dve", lambda v, j=j, rwo=rwo: v.tensor_scalar(
                        rww[:, rwo + j * E: rwo + (j + 1) * E], g(j, 24),
                        small[:, 36 + 2 * j:37 + 2 * j], None, ALU.mult))
                    p.do("dve", lambda v, j=j: v.tensor_scalar(g(j, 40), g(j, 24), BIG, None, ALU.mult))
                    p.do("dve", lambda v, j=j: v.tensor_sub(g(j, 16), g(j, 16), g(j, 40)))
                else:
                    p.do("dve", lambda v, j=j: v.tensor_scalar(g(j, 40), g(j, 24),
                                                               small[:, 37 + 2 * j:38 + 2 * j],
                                                               None, ALU.mult))
                    p.do("dve", lambda v, j=j, rwo=rwo: v.tensor_add(
                        rww[:, rwo + j * E: rwo + (j + 1) * E],
                        rww[:, rwo + j * E: rwo + (j + 1) * E], g(j, 40)))
            p.guard("dve")

    wslot_free = [p.sems["te"].n] * 6
    ffn_base = [0]

    x_tm_a = x_tm
    def xt_of(k):
        return x_tm_a if k % 2 == 0 else x_tm2

    def xn2_of(k):
        return xn2_fm if k % 2 == 0 else xn2_b

    ws = {}
    wst = {"wout_cons": 0}
    wbv16 = watb.bitcast(mybir.dt.bfloat16)

    def wissue(key, t, W_hbm, anchor, k0=False):
        if k0:
            p.wait("act", "dst", p.sems["dst"].n, force=True)
        else:
            p.wait("act", "te", max(anchor, 1), force=True)
        dst = wbv16[:, (t % 2) * 4096:(t % 2) * 4096 + 4096].rearrange("p (c f) -> p c f", c=2)
        src = W_hbm.rearrange("(c p) f -> p c f", p=128)[:, 2 * t:2 * t + 2, :]
        sem = "dwc" if t % 2 == 0 else "dwd"
        ws[(key, t)] = (sem, p.dma("act", dst, src, sem=sem))

    def stages_A(k):
        x_tm = xt_of(k)
        asx = {}
        def s_wd():
            wissue("wa", 0, WAT_HBM, wst["wout_cons"], k0=(k == 0))
            wissue("wa", 1, WAT_HBM, wst["wout_cons"], k0=(k == 0))
        def s_xln():
            p.wait("gps", "dst", p.sems["dst"].n)
            p.wait("gps", "dve", p.sems["dve"].n)
            p.wait("gps", "te", p.sems["te"].n) if k == 0 else None
            ldx = p.dma("gps", x_tm.rearrange("p (s d) -> p s d", s=2),
                        X.rearrange("(s n) d -> n s d", s=2)[k * TOK:(k + 1) * TOK], sem="dx")
            p.wait("dve", "dx", ldx)
            p.wait("act", "dx", ldx)
            asx["xnv"] = ln(xn1_fm, "ln1", k, x_tm)
        def s_xtp():
            asx["ln1tp"] = ln_tp(xn1_fm, asx["xnv"])
        def s_pa():
            proj_and_factor(WAT_HBM, a_re, a_im, "a", "wa", asx["ln1tp"],
                            [("wa", 2, WAT_HBM), ("wa", 3, WAT_HBM),
                             ("wx", 0, WXT_HBM), ("wx", 1, WXT_HBM)])
            if k == 0:
                tapdma("afac", a_re); tapdma("aim", a_im)
        def s_px():
            proj_and_factor(WXT_HBM, xc_re, xc_im, "x", "wx", asx["ln1tp"],
                            [("wx", 2, WXT_HBM), ("wx", 3, WXT_HBM),
                             ("wo", 0, WOUTP), ("wo", 1, WOUTP)])
            if k == 0:
                tapdma("xcre", xc_re); tapdma("xcim", xc_im)
        def s_xr():
            cin = (k % 2) * 2 * NCI
            cout = ((k + 1) % 2) * 2 * NCI
            q4b = lambda ap: ap.rearrange("p (c s u) -> p c s u", c=NCI, s=2)
            p.do("dve", lambda v: v.tensor_copy(q4b(xrh_re)[:, :, :, 1:TOK], q4(xc_re)[:, :, :, 0:TOK - 1]))
            p.do("dve", lambda v: v.tensor_scalar(q4b(xrh_iN)[:, :, :, 1:TOK],
                                                  q4(xc_im)[:, :, :, 0:TOK - 1], -1.0, None, ALU.mult))
            p.do("dve", lambda v, cin=cin: v.tensor_copy(q4b(xrh_re)[:, :, :, 0],
                                                         qcs(carry_re[:, cin:cin + 2 * NCI])))
            p.do("dve", lambda v, cin=cin: v.tensor_copy(q4b(xrh_iN)[:, :, :, 0],
                                                         qcs(carry_iN[:, cin:cin + 2 * NCI])))
            p.guard("dve")
            p.do("dve", lambda v: v.tensor_tensor(q4b(xrl_re)[:, :, :, 1:TOK],
                                                  q4(xc_re)[:, :, :, 0:TOK - 1],
                                                  q4b(xrh_re)[:, :, :, 1:TOK], ALU.subtract))
            p.do("dve", lambda v: v.scalar_tensor_tensor(q4b(xrl_iN)[:, :, :, 1:TOK],
                                                         q4(xc_im)[:, :, :, 0:TOK - 1], -1.0,
                                                         q4b(xrh_iN)[:, :, :, 1:TOK],
                                                         ALU.mult, ALU.subtract))
            p.do("dve", lambda v, cin=cin: v.tensor_tensor(q4b(xrl_re)[:, :, :, 0],
                                                           qcs(carry_re[:, cin:cin + 2 * NCI]),
                                                           q4b(xrh_re)[:, :, :, 0], ALU.subtract))
            p.do("dve", lambda v, cin=cin: v.tensor_tensor(q4b(xrl_iN)[:, :, :, 0],
                                                           qcs(carry_iN[:, cin:cin + 2 * NCI]),
                                                           q4b(xrh_iN)[:, :, :, 0], ALU.subtract))
            p.do("dve", lambda v, cout=cout: v.tensor_copy(qcs(carry_re[:, cout:cout + 2 * NCI]),
                                                           q4(xc_re)[:, :, :, TOK - 1]))
            p.do("dve", lambda v, cout=cout: v.tensor_scalar(qcs(carry_iN[:, cout:cout + 2 * NCI]),
                                                             q4(xc_im)[:, :, :, TOK - 1],
                                                             -1.0, None, ALU.mult))
            xrv = p.bump("dve", "dve")
            p.wait("te", "dve", xrv)

        def s_u():
            zr_pr = (zr_h, zr_l); zi_pr = (zi_h, zi_l)
            xre_pr = (xrh_re, xrl_re); xiN_pr = (xrh_iN, xrl_iN)
            for mc in range(NCI):
                terms_re = mm3([(prtile(zr_pr, kc, mc), xrpair(xre_pr, kc)) for kc in range(NCI)] +
                               [(prtile(zi_pr, kc, mc), xrpair(xiN_pr, kc)) for kc in range(NCI)])
                bre = p.take_banks(1)[0]
                mm_group(bre, T, terms_re)
                uv = p.bump("te", "te")
                p.wait("dve", "te", uv)
                sl = slice(mc * T, (mc + 1) * T)
                p.do("dve", lambda v, b=bre, sl=sl: v.tensor_copy(u_re[:, sl], pb[b][:, 0:T]))
                p.bump("dve", "dve")
                p.release_bank(bre, "dve")
                tP = mm3([(prtile(zi_pr, kc, mc), xrpair(xre_pr, kc)) for kc in range(NCI)])
                tQ = mm3([(prtile(zr_pr, kc, mc), xrpair(xiN_pr, kc)) for kc in range(NCI)])
                psub(tP, tQ, T,
                     lambda v, pP, pQ, sl=sl: v.tensor_tensor(u_im[:, sl], pP[:, 0:T], pQ[:, 0:T],
                                                              ALU.subtract))
            if k == 0:
                tapdma("ure", u_re); tapdma("uim", u_im)
        out = [(6, s_wd), (40, s_xln), (48, s_xtp), (65, s_pa), (98, s_px),
               (116, s_xr), (130, s_u)]
        scpts = [140, 145, 150, 155, 160, 165, 170, 175]
        nsl = len(scpts)
        for si in range(nsl):
            ta = (LC * si) // nsl
            tb = (LC * (si + 1)) // nsl
            out.append((scpts[si], lambda ta=ta, tb=tb: scan_inner(ta, tb)))
        out.append((181, lambda: scan_chain(1, NJ // 2 + 1)))
        out.append((186, lambda: scan_chain(NJ // 2 + 1, NJ + 1)))
        out.append((191, scan_app))
        return out

    def stages_B(k):
        x_tm = xt_of(k)
        bs = {}
        def s_h():
            s_reh, s_rel = bpair(a_re)
            s_imh, s_iml = bpair(a_im)
            p.do("dve", lambda v: v.tensor_copy(s_reh, u_re))
            p.do("dve", lambda v: v.tensor_copy(s_imh, u_im))
            p.guard("dve")
            p.do("dve", lambda v: v.tensor_sub(s_rel, u_re, s_reh))
            p.do("dve", lambda v: v.tensor_sub(s_iml, u_im, s_imh))
            spv = p.bump("dve", "dve")
            p.wait("te", "dve", spv)
            vtr_pr = (vtr_h, vtr_l); vti_pr = (vti_h, vti_l)
            sre_pr = (s_reh, s_rel); sim_pr = (s_imh, s_iml)
            for mc in range(NCI):
                sl = slice(mc * T, (mc + 1) * T)
                tP = mm3([(prtile(vtr_pr, kc, mc), xrpair(sre_pr, kc)) for kc in range(NCI)])
                tQ = mm3([(prtile(vti_pr, kc, mc), xrpair(sim_pr, kc)) for kc in range(NCI)])
                def hevac(v, pP, pQ, sl=sl, mc=mc):
                    return v.tensor_tensor(xn_tm[:, mc * T:(mc + 1) * T], pP[:, 0:T], pQ[:, 0:T],
                                           ALU.subtract)
                psub(tP, tQ, T, hevac)
                p.do("dve", lambda v, sl=sl, mc=mc: v.tensor_add(xn_tm[:, mc * T:(mc + 1) * T],
                                                                 xn_tm[:, mc * T:(mc + 1) * T],
                                                                 xc_re[:, sl]))
                p.guard("dve")
                p.do("dve", lambda v, sl=sl, mc=mc: v.tensor_copy(hreh[:, sl], xn_tm[:, mc * T:(mc + 1) * T]))
                p.guard("dve")
                p.do("dve", lambda v, sl=sl, mc=mc: v.tensor_tensor(hrel[:, sl],
                                                                    xn_tm[:, mc * T:(mc + 1) * T],
                                                                    hreh[:, sl], ALU.subtract))
                p.bump("dve", "dve")
            bs["hpv"] = p.sems["dve"].n
            if k == 0:
                tapdma("sre", u_re); tapdma("sim", u_im)
        def s_op():
            p.wait("te", "dve", bs["hpv"])
            p.wait("te", *ws[("wo", 0)])
            p.wait("te", *ws[("wo", 1)])
            wov = watb.bitcast(mybir.dt.bfloat16)
            for j in range(2):
                for n in range(2):
                    b = p.take_banks(1)[0]
                    terms = []
                    for cc in range(NCI):
                        lh = hreh[:, cc * T + j * 128: cc * T + (j + 1) * 128]
                        ll = hrel[:, cc * T + j * 128: cc * T + (j + 1) * 128]
                        rh = wov[:, cc * 2048 + n * 512: cc * 2048 + (n + 1) * 512]
                        rl = wov[:, cc * 2048 + 1024 + n * 512: cc * 2048 + 1024 + (n + 1) * 512]
                        terms += [(lh, rh), (lh, rl), (ll, rh)]
                    mm_group(b, 512, terms)
                    ov = p.bump("te", "te")
                    p.wait("dve", "te", ov)
                    sl = slice(j * D + n * 512, j * D + (n + 1) * 512)
                    p.do("dve", lambda v, b=b, sl=sl: v.tensor_add(x_tm[:, sl], x_tm[:, sl], pb[b][:, 0:512]))
                    p.bump("dve", "dve")
                    p.release_bank(b, "dve")
            wst["wout_cons"] = p.sems["te"].n
            if k == 0:
                tapdma("x1", x_tm)
        def s_l2():
            bs["xn2v"] = ln(xn2_of(k), "ln2", k, x_tm)
        def s_l2tp():
            bs["x2v"] = ln_tp(xn2_of(k), bs["xn2v"])
        def s_g():
            gate_topk(k, xn2_of(k), bs["x2v"])
            bs["rwv"] = p.sems["dve"].n
            if k == 0:
                tapdma("rww", rww[:, 0:2 * E])
        def s_rb():
            p.wait("te", "dve", bs["rwv"])
            rwo = (k % 2) * 2 * E
            for j in range(2):
                b = p.take_banks(1)[0]
                p.do("te", lambda t, b=b, j=j, rwo=rwo: t.transpose(pb[b][0:E, 0:128],
                                                                    rww[:, rwo + j * E: rwo + (j + 1) * E],
                                                                    i128))
                tv = p.bump("te", "te")
                p.wait("dve", "te", tv)
                p.do("dve", lambda v, b=b, j=j: v.tensor_copy(rwf9[0:E, j * TOK:(j + 1) * TOK],
                                                              pb[b][0:E, 0:128]))
                p.bump("dve", "dve")
                p.release_bank(b, "dve")
            p.wait("te", "dve")
            for j in range(2):
                for n in range(2):
                    b = p.take_banks(1)[0]
                    p.do("te", lambda t, b=b, j=j, n=n:
                         t.matmul(pb[b][:, 0:512], rwf9[0:9, j * TOK:(j + 1) * TOK],
                                  b2sb[0:9, n * 512:(n + 1) * 512], start=True, stop=True))
                    tv = p.bump("te", "te")
                    p.wait("dve", "te", tv)
                    sl = slice(j * D + n * 512, j * D + (n + 1) * 512)
                    p.do("dve", lambda v, b=b, sl=sl: v.tensor_add(x_tm[:, sl], x_tm[:, sl],
                                                                   pb[b][:, 0:512]))
                    p.bump("dve", "dve")
                    p.release_bank(b, "dve")


        return [(208, s_h), (228, s_op), (242, s_l2), (250, s_l2tp),
                (260, s_g), (275, s_rb)]

    for _pt, f in stages_A(0):
        f()
    for _pt, f in stages_B(0):
        f()
    pend = {}
    for k in range(NCH):
        x_tm = xt_of(k)
        rwo = (k % 2) * 2 * E
        if k == 0:
            p.wait("sync", "dst", repack_done)
        units = []
        for e in range(9):
            W1s = W1[e] if e < E else SW1
            W3s = W3[e] if e < E else SW3
            W2s = W2[e] if e < E else SW2
            hstate = {}
            for hf in range(NH):
                def dma_fn(slot, sem, e=e, hf=hf):
                    return p.dma("sync", slot.bitcast(mybir.dt.bfloat16), W1P[e, hf], sem=sem)
                def comp_fn(slot, ld, hf=hf, e=e):
                    p.wait("te", ld[0], ld[1])
                    b = p.take_banks(1)[0]
                    sv = slot.bitcast(mybir.dt.bfloat16)
                    x2h, x2l = bpair(xn2_of(k))
                    terms = []
                    for kc in range(ND):
                        wh = sv[:, kc * 128:(kc + 1) * 128]
                        wl = sv[:, 1024 + kc * 128: 1024 + (kc + 1) * 128]
                        xh = x2h[:, kc * T:(kc + 1) * T]
                        xl = x2l[:, kc * T:(kc + 1) * T]
                        terms += [(wh, xh), (wh, xl), (wl, xh)]
                    mm_group(b, T, terms)
                    tv = p.bump("te", "te")
                    p.wait("act", "te", tv)
                    p.do("act", lambda s, b=b, hf=hf, e=e:
                         s.activation(hffn[:, hf * T:(hf + 1) * T], pb[b][:, 0:T], AF.Identity,
                                      bias=b1s[:, e * NH + hf: e * NH + hf + 1]))
                    p.bump("act", "act")
                    p.release_bank(b, "act")
                    return tv
                units.append((dma_fn, comp_fn))
            for hf in range(NH):
                def dma_fn(slot, sem, e=e, hf=hf):
                    return p.dma("sync", slot.bitcast(mybir.dt.bfloat16), W3P[e, hf], sem=sem)
                def comp_fn(slot, ld, hf=hf, e=e, hstate=hstate):
                    p.wait("te", ld[0], ld[1])
                    b = p.take_banks(1)[0]
                    sv = slot.bitcast(mybir.dt.bfloat16)
                    x2h, x2l = bpair(xn2_of(k))
                    terms = []
                    for kc in range(ND):
                        wh = sv[:, kc * 128:(kc + 1) * 128]
                        wl = sv[:, 1024 + kc * 128: 1024 + (kc + 1) * 128]
                        xh = x2h[:, kc * T:(kc + 1) * T]
                        xl = x2l[:, kc * T:(kc + 1) * T]
                        terms += [(wh, xh), (wh, xl), (wl, xh)]
                    mm_group(b, T, terms)
                    tv = p.bump("te", "te")
                    p.wait("dve", "te", tv)
                    p.wait("dve", "act", p.sems["act"].n)
                    hs = hmul[:, (hf % 4) * T:((hf % 4) + 1) * T]
                    p.do("dve", lambda v, b=b, hf=hf, e=e, hs=hs:
                         v.scalar_tensor_tensor(hs, pb[b][:, 0:T],
                                                b3s[:, e * NH + hf: e * NH + hf + 1],
                                                hffn[:, hf * T:(hf + 1) * T], ALU.add, ALU.mult))
                    mv = p.bump("dve", "dve")
                    p.release_bank(b, "dve")
                    hh, hl = hpair(hf)
                    p.wait("act", "dve", mv)
                    p.do("act", lambda s, hs=hs: s.activation(hs, hs, AF.Silu))
                    p.guard("act")
                    p.do("act", lambda s, hh=hh, hs=hs: s.copy(hh, hs))
                    cv = p.bump("act", "act")
                    p.wait("dve", "act", cv)
                    p.do("dve", lambda v, hh=hh, hl=hl, hs=hs: v.tensor_sub(hl, hs, hh))
                    hstate[hf] = p.bump("dve", "dve")
                    return tv
                units.append((dma_fn, comp_fn))
            ystate = {}
            for hf in range(NH):
                def dma_fn(slot, sem, e=e, hf=hf):
                    return p.dma("sync", slot.bitcast(mybir.dt.bfloat16), W2P[e, hf], sem=sem)
                def comp_fn(slot, ld, hf=hf, e=e, ystate=ystate, hstate=hstate):
                    hh, hl = hpair(hf)
                    if hf == 0:
                        ystate["banks"] = p.take_banks(4)
                        p.hold_banks(ystate["banks"])
                        p.wait("te", "dve", hstate[NH - 1])
                    p.wait("te", ld[0], ld[1])
                    sv = slot.bitcast(mybir.dt.bfloat16)
                    for j in range(2):
                        for n in range(2):
                            lh = hh[:, j * 128:(j + 1) * 128]
                            ll = hl[:, j * 128:(j + 1) * 128]
                            rh = sv[:, n * 512:(n + 1) * 512]
                            rl = sv[:, 1024 + n * 512: 1024 + (n + 1) * 512]
                            for ti, (lt, rr) in enumerate(((lh, rh), (lh, rl), (ll, rh))):
                                p.do("te", lambda t, b=ystate["banks"][2 * j + n], lt=lt, rr=rr,
                                     st=(hf == 0 and ti == 0), sp=(hf == NH - 1 and ti == 2):
                                     t.matmul(pb[b][:, 0:512], lt, rr, start=st, stop=sp))
                    tv = p.bump("te", "te")
                    if hf == NH - 1:
                        p.wait("dve", "te", tv)
                        for j in range(2):
                            for n in range(2):
                                b = ystate["banks"][2 * j + n]
                                sl = slice(j * D + n * 512, j * D + (n + 1) * 512)
                                sc = rww[:, rwo + j * E + e: rwo + j * E + e + 1] if e < E else 1.0
                                p.do("dve", lambda v, b=b, sl=sl, sc=sc, x_tm=x_tm:
                                     v.scalar_tensor_tensor(x_tm[:, sl], pb[b][:, 0:512], sc,
                                                            x_tm[:, sl], ALU.mult, ALU.add))
                                p.bump("dve", "dve")
                                p.release_bank(b, "dve")
                        p.unhold_banks(ystate["banks"])
                    return tv
                units.append((dma_fn, comp_fn))
        G = 4
        DEPTH = 6
        anchor = p.sems["te"].n
        for _i in range(DEPTH):
            wslot_free[_i] = max(wslot_free[_i], anchor)
        NU = len(units)
        if k + 1 < NCH:
            pend = {}
            for pt, f in stages_A(k + 1) + stages_B(k + 1):
                pend.setdefault(pt, []).append(f)
        else:
            pend = {}
        ngrp = (NU + G - 1) // G
        gsem = ["dwa" if m % 2 == 0 else "dwb" for m in range(ngrp)]
        gend = {}

        def slot_ap(i):
            return wbuf[:, (i % DEPTH) * 1024:(i % DEPTH) * 1024 + 1024]

        def issue(i):
            p.wait("sync", "te", wslot_free[i % DEPTH], force=True)
            m = i // G
            v = units[i][0](slot_ap(i), gsem[m])
            if i == NU - 1:
                dn = 0
                while (i + 1 + dn) % G != 0:
                    p.dma("sync", dumm[0:1, dn * 8:dn * 8 + 8], ONES[0:1, 0:8], sem=gsem[m])
                    v = p.sems[gsem[m]].n
                    dn += 1
                gend[m] = v
            elif i % G == G - 1:
                gend[m] = v

        for i in range(min(DEPTH, NU)):
            issue(i)
        for i in range(NU):
            m = i // G
            wslot_free[i % DEPTH] = units[i][1](slot_ap(i), (gsem[m], gend[m]))
            if i in pend:
                for f in pend.pop(i):
                    f()
            ip = i + DEPTH
            if ip < NU:
                issue(ip)

        if k == 0:
            tapdma("xout", x_tm)
        p.wait("gps", "dve", p.sems["dve"].n)
        p.dma("gps", OUT.rearrange("(s n) d -> n s d", s=2)[k * TOK:(k + 1) * TOK],
              x_tm.rearrange("p (s d) -> p s d", s=2), sem="dst")

    p.wait("gps", "dst", p.sems["dst"].n)

    import contextlib
    with contextlib.ExitStack() as st:
        for s in p.sems.values():
            s.h = st.enter_context(nc.semaphore("s_" + s.name))
        blk = st.enter_context(nc.Block())
        blk.tensor(lambda e: [f(e) for f in p.ops["te"]])
        blk.vector(lambda e: [f(e) for f in p.ops["dve"]])
        blk.scalar(lambda e: [f(e) for f in p.ops["act"]])
        blk.gpsimd(lambda e: [f(e) for f in p.ops["gps"]])
        blk.sync(lambda e: [f(e) for f in p.ops["sync"]])
    ninstr = {k: len(v) for k, v in p.ops.items()}
    return nc, ninstr


def host_inputs(inp, core, n_cores=8, rows_per_core=4096):
    seqs_per_core = 16 // n_cores
    x = np.ascontiguousarray(
        inp["L0_attn_x"].reshape(16, 2048, 1024)[core * seqs_per_core:(core + 1) * seqs_per_core]
    ).reshape(-1, D)[:rows_per_core]
    m = {
        "x": x,
        "w_a": inp["L0_attn_w_a"], "w_x": inp["L0_attn_w_x"], "w_out": inp["L0_attn_w_out"],
        "mat_v_real": inp["L0_attn_mat_v_real"], "mat_v_imag": inp["L0_attn_mat_v_imag"],
        "h0_re": inp["L0_attn_last_hidden_real"], "h0_im": inp["L0_attn_last_hidden_imag"],
        "gate_w": inp["L0_ffn_gate_w"],
        "gate_bb": np.ascontiguousarray(np.broadcast_to(inp["L0_ffn_gate_b"], (128, E))),
        "w1": inp["L0_ffn_w1"], "b1": inp["L0_ffn_b1"],
        "w3": inp["L0_ffn_w3"], "b3": inp["L0_ffn_b3"],
        "w2": inp["L0_ffn_w2"], "b2": inp["L0_ffn_b2"],
        "sw1": inp["L0_ffn_sw1"], "sb1": inp["L0_ffn_sb1"],
        "sw3": inp["L0_ffn_sw3"], "sb3": inp["L0_ffn_sb3"],
        "sw2": inp["L0_ffn_sw2"], "sb2": inp["L0_ffn_sb2"],
        "eye512": np.eye(INNER, dtype=np.float32),
        "i128": np.eye(128, dtype=np.float32),
        "iota8": np.ascontiguousarray(np.broadcast_to(np.arange(E, dtype=np.float32), (128, E))),
        "ones": np.ones((1, 128), dtype=np.float32),
    }
    return {k: np.ascontiguousarray(np.asarray(v, dtype=np.float32)) for k, v in m.items()}


from concourse.bass_utils import run_bass_kernel_spmd

_NC_CACHE = {}

def kernel(**inputs):
    n_cores = 8
    rows = 4096
    if "nc" not in _NC_CACHE:
        _NC_CACHE["nc"], _ = build(rows_per_core=rows, n_chunks=None, tap_names=(),
                                   newton_iters=4)
    nc = _NC_CACHE["nc"]
    in_maps = [host_inputs(inputs, core=c, n_cores=n_cores, rows_per_core=rows)
               for c in range(n_cores)]
    res = run_bass_kernel_spmd(nc, in_maps, core_ids=list(range(n_cores)))
    out = np.stack([res.results[c]["out"].reshape(2, 2048, D) for c in range(n_cores)])
    return out.reshape(16, 2048, D).astype(np.float32)


# version 17
# speedup vs baseline: 1.0095x; 1.0095x over previous
import numpy as np
import concourse.bass as bass
import concourse.mybir as mybir

F32 = mybir.dt.float32
AF = mybir.ActivationFunctionType
ALU = mybir.AluOpType
AX = mybir.AxisListType

D = 1024
INNER = 512
HID = 1408
E = 8
TOK = 128
T = 2 * TOK
ND = D // 128
NCI = INNER // 128
NH = HID // 128
LC = 16
NJ = TOK // LC
EPS = 1e-5
BIG = 1e9
ENGS = ("te", "dve", "act", "gps", "sync")
SEM_NAMES = ("te", "dve", "act", "gps", "dw", "dx", "dst", "dwa", "dwb", "dwc", "dwd")


class PSem:
    def __init__(self, name):
        self.name = name
        self.h = None
        self.n = 0


class Planner:
    def __init__(self):
        self.ops = {e: [] for e in ENGS}
        self.sems = {k: PSem(k) for k in SEM_NAMES}
        self.waited = {e: {} for e in ENGS}
        self.bank_free = [("te", 0)] * 8
        self.bank_rr = 0
        self.dirty = {e: False for e in ENGS}

    AUTOB = ("dve", "act", "gps")

    def do(self, eng, f, _raw=False):
        if _raw or eng not in self.AUTOB:
            self.ops[eng].append(f)
            return
        if self.dirty[eng]:
            self.dirty[eng] = False
            self.wait(eng, eng)
        s = self.sems[eng]
        s.n += 1
        def g(e, f=f, s=s):
            ins = f(e)
            ins.then_inc(s.h, 1)
            return ins
        self.ops[eng].append(g)
        self.dirty[eng] = True

    def bump(self, eng, sem_name, amt=1):
        if eng in self.AUTOB and sem_name == eng:
            return self.sems[sem_name].n
        s = self.sems[sem_name]
        s.n += amt
        last = self.ops[eng][-1]
        def g(e, last=last, s=s, amt=amt):
            ins = last(e)
            ins.then_inc(s.h, amt)
            return ins
        self.ops[eng][-1] = g
        return s.n

    def wait(self, eng, sem_name, val=None, force=False):
        s = self.sems[sem_name]
        val = s.n if val is None else val
        if val <= 0:
            return
        if not force and self.waited[eng].get(sem_name, -1) >= val:
            return
        assert val <= s.n, f"{eng} waits {sem_name}>={val} > planned {s.n}"
        self.waited[eng][sem_name] = val
        self.do(eng, lambda e, s=s, val=val: e.wait_ge(s.h, val), _raw=True)

    def dma(self, eng, out, in_, sem="dw"):
        if eng == "gps":
            self.wait(eng, sem, self.sems[sem].n)
        self.do(eng, lambda e, out=out, in_=in_: e.dma_start(out=out, in_=in_), _raw=True)
        return self.bump(eng, sem, 16)

    def take_banks(self, n):
        out = []
        tries = 0
        while len(out) < n:
            b = self.bank_rr % 8
            self.bank_rr += 1
            tries += 1
            assert tries < 64, "bank allocator livelock (too many held banks)"
            if b in getattr(self, "held", set()) or b in out:
                continue
            sname, v = self.bank_free[b]
            self.wait("te", sname, v)
            out.append(b)
        return out

    def hold_banks(self, banks):
        if not hasattr(self, "held"):
            self.held = set()
        self.held |= set(banks)

    def unhold_banks(self, banks):
        self.held -= set(banks)

    def release_bank(self, b, sem_name):
        self.bank_free[b] = (sem_name, self.sems[sem_name].n)

    def guard(self, eng):
        self.bump(eng, eng)
        self.wait(eng, eng)


def build(rows_per_core=4096, n_chunks=None, tap_names=(), newton_iters=4):
    p = Planner()
    nc = bass.Bass("TRN2", target_bir_lowering=False, debug=False)
    NCH = (rows_per_core // T) if n_chunks is None else n_chunks
    SEQ = rows_per_core // 2

    def din(name, shape):
        return nc.dram_tensor(name, shape, F32, kind="ExternalInput").ap()
    X = din("x", [rows_per_core, D])
    WA = din("w_a", [2 * INNER, D]); WXm = din("w_x", [2 * INNER, D])
    WOUT = din("w_out", [D, INNER])
    MVR = din("mat_v_real", [INNER, INNER]); MVI = din("mat_v_imag", [INNER, INNER])
    H0R = din("h0_re", [INNER]); H0I = din("h0_im", [INNER])
    GW = din("gate_w", [E, D]); GBB = din("gate_bb", [128, E])
    W1 = din("w1", [E, D, HID]); B1 = din("b1", [E, HID])
    W3 = din("w3", [E, D, HID]); B3 = din("b3", [E, HID])
    W2 = din("w2", [E, HID, D]); B2 = din("b2", [E, D])
    SW1 = din("sw1", [D, HID]); SB1 = din("sb1", [HID])
    SW3 = din("sw3", [D, HID]); SB3 = din("sb3", [HID])
    SW2 = din("sw2", [HID, D]); SB2 = din("sb2", [D])
    EYE = din("eye512", [INNER, INNER])
    I128D = din("i128", [128, 128])
    IOTA8 = din("iota8", [128, E])
    ONES = din("ones", [1, 128])
    OUT = nc.dram_tensor("out", [rows_per_core, D], F32, kind="ExternalOutput").ap()
    BF = mybir.dt.bfloat16
    WAT_HBM = nc.dram_tensor("waT_hbm", [D, 4 * INNER], BF, kind="Internal").ap()
    WXT_HBM = nc.dram_tensor("wxT_hbm", [D, 4 * INNER], BF, kind="Internal").ap()
    W1P = nc.dram_tensor("w1p_hbm", [9, NH, 128, 2 * D], BF, kind="Internal").ap()
    W3P = nc.dram_tensor("w3p_hbm", [9, NH, 128, 2 * D], BF, kind="Internal").ap()
    W2P = nc.dram_tensor("w2p_hbm", [9, NH, 128, 2 * D], BF, kind="Internal").ap()
    WOUTP = nc.dram_tensor("woutp_hbm", [INNER, 2 * D], BF, kind="Internal").ap()

    TAPCOLS = {"Zr": NCI * INNER, "Zi": NCI * INNER, "ln1": 2 * D, "afac": NCI * T,
               "aim": NCI * T, "xcre": NCI * T, "xcim": NCI * T, "ure": NCI * T,
               "uim": NCI * T, "sre": NCI * T, "sim": NCI * T, "hre": NCI * T,
               "x1": 2 * D, "ln2": 2 * D, "rww": 2 * E, "hffn0": NH * T, "xout": 2 * D}
    taps = {}
    for name in tap_names:
        taps[name] = nc.dram_tensor("tap_" + name, [128, TAPCOLS[name]], F32,
                                    kind="ExternalOutput").ap()

    def tapdma(name, src, eng_dep="dve"):
        if name not in taps:
            return
        p.wait("gps", eng_dep, p.sems[eng_dep].n)
        st = p.dma("gps", taps[name], src, sem="dst")
        p.wait("dve", "dst", st)
        p.wait("act", "dst", st)
        p.wait("te", "dst", st)

    def S(name, cols, dtype=F32):
        return nc.alloc_sbuf_tensor(name, [128, cols], dtype).ap()

    i128 = S("c_i128", 128)
    iota8 = S("c_iota8", E)
    gbb = S("c_gbb", E)
    ones = S("c_ones", 128)
    h0 = S("c_h0", 2 * NCI)
    gwT = S("r_gwT", ND * E)
    gwTp = S("r_gwTp", ND * E)
    b1s = S("r_b1", 9 * NH); b3s = S("r_b3", 9 * NH)
    b2sb = S("r_b2sb", D)
    rwf9 = S("r_rwf9", 2 * TOK)
    woutT = S("r_woutT", NCI * D)
    Zr = S("r_Zr", NCI * INNER); Zi = S("r_Zi", NCI * INNER); ZrN = S("r_ZrN", NCI * INNER)
    vT_r = S("r_vTr", NCI * INNER); vT_iN = S("r_vTiN", NCI * INNER)

    pool = S("pool", 11264)
    watb = S("watb", 4096)
    wbuf = S("wbuf", NH * D)
    x_tm = S("x_tm", 2 * D)
    xn_tm = S("xn_tm", 2 * D)
    scr = S("scr", 2 * D)
    scr2 = S("scr2", D)
    tmps = [S(f"tmp{i}", 128) for i in range(6)]
    gtmps = [S(f"gtmp{i}", 128) for i in range(4)]
    small = S("small", 64)
    carry_re = S("carry_re", 2 * 2 * NCI); carry_iN = S("carry_iN", 2 * 2 * NCI)
    scb_re = S("scb_re", 2 * NCI); scb_im = S("scb_im", 2 * NCI)
    scar_r = S("scar_r", NCI * 2 * NJ); scar_i = S("scar_i", NCI * 2 * NJ)
    gate = S("gate", 2 * 48)
    dumm = S("dumm", 64)
    rww = S("rww", 2 * 2 * E)

    def Rg(start, cols, t=pool):
        return t[:, start:start + cols]
    xn1_fm = Rg(0, ND * T)
    xr_re = Rg(0, NCI * T); xr_iN = Rg(NCI * T, NCI * T)
    a_re = scr[:, 0:NCI * T]; a_im = scr[:, NCI * T:2 * NCI * T]
    xn2_fm = Rg(2048, ND * T)
    xc_re = Rg(4096, NCI * T); xc_im = Rg(4096 + NCI * T, NCI * T)
    u_re = Rg(6144, NCI * T); u_im = Rg(6144 + NCI * T, NCI * T)
    P_re = Rg(8192, NCI * T); P_im = Rg(8192 + NCI * T, NCI * T)
    h_re = Rg(10240, NCI * T)
    hffn = woutT[:, 0:NH * T]
    hmul = S("hmul4", 4 * T)
    xn2_b = wbuf[:, 6144:8192]
    x_tm2 = wbuf[:, 8960:11008]
    v_r = Rg(0, 2048); v_iN = Rg(2048, 2048); vT_i = Rg(4096, 2048)
    Xr = Rg(6144, 2048); Xi = Rg(8192, 2048)
    Rr = x_tm[:, 0:2048]; Ri = xn_tm[:, 0:2048]; RiN = Rg(0, 2048, scr)
    eye = Rg(0, 2048, watb)
    BF16 = mybir.dt.bfloat16
    def bview(reg, half):
        return reg.bitcast(BF16)[:, half * 2048:(half + 1) * 2048]
    Xbr, Xbi = bview(Xr, 0), bview(Xr, 1)
    Zbr, Zbi = bview(Xi, 0), bview(Xi, 1)
    Rbr, Rbi = bview(Rr, 0), bview(Rr, 1)
    RbiN = bview(Ri, 0)
    vTbr, vTbi = bview(v_r, 0), bview(v_r, 1)
    vTbiN = bview(v_iN, 0)
    Xr_p = v_iN.bitcast(F32) if False else None
    wstage = watb

    pb = [nc.alloc_psum_tensor(f"pb{i}", [128, 512], F32).ap() for i in range(8)]

    def bpair(reg):
        n = reg.shape[1]
        bc = reg.bitcast(mybir.dt.bfloat16)
        return bc[:, 0:n], bc[:, n:2 * n]

    def hpair(hf):
        bc = hffn.bitcast(mybir.dt.bfloat16)
        return bc[:, hf * 2 * T: hf * 2 * T + T], bc[:, hf * 2 * T + T: hf * 2 * T + 2 * T]

    def mm3(terms):
        out = []
        for (lp, rp) in terms:
            lh, ll = lp
            rh, rl = rp
            out += [(lh, rh), (lh, rl), (ll, rh)]
        return out

    def psub(lhs_terms_p, lhs_terms_q, cols, evac):
        bP, bQ = p.take_banks(2)
        mm_group(bP, cols, lhs_terms_p)
        mm_group(bQ, cols, lhs_terms_q)
        tv = p.bump("te", "te")
        p.wait("dve", "te", tv)
        p.do("dve", lambda v, bP=bP, cols=cols: v.tensor_copy(scr2[:, 0:cols], pb[bP][:, 0:cols]))
        p.do("dve", lambda v, bQ=bQ, cols=cols: evac(v, scr2[:, 0:cols], pb[bQ]))
        p.bump("dve", "dve")
        p.release_bank(bP, "dve"); p.release_bank(bQ, "dve")

    xrh_re, xrl_re = bpair(xr_re)
    xrh_iN, xrl_iN = bpair(xr_iN)

    def prtile(pr, kc, mc):
        h, l = pr
        return (mtile(h, kc, mc), mtile(l, kc, mc))

    def xrpair(pr, kc):
        h, l = pr
        return (h[:, kc * T:(kc + 1) * T], l[:, kc * T:(kc + 1) * T])

    hreh, hrel = bpair(h_re)

    def mtile(msb, kc, mc):
        return msb[:, kc * INNER + mc * 128: kc * INNER + (mc + 1) * 128]

    def q5(ap):
        return ap.rearrange("p (c s j u) -> p c s j u", c=NCI, s=2, j=NJ)

    def q4(ap):
        return ap.rearrange("p (c s u) -> p c s u", c=NCI, s=2)

    def qcs(ap):
        return ap.rearrange("p (c s) -> p c s", c=NCI)

    p.dma("sync", i128, I128D)
    p.dma("sync", iota8, IOTA8)
    p.dma("sync", gbb, GBB)
    p.dma("sync", ones[0:1, :], ONES)
    p.dma("sync", h0[:, 0:NCI] if False else scr2[0:NCI, 0:128],
          H0R.rearrange("(c p) -> c p", p=128))
    p.dma("sync", scr2[32:32 + NCI, 0:128], H0I.rearrange("(c p) -> c p", p=128))
    p.dma("sync", b2sb[0:E, :], B2)
    p.dma("sync", b2sb[E:E + 1, :], SB2.unsqueeze(0))
    p.dma("sync", rwf9[E:E + 1, 0:TOK], ONES)
    p.dma("sync", rwf9[E:E + 1, TOK:2 * TOK], ONES)
    p.dma("sync", v_r.rearrange("p (c j) -> p c j", c=NCI),
          MVR.rearrange("(c p) j -> p c j", p=128))
    p.dma("sync", v_iN.rearrange("p (c j) -> p c j", c=NCI),
          MVI.rearrange("(c p) j -> p c j", p=128))
    p.dma("sync", scr[0:E, 0:D], GW)
    ld0 = p.dma("sync", wstage.rearrange("p (o j) -> p o j", o=ND),
                WOUT.rearrange("(o p) j -> p o j", p=128))

    p.wait("dve", "dw", ld0)
    p.do("dve", lambda v: v.memset(scb_re, 0.0))
    p.do("dve", lambda v: v.memset(scb_im, 0.0))
    p.bump("dve", "dve")

    def transp(src_ap, *evacs):
        kp, m = src_ap.shape[0], src_ap.shape[1]
        bp = src_ap.base_partition()
        b = p.take_banks(1)[0]
        p.do("te", lambda t, s=src_ap, b=b, kp=kp, m=m, bp=bp:
             t.transpose(pb[b][0:m, 0:kp], s, i128[bp:bp + kp, bp:bp + kp]))
        tv = p.bump("te", "te")
        p.wait("dve", "te", tv)
        for ev in evacs:
            p.do("dve", lambda v, b=b, ev=ev: ev(v, pb[b]))
        p.bump("dve", "dve")
        p.release_bank(b, "dve")

    p.wait("te", "dw", ld0)
    for (src, dst) in ((v_r, vT_r), (v_iN, vT_i)):
        for cj in range(NCI):
            for ci in range(NCI):
                transp(src[:, ci * INNER + cj * 128: ci * INNER + (cj + 1) * 128],
                       lambda v, ps, dst=dst, ci=ci, cj=cj:
                       v.tensor_copy(dst[:, cj * INNER + ci * 128: cj * INNER + (ci + 1) * 128],
                                     ps[:, 0:128]))
    p.do("dve", lambda v: v.tensor_scalar_mul(vT_iN, vT_i, -1.0))
    p.do("dve", lambda v: v.tensor_scalar_mul(v_iN, v_iN, -1.0))
    p.bump("dve", "dve")

    for c in range(NCI):
        for o in range(ND):
            transp(wstage[:, o * INNER + c * 128: o * INNER + (c + 1) * 128],
                   lambda v, ps, c=c, o=o:
                   v.tensor_copy(woutT[:, c * D + o * 128: c * D + (o + 1) * 128], ps[:, 0:128]))
    for c in range(ND):
        transp(scr[0:E, c * 128:(c + 1) * 128],
               lambda v, ps, c=c: v.tensor_copy(gwT[:, c * E:(c + 1) * E], ps[:, 0:E]))
    transp(scr2[0:NCI, 0:128],
           lambda v, ps: v.tensor_copy(h0[:, 0:NCI], ps[:, 0:NCI]))
    transp(scr2[32:32 + NCI, 0:128],
           lambda v, ps: v.tensor_copy(h0[:, NCI:2 * NCI], ps[:, 0:NCI]))
    p.do("dve", lambda v: v.tensor_copy(
        qcs(carry_re[:, 0:2 * NCI]), h0[:, 0:NCI].unsqueeze(2).to_broadcast([128, NCI, 2])))
    p.do("dve", lambda v: v.tensor_scalar(
        qcs(carry_iN[:, 0:2 * NCI]), h0[:, NCI:2 * NCI].unsqueeze(2).to_broadcast([128, NCI, 2]),
        -1.0, None, ALU.mult))
    p.bump("dve", "dve")
    for (BB, SBB, dst) in ((B1, SB1, b1s), (B3, SB3, b3s)):
        p.wait("sync", "dve", p.sems["dve"].n)
        p.dma("sync", scr[0:E, 0:HID], BB)
        ldb = p.dma("sync", scr[E:E + 1, 0:HID], SBB.unsqueeze(0))
        p.wait("te", "dw", ldb)
        for hc in range(NH):
            transp(scr[0:9, hc * 128:(hc + 1) * 128],
                   lambda v, ps, hc=hc, dst=dst: v.tensor_copy(
                       dst.rearrange("p (e h) -> p e h", e=9)[:, :, hc], ps[:, 0:9]))

    for (W_in, W_hbm) in ((WA, WAT_HBM), (WXm, WXT_HBM)):
        p.wait("sync", "te", p.sems["te"].n)
        p.wait("sync", "dve", p.sems["dve"].n)
        ld = p.dma("sync", wbuf[:, 0:ND * D].rearrange("p (o d) -> p o d", o=ND),
                   W_in.rearrange("(o p) d -> p o d", p=128))
        p.wait("te", "dw", ld)
        scb = scr.bitcast(mybir.dt.bfloat16)
        for dc in range(ND):
            for o in range(ND):
                b = p.take_banks(1)[0]
                p.do("te", lambda t, o=o, dc=dc, b=b:
                     t.transpose(pb[b][:, 0:128],
                                 wbuf[:, o * D + dc * 128: o * D + (dc + 1) * 128], i128))
                tv = p.bump("te", "te")
                p.wait("dve", "te", tv)
                p.do("dve", lambda v, b=b, o=o:
                     v.tensor_copy(scb[:, o * 64: o * 64 + 64], pb[b][:, 0:128:2]))
                p.do("dve", lambda v, b=b, o=o:
                     v.tensor_copy(scb[:, 512 + o * 64: 512 + o * 64 + 64], pb[b][:, 1:128:2]))
                p.guard("dve")
                p.do("dve", lambda v, b=b, o=o:
                     v.tensor_sub(scb[:, 1024 + o * 64: 1024 + o * 64 + 64], pb[b][:, 0:128:2],
                                  scb[:, o * 64: o * 64 + 64]))
                p.do("dve", lambda v, b=b, o=o:
                     v.tensor_sub(scb[:, 1536 + o * 64: 1536 + o * 64 + 64], pb[b][:, 1:128:2],
                                  scb[:, 512 + o * 64: 512 + o * 64 + 64]))
                p.bump("dve", "dve")
                p.release_bank(b, "dve")
            p.wait("gps", "dve", p.sems["dve"].n)
            st = p.dma("gps", W_hbm[dc * 128:(dc + 1) * 128, :], scb[:, 0:2048], sem="dst")
            p.wait("dve", "dst", st)
    watw_done = p.sems["dst"].n

    def repack(Wsrcs, Wdst, colblocks):
        p.wait("sync", "dw", p.sems["dw"].n)
        stg = watb[:, 2048:4096].bitcast(mybir.dt.bfloat16)
        cdim = ND if colblocks else NH
        for e in range(9):
            Ws = Wsrcs[0][e] if e < E else Wsrcs[1]
            p.wait("sync", "dst", p.sems["dst"].n)
            p.wait("sync", "gps", p.sems["gps"].n)
            ldw = p.dma("sync", wbuf.rearrange("p (c h) -> p c h", c=cdim),
                        Ws.rearrange("(c p) h -> p c h", p=128))
            p.wait("gps", "dw", ldw)
            for hf in range(NH):
                if colblocks:
                    blk = wbuf.rearrange("p (c h) -> p c h", c=ND)[:, :, hf * 128:(hf + 1) * 128]
                    sl = stg[:, 0:2048].rearrange("p (x c q) -> p x c q", x=2, c=ND)
                    hi, lo = sl[:, 0], sl[:, 1]
                else:
                    blk = wbuf[:, hf * D:(hf + 1) * D]
                    hi, lo = stg[:, 0:1024], stg[:, 1024:2048]
                p.wait("gps", "dst", p.sems["dst"].n)
                p.do("gps", lambda g, blk=blk, hi=hi: g.tensor_copy(hi, blk))
                p.guard("gps")
                p.do("gps", lambda g, blk=blk, hi=hi, lo=lo: g.tensor_tensor(lo, blk, hi, ALU.subtract))
                p.bump("gps", "gps")
                p.wait("gps", "gps", p.sems["gps"].n)
                p.dma("gps", Wdst[e, hf], stg[:, 0:2048] if not colblocks else None, sem="dst") \
                    if False else None
                p.dma("gps", Wdst[e, hf], stg[:, 0:2048], sem="dst")

    p.wait("sync", "dst", watw_done)
    ld_eye = p.dma("sync", eye.rearrange("p (c j) -> p c j", c=NCI),
                   EYE.rearrange("(c p) j -> p c j", p=128))
    repack((W1, SW1), W1P, True)
    repack((W3, SW3), W3P, True)
    repack((W2, SW2), W2P, False)
    repack_done = p.sems["dst"].n
    p.wait("dve", "dw", ld_eye)
    p.do("dve", lambda v: v.tensor_mul(scr[:, 0:2048], v_r, v_r))
    p.do("dve", lambda v: v.tensor_mul(scr2, v_iN[:, 0:D], v_iN[:, 0:D]))
    p.do("dve", lambda v: v.tensor_add(scr[:, 0:D], scr[:, 0:D], scr2))
    p.do("dve", lambda v: v.tensor_mul(scr2, v_iN[:, D:2 * D], v_iN[:, D:2 * D]))
    p.do("dve", lambda v: v.tensor_add(scr[:, D:2 * D], scr[:, D:2 * D], scr2))
    p.do("dve", lambda v: v.reduce_sum(small[:, 0:1], scr[:, 0:2048], axis=AX.X))
    p.do("dve", lambda v: v.memset(scr2[:, 0:1], 1.0))
    rsum_done = p.bump("dve", "dve")
    b = p.take_banks(1)[0]
    p.wait("te", "dve", rsum_done)
    p.do("te", lambda t, b=b: t.matmul(pb[b][0:1, 0:1], small[:, 0:1], scr2[:, 0:1],
                                       start=True, stop=True))
    tv = p.bump("te", "te")
    p.wait("dve", "te", tv)
    p.do("dve", lambda v, b=b: v.reciprocal(small[0:1, 2:3], pb[b][0:1, 0:1]))
    p.bump("dve", "dve")
    p.release_bank(b, "dve")
    b = p.take_banks(1)[0]
    p.wait("te", "dve")
    p.do("te", lambda t, b=b: t.matmul(pb[b][:, 0:1], ones[0:1, 0:128], small[0:1, 2:3],
                                       start=True, stop=True))
    tv = p.bump("te", "te")
    p.wait("dve", "te", tv)
    p.do("dve", lambda v, b=b: v.tensor_copy(small[:, 4:5], pb[b][:, 0:1]))
    p.bump("dve", "dve")
    p.release_bank(b, "dve")
    p.guard("dve")
    p.do("dve", lambda v: v.tensor_scalar(Xbr, vT_r, small[:, 4:5], None, ALU.mult))
    p.do("dve", lambda v: v.tensor_scalar(Xbi, vT_iN, small[:, 4:5], None, ALU.mult))
    p.do("dve", lambda v: v.tensor_scalar(Zbr, v_r, small[:, 4:5], None, ALU.mult))
    p.do("dve", lambda v: v.tensor_scalar(Zbi, v_iN, small[:, 4:5], None, ALU.mult))
    p.do("dve", lambda v: v.tensor_copy(vTbr, vT_r))
    p.do("dve", lambda v: v.tensor_copy(vTbi, vT_i))
    p.do("dve", lambda v: v.tensor_copy(vTbiN, vT_iN))
    nv = p.bump("dve", "dve")
    p.wait("te", "dve", nv)

    def mm_group(bank, out_cols, terms):
        n = len(terms)
        for i, (lt, rh) in enumerate(terms):
            p.do("te", lambda t, b=bank, lt=lt, rh=rh, i=i, n=n, oc=out_cols:
                 t.matmul(pb[b][:, 0:oc], lt, rh, start=(i == 0), stop=(i == n - 1)))

    def m512_terms(pairs, mc):
        out = []
        for (lsb, rsb) in pairs:
            for kc in range(NCI):
                out.append((mtile(lsb, kc, mc), rsb[:, kc * INNER:(kc + 1) * INNER]))
        return out

    def newton_iter(xr, xi, zr, zi, rr, ri, riN, vtr, vti, vtiN, cast):
        banks = p.take_banks(8)
        for mc in range(NCI):
            mm_group(banks[mc], INNER, m512_terms(((vtr, xr), (vtiN, xi)), mc))
            mm_group(banks[4 + mc], INNER, m512_terms(((vti, xr), (vtr, xi)), mc))
        t1v = p.bump("te", "te")
        p.wait("dve", "te", t1v)
        for mc in range(NCI):
            sl = slice(mc * INNER, (mc + 1) * INNER)
            p.do("dve", lambda v, b=banks[mc], sl=sl, rr=rr: v.tensor_sub(rr[:, sl], eye[:, sl], pb[b][:, 0:INNER]))
            p.do("dve", lambda v, b=banks[4 + mc], sl=sl, ri=ri: v.tensor_scalar_mul(ri[:, sl], pb[b][:, 0:INNER], -1.0))
            p.do("dve", lambda v, b=banks[4 + mc], sl=sl, riN=riN: v.tensor_copy(riN[:, sl], pb[b][:, 0:INNER]))
            p.bump("dve", "dve")
            p.release_bank(banks[mc], "dve"); p.release_bank(banks[4 + mc], "dve")
        p.wait("te", "dve")
        banks = p.take_banks(8)
        for mc in range(NCI):
            mm_group(banks[mc], INNER, m512_terms(((zr, rr), (zi, riN)), mc))
            mm_group(banks[4 + mc], INNER, m512_terms(((zr, ri), (zi, rr)), mc))
        dv = p.bump("te", "te")
        p.wait("dve", "te", dv)
        for mc in range(NCI):
            sl = slice(mc * INNER, (mc + 1) * INNER)
            p.do("dve", lambda v, b=banks[mc], sl=sl, xr=xr: v.tensor_add(xr[:, sl], xr[:, sl], pb[b][:, 0:INNER]))
            p.do("dve", lambda v, b=banks[4 + mc], sl=sl, xi=xi: v.tensor_add(xi[:, sl], xi[:, sl], pb[b][:, 0:INNER]))
            p.bump("dve", "dve")
            p.release_bank(banks[mc], "dve"); p.release_bank(banks[4 + mc], "dve")
        p.wait("te", "dve")
        banks = p.take_banks(8)
        for mc in range(NCI):
            mm_group(banks[mc], INNER, m512_terms(((rr, zr), (riN, zi)), mc))
            mm_group(banks[4 + mc], INNER, m512_terms(((ri, zr), (rr, zi)), mc))
        ev = p.bump("te", "te")
        p.wait("dve", "te", ev)
        for mc in range(NCI):
            sl = slice(mc * INNER, (mc + 1) * INNER)
            p.do("dve", lambda v, b=banks[mc], sl=sl, zr=zr: v.tensor_add(zr[:, sl], zr[:, sl], pb[b][:, 0:INNER]))
            p.do("dve", lambda v, b=banks[4 + mc], sl=sl, zi=zi: v.tensor_add(zi[:, sl], zi[:, sl], pb[b][:, 0:INNER]))
            p.bump("dve", "dve")
            p.release_bank(banks[mc], "dve"); p.release_bank(banks[4 + mc], "dve")
        p.wait("te", "dve")

    for it in range(34):
        newton_iter(Xbr, Xbi, Zbr, Zbi, Rbr, Rbi, RbiN, vTbr, vTbi, vTbiN, True)
    p.wait("dve", "te", p.sems["te"].n)
    p.do("dve", lambda v: v.tensor_copy(Zr, Zbr))
    p.do("dve", lambda v: v.tensor_copy(Zi, Zbi))
    p.do("dve", lambda v: v.tensor_copy(scr, Xbr))
    p.bump("dve", "dve")
    p.do("dve", lambda v: v.tensor_copy(Xi, Xbi))
    p.do("dve", lambda v: v.tensor_copy(Xr, scr))
    p.bump("dve", "dve")
    p.wait("te", "dve", p.sems["dve"].n)
    for it in range(newton_iters):
        newton_iter(Xr, Xi, Zr, Zi, Rr, Ri, RiN, vT_r, vT_i, vT_iN, False)

    tapdma("Zr", Zr); tapdma("Zi", Zi)
    zr_h, zr_l = bpair(ZrN)
    zi_h, zi_l = bpair(Zr)
    vtr_h, vtr_l = bpair(vT_iN)
    vti_h, vti_l = bpair(vT_r)
    gwh, gwl = bpair(gwTp)
    p.do("dve", lambda v: v.tensor_copy(zr_h, Zr))
    p.guard("dve")
    p.do("dve", lambda v: v.tensor_sub(zr_l, Zr, zr_h))
    p.do("dve", lambda v: v.tensor_copy(zi_h, Zi))
    p.guard("dve")
    p.do("dve", lambda v: v.tensor_sub(zi_l, Zi, zi_h))
    p.do("dve", lambda v: v.tensor_copy(vtr_h, vT_r))
    p.guard("dve")
    p.do("dve", lambda v: v.tensor_sub(vtr_l, vT_r, vtr_h))
    p.do("dve", lambda v: v.tensor_copy(vti_h, vT_i))
    p.guard("dve")
    p.do("dve", lambda v: v.tensor_sub(vti_l, vT_i, vti_h))
    p.do("dve", lambda v: v.tensor_copy(gwh, gwT))
    p.guard("dve")
    p.do("dve", lambda v: v.tensor_sub(gwl, gwT, gwh))
    p.bump("dve", "dve")
    scrb = scr.bitcast(mybir.dt.bfloat16)
    for c in range(NCI):
        p.wait("dve", "dst", p.sems["dst"].n)
        p.do("dve", lambda v, c=c: v.tensor_copy(scrb[:, 0:1024], woutT[:, c * D:(c + 1) * D]))
        p.guard("dve")
        p.do("dve", lambda v, c=c: v.tensor_sub(scrb[:, 1024:2048], woutT[:, c * D:(c + 1) * D],
                                                scrb[:, 0:1024]))
        p.bump("dve", "dve")
        p.wait("gps", "dve", p.sems["dve"].n)
        p.dma("gps", WOUTP[c * 128:(c + 1) * 128, :], scrb[:, 0:2048], sem="dst")
    p.wait("dve", "dst", p.sems["dst"].n)

    def ln(dst_fm, tapname, k, x_tm):
        p.wait("dve", "te", p.sems["te"].n)
        for j in range(2):
            p.do("dve", lambda v, j=j: v.reduce_sum(small[:, 8 + j:9 + j],
                                                    x_tm[:, j * D:(j + 1) * D], axis=AX.X))
        sv = p.bump("dve", "dve")
        p.wait("act", "dve", sv)
        for j in range(2):
            p.do("act", lambda s, j=j: s.activation(scr2, x_tm[:, j * D:(j + 1) * D], AF.Square,
                                                    accum_out=small[:, 10 + j:11 + j]))
        qv = p.bump("act", "act")
        p.wait("dve", "act", qv)
        for j in range(2):
            p.do("dve", lambda v, j=j: v.tensor_scalar_mul(small[:, 12 + j:13 + j],
                                                           small[:, 8 + j:9 + j], 1.0 / D))
            p.do("dve", lambda v, j=j: v.tensor_scalar_mul(small[:, 14 + j:15 + j],
                                                           small[:, 10 + j:11 + j], 1.0 / D))
        p.guard("dve")
        for j in range(2):
            p.do("dve", lambda v, j=j: v.tensor_scalar(small[:, 16 + j:17 + j],
                                                       small[:, 12 + j:13 + j],
                                                       small[:, 12 + j:13 + j], -1.0,
                                                       ALU.mult, ALU.mult))
        p.guard("dve")
        for j in range(2):
            p.do("dve", lambda v, j=j: v.tensor_add(small[:, 14 + j:15 + j],
                                                    small[:, 14 + j:15 + j], small[:, 16 + j:17 + j]))
            p.do("dve", lambda v, j=j: v.tensor_scalar_add(small[:, 14 + j:15 + j],
                                                           small[:, 14 + j:15 + j], EPS))
        p.guard("dve")
        for j in range(2):
            p.do("dve", lambda v, j=j: v.reciprocal(small[:, 18 + j:19 + j], small[:, 14 + j:15 + j]))
        rv = p.bump("dve", "dve")
        p.wait("act", "dve", rv)
        for j in range(2):
            p.do("act", lambda s, j=j: s.activation(small[:, 20 + j:21 + j],
                                                    small[:, 18 + j:19 + j], AF.Sqrt))
        rstdv = p.bump("act", "act")
        p.wait("dve", "act", rstdv)
        for j in range(2):
            p.do("dve", lambda v, j=j:
                 v.tensor_scalar(xn_tm[:, j * D:(j + 1) * D], x_tm[:, j * D:(j + 1) * D],
                                 small[:, 12 + j:13 + j], small[:, 20 + j:21 + j],
                                 ALU.subtract, ALU.mult))
        xnv = p.bump("dve", "dve")
        if k == 0:
            tapdma(tapname, xn_tm)
        return xnv

    def ln_tp(dst_fm, xnv):
        p.wait("te", "dve", xnv)
        dh, dl = bpair(dst_fm)
        for c in range(ND):
            for j in range(2):
                sl = slice(c * T + j * 128, c * T + (j + 1) * 128)
                transp(xn_tm[:, j * D + c * 128: j * D + (c + 1) * 128],
                       lambda v, ps, sl=sl: v.tensor_copy(dh[:, sl], ps[:, 0:128]),
                       lambda v, ps, sl=sl: v.tensor_tensor(dl[:, sl], ps[:, 0:128],
                                                            dh[:, sl], ALU.subtract))
        return p.sems["dve"].n

    def proj_and_factor(W_hbm, dst_re, dst_im, kind, key, xv, nxt):
        p.wait("te", "dve", xv)
        banks = p.take_banks(8)
        wbv = watb.bitcast(mybir.dt.bfloat16)
        x1h, x1l = bpair(xn1_fm)
        for tn in range(4):
            p.wait("te", *ws[(key, tn)])
            for m in range(8):
                for kcl in range(2):
                    kca = 2 * tn + kcl
                    base = (tn % 2) * 4096 + kcl * 2048
                    wh = wbv[:, base + m * 128: base + (m + 1) * 128]
                    wl = wbv[:, base + 1024 + m * 128: base + 1024 + (m + 1) * 128]
                    xh = x1h[:, kca * T:(kca + 1) * T]
                    xl = x1l[:, kca * T:(kca + 1) * T]
                    for ti, (lt, rh) in enumerate(((wh, xh), (wh, xl), (wl, xh))):
                        p.do("te", lambda t, b=banks[m], lt=lt, rh=rh, st=(kca == 0 and ti == 0),
                             sp=(kca == 7 and ti == 2):
                             t.matmul(pb[b][:, 0:T], lt, rh, start=st, stop=sp))
            pv = p.bump("te", "te")
            if nxt[tn] is not None:
                k2, t2, W2 = nxt[tn]
                wissue(k2, t2, W2, pv)
        p.wait("dve", "te", pv)
        fs = xn_tm
        for cc in range(NCI):
            p.do("dve", lambda v, b=banks[cc], cc=cc:
                 v.tensor_copy(fs[:, cc * 512: cc * 512 + T], pb[b][:, 0:T]))
            p.bump("dve", "dve")
            p.release_bank(banks[cc], "dve")
            p.do("dve", lambda v, b=banks[4 + cc], cc=cc:
                 v.tensor_copy(fs[:, cc * 512 + T:(cc + 1) * 512], pb[b][:, 0:T]))
            p.bump("dve", "dve")
            p.release_bank(banks[4 + cc], "dve")
        p.wait("act", "dve", p.sems["dve"].n)
        m2f = pool[:, 8192:10240]
        p.do("act", lambda s: s.activation(m2f, fs, AF.Square))
        sqv = p.bump("act", "act")
        p.wait("dve", "act", sqv)
        m2 = m2f.rearrange("p (c a u) -> p c a u", c=NCI, a=2)
        s2c = scr2.rearrange("p (c u) -> p c u", c=NCI)
        p.do("dve", lambda v: v.tensor_add(s2c, m2[:, :, 0, :], m2[:, :, 1, :]))
        m2v = p.bump("dve", "dve")
        p.wait("act", "dve", m2v)
        if kind == "a":
            p.do("act", lambda s: s.activation(m2f[:, 0:D], scr2, AF.Sqrt))
            rtv = p.bump("act", "act")
            p.wait("dve", "act", rtv)
            p.do("dve", lambda v: v.tensor_scalar_add(scr2, scr2, 1.0))
            p.guard("dve")
            p.do("dve", lambda v: v.reciprocal(scr2, scr2))
            p.guard("dve")
            p.do("dve", lambda v: v.tensor_mul(scr2, scr2, m2f[:, 0:D]))
            p.guard("dve")
        else:
            p.do("act", lambda s: s.activation(scr2, scr2, AF.Sqrt))
            rtv = p.bump("act", "act")
            p.wait("dve", "act", rtv)
        fre = fs.rearrange("p (c a u) -> p c a u", c=NCI, a=2)
        p.do("dve", lambda v: v.tensor_mul(dst_re.rearrange("p (c u) -> p c u", c=NCI),
                                           fre[:, :, 0, :], s2c))
        p.do("dve", lambda v: v.tensor_mul(dst_im.rearrange("p (c u) -> p c u", c=NCI),
                                           fre[:, :, 1, :], s2c))
        p.bump("dve", "dve")

    def scan_inner(ta, tb, st):
        qa_r, qa_i = q5(a_re), q5(a_im)
        qu_r, qu_i = q5(u_re), q5(u_im)
        qP_r, qP_i = q5(P_re), q5(P_im)
        def t128(i):
            return tmps[i][:, 0:NCI * 2 * NJ].rearrange("p (c s j) -> p c s j", c=NCI, s=2)
        def g128(i):
            return gtmps[i][:, 0:NCI * 2 * NJ].rearrange("p (c s j) -> p c s j", c=NCI, s=2)
        t1_, t2_, t3_, t4_, t5_, t6_ = (t128(i) for i in range(6))
        g1, g2, g3, g4 = (g128(i) for i in range(4))
        t1, t2, t3, t4, t5, t6 = t1_, t2_, t3_, t4_, t5_, t6_
        p.wait("gps", "dve", st["uv"])
        for t in range(ta, tb):
            ar, ai = qa_r[:, :, :, :, t], qa_i[:, :, :, :, t]
            ur, ui = qu_r[:, :, :, :, t], qu_i[:, :, :, :, t]
            pr, pi = qP_r[:, :, :, :, t], qP_i[:, :, :, :, t]
            if t == 0:
                p.do("dve", lambda v, ar=ar, ur=ur: v.tensor_mul(t1, ar, ur))
                p.do("dve", lambda v, ai=ai, ui=ui: v.tensor_mul(t2, ai, ui))
                p.do("dve", lambda v, ar=ar, ui=ui: v.tensor_mul(t3, ar, ui))
                p.do("dve", lambda v, ai=ai, ur=ur: v.tensor_mul(t4, ai, ur))
                p.do("dve", lambda v, ur=ur: v.tensor_sub(ur, t1, t2))
                p.do("dve", lambda v, ui=ui: v.tensor_add(ui, t3, t4))
                p.do("gps", lambda g, pr=pr, ar=ar: g.tensor_copy(pr, ar))
                p.do("gps", lambda g, pi=pi, ai=ai: g.tensor_copy(pi, ai))
            else:
                urp, uip = qu_r[:, :, :, :, t - 1], qu_i[:, :, :, :, t - 1]
                prp, pip = qP_r[:, :, :, :, t - 1], qP_i[:, :, :, :, t - 1]
                p.do("dve", lambda v, urp=urp, ur=ur: v.tensor_add(t5, urp, ur))
                p.do("dve", lambda v, uip=uip, ui=ui: v.tensor_add(t6, uip, ui))
                p.do("dve", lambda v, ar=ar: v.tensor_mul(t1, ar, t5))
                p.do("dve", lambda v, ai=ai: v.tensor_mul(t2, ai, t6))
                p.do("dve", lambda v, ur=ur: v.tensor_sub(ur, t1, t2))
                p.do("dve", lambda v, ar=ar: v.tensor_mul(t3, ar, t6))
                p.do("dve", lambda v, ai=ai: v.tensor_mul(t4, ai, t5))
                p.do("dve", lambda v, ui=ui: v.tensor_add(ui, t3, t4))
                p.do("gps", lambda g, ar=ar, prp=prp: g.tensor_mul(g1, ar, prp))
                p.do("gps", lambda g, ai=ai, pip=pip: g.tensor_mul(g2, ai, pip))
                p.do("gps", lambda g, pr=pr: g.tensor_sub(pr, g1, g2))
                p.do("gps", lambda g, ar=ar, pip=pip: g.tensor_mul(g3, ar, pip))
                p.do("gps", lambda g, ai=ai, prp=prp: g.tensor_mul(g4, ai, prp))
                p.do("gps", lambda g, pi=pi: g.tensor_add(pi, g3, g4))
        p.bump("dve", "dve")
        st["ptv"] = p.bump("gps", "gps")

    def scan_chain(ja, jb, st):
        qu_r, qu_i = q5(u_re), q5(u_im)
        qP_r, qP_i = q5(P_re), q5(P_im)
        qs_r = scar_r.rearrange("p (c s j) -> p c s j", c=NCI, s=2)
        qs_i = scar_i.rearrange("p (c s j) -> p c s j", c=NCI, s=2)
        def t8(i):
            return tmps[i][:, 0:8].rearrange("p (c s) -> p c s", c=NCI)
        t1b, t2b, t3b, t4b, t5b = (t8(i) for i in range(5))
        p.wait("dve", "gps", st["ptv"])
        if ja == 1:
            p.do("dve", lambda v: v.tensor_copy(qs_r[:, :, :, 0], qcs(scb_re)))
            p.do("dve", lambda v: v.tensor_copy(qs_i[:, :, :, 0], qcs(scb_im)))
        for j in range(ja, jb):
            Ar = qP_r[:, :, :, j - 1, LC - 1]; Ai = qP_i[:, :, :, j - 1, LC - 1]
            Br = qu_r[:, :, :, j - 1, LC - 1]; Bi = qu_i[:, :, :, j - 1, LC - 1]
            Srp = qs_r[:, :, :, j - 1]; Sip = qs_i[:, :, :, j - 1]
            Sr = qs_r[:, :, :, j] if j < NJ else qcs(scb_re)
            Si = qs_i[:, :, :, j] if j < NJ else qcs(scb_im)
            p.do("dve", lambda v, Ar=Ar, Srp=Srp: v.tensor_mul(t1b, Ar, Srp))
            p.do("dve", lambda v, Ai=Ai, Sip=Sip: v.tensor_mul(t2b, Ai, Sip))
            p.do("dve", lambda v, Ar=Ar, Sip=Sip: v.tensor_mul(t3b, Ar, Sip))
            p.do("dve", lambda v, Ai=Ai, Srp=Srp: v.tensor_mul(t4b, Ai, Srp))
            p.do("dve", lambda v: v.tensor_sub(t5b, t1b, t2b))
            p.do("dve", lambda v, Sr=Sr, Br=Br: v.tensor_add(Sr, t5b, Br))
            p.do("dve", lambda v: v.tensor_add(t5b, t3b, t4b))
            p.do("dve", lambda v, Si=Si, Bi=Bi: v.tensor_add(Si, t5b, Bi))
        p.bump("dve", "dve")

    def scan_app(st):
        qP_r, qP_i = q5(P_re), q5(P_im)
        p.wait("dve", "gps", st["ptv"])
        qs_r = scar_r.rearrange("p (c s j) -> p c s j", c=NCI, s=2)
        qs_i = scar_i.rearrange("p (c s j) -> p c s j", c=NCI, s=2)
        Srb = qs_r.unsqueeze(4).to_broadcast([128, NCI, 2, NJ, LC])
        Sib = qs_i.unsqueeze(4).to_broadcast([128, NCI, 2, NJ, LC])
        sA = scr[:, 0:1024].rearrange("p (c s j u) -> p c s j u", c=NCI, s=2, j=NJ)
        sB = scr[:, 1024:2048].rearrange("p (c s j u) -> p c s j u", c=NCI, s=2, j=NJ)
        p.do("dve", lambda v: v.tensor_mul(sA, qP_r, Srb))
        p.do("dve", lambda v: v.tensor_mul(sB, qP_i, Sib))
        p.do("dve", lambda v: v.tensor_sub(scr[:, 0:1024], scr[:, 0:1024], scr[:, 1024:2048]))
        p.do("dve", lambda v: v.tensor_add(u_re, u_re, scr[:, 0:1024]))
        p.do("dve", lambda v: v.tensor_mul(sA, qP_r, Sib))
        p.do("dve", lambda v: v.tensor_mul(sB, qP_i, Srb))
        p.do("dve", lambda v: v.tensor_add(scr[:, 0:1024], scr[:, 0:1024], scr[:, 1024:2048]))
        p.do("dve", lambda v: v.tensor_add(u_im, u_im, scr[:, 0:1024]))
        p.bump("dve", "dve")

    def gate_topk(k, xn2q, x2v):
        rwo = (k % 2) * 2 * E
        p.wait("te", "dve", x2v)
        p.wait("dve", "te", p.sems["te"].n)
        x2h, x2l = bpair(xn2q)
        for j in range(2):
            b = p.take_banks(1)[0]
            terms = []
            for kc in range(ND):
                lh = x2h[:, kc * T + j * 128: kc * T + (j + 1) * 128]
                ll = x2l[:, kc * T + j * 128: kc * T + (j + 1) * 128]
                rh = gwh[:, kc * E:(kc + 1) * E]
                rl = gwl[:, kc * E:(kc + 1) * E]
                terms += [(lh, rh), (lh, rl), (ll, rh)]
            mm_group(b, E, terms)
            tv = p.bump("te", "te")
            p.wait("dve", "te", tv)
            p.do("dve", lambda v, b=b, j=j: v.tensor_copy(gate[:, j * 48: j * 48 + E], pb[b][:, 0:E]))
            p.bump("dve", "dve")
            p.release_bank(b, "dve")
        def g(j, off, n=E):
            return gate[:, j * 48 + off: j * 48 + off + n]
        for j in range(2):
            p.do("dve", lambda v, j=j: v.reduce_max(small[:, 24 + j:25 + j], g(j, 0), axis=AX.X))
        p.guard("dve")
        for j in range(2):
            p.do("dve", lambda v, j=j: v.tensor_scalar_mul(small[:, 26 + j:27 + j],
                                                           small[:, 24 + j:25 + j], -1.0))
        ngv = p.bump("dve", "dve")
        p.wait("act", "dve", ngv)
        for j in range(2):
            p.do("act", lambda s, j=j: s.activation(g(j, 8), g(j, 0), AF.Exp,
                                                    bias=small[:, 26 + j:27 + j]))
        ev = p.bump("act", "act")
        p.wait("dve", "act", ev)
        for j in range(2):
            p.do("dve", lambda v, j=j: v.reduce_sum(small[:, 28 + j:29 + j], g(j, 8), axis=AX.X))
        p.guard("dve")
        for j in range(2):
            p.do("dve", lambda v, j=j: v.reciprocal(small[:, 30 + j:31 + j], small[:, 28 + j:29 + j]))
        p.guard("dve")
        for j in range(2):
            p.do("dve", lambda v, j=j: v.tensor_scalar(g(j, 8), g(j, 8), small[:, 30 + j:31 + j],
                                                       None, ALU.mult))
            p.do("dve", lambda v, j=j: v.tensor_add(g(j, 16), g(j, 8), gbb))
        p.guard("dve")
        for pass_i in range(2):
            for j in range(2):
                p.do("dve", lambda v, j=j: v.reduce_max(small[:, 32 + j:33 + j], g(j, 16), axis=AX.X))
            p.guard("dve")
            for j in range(2):
                p.do("dve", lambda v, j=j: v.tensor_scalar(g(j, 24), g(j, 16),
                                                           small[:, 32 + j:33 + j], None, ALU.is_ge))
            p.guard("dve")
            for j in range(2):
                p.do("dve", lambda v, j=j: v.tensor_scalar(g(j, 32), g(j, 24), -BIG, BIG,
                                                           ALU.mult, ALU.add))
                p.do("dve", lambda v, j=j: v.tensor_add(g(j, 32), g(j, 32), iota8))
            p.guard("dve")
            for j in range(2):
                p.do("dve", lambda v, j=j: v.tensor_reduce(small[:, 34 + j:35 + j], g(j, 32),
                                                           AX.X, ALU.min))
            p.guard("dve")
            for j in range(2):
                p.do("dve", lambda v, j=j: v.tensor_scalar(g(j, 24), iota8,
                                                           small[:, 34 + j:35 + j], None, ALU.is_equal))
            p.guard("dve")
            for j in range(2):
                p.do("dve", lambda v, j=j: v.tensor_mul(g(j, 40), g(j, 8), g(j, 24)))
            p.guard("dve")
            for j in range(2):
                p.do("dve", lambda v, j=j, pi=pass_i:
                     v.reduce_sum(small[:, 36 + 2 * j + pi:37 + 2 * j + pi], g(j, 40), axis=AX.X))
            p.guard("dve")
            for j in range(2):
                if pass_i == 0:
                    p.do("dve", lambda v, j=j, rwo=rwo: v.tensor_scalar(
                        rww[:, rwo + j * E: rwo + (j + 1) * E], g(j, 24),
                        small[:, 36 + 2 * j:37 + 2 * j], None, ALU.mult))
                    p.do("dve", lambda v, j=j: v.tensor_scalar(g(j, 40), g(j, 24), BIG, None, ALU.mult))
                    p.do("dve", lambda v, j=j: v.tensor_sub(g(j, 16), g(j, 16), g(j, 40)))
                else:
                    p.do("dve", lambda v, j=j: v.tensor_scalar(g(j, 40), g(j, 24),
                                                               small[:, 37 + 2 * j:38 + 2 * j],
                                                               None, ALU.mult))
                    p.do("dve", lambda v, j=j, rwo=rwo: v.tensor_add(
                        rww[:, rwo + j * E: rwo + (j + 1) * E],
                        rww[:, rwo + j * E: rwo + (j + 1) * E], g(j, 40)))
            p.guard("dve")

    wslot_free = [p.sems["te"].n] * 6
    ffn_base = [0]

    x_tm_a = x_tm
    def xt_of(k):
        return x_tm_a if k % 2 == 0 else x_tm2

    def xn2_of(k):
        return xn2_fm if k % 2 == 0 else xn2_b

    ws = {}
    wst = {"wout_cons": 0}
    gsub = {}
    wbv16 = watb.bitcast(mybir.dt.bfloat16)

    def wissue(key, t, W_hbm, anchor, k0=False):
        if k0:
            p.wait("act", "dst", p.sems["dst"].n, force=True)
        else:
            p.wait("act", "te", max(anchor, 1), force=True)
        dst = wbv16[:, (t % 2) * 4096:(t % 2) * 4096 + 4096].rearrange("p (c f) -> p c f", c=2)
        src = W_hbm.rearrange("(c p) f -> p c f", p=128)[:, 2 * t:2 * t + 2, :]
        sem = "dwc" if t % 2 == 0 else "dwd"
        ws[(key, t)] = (sem, p.dma("act", dst, src, sem=sem))

    def stages_A(k):
        x_tm = xt_of(k)
        asx = {}
        def s_wd():
            wissue("wa", 0, WAT_HBM, wst["wout_cons"], k0=(k == 0))
            wissue("wa", 1, WAT_HBM, wst["wout_cons"], k0=(k == 0))
        def s_xln():
            p.wait("gps", "dst", p.sems["dst"].n)
            p.wait("gps", "dve", p.sems["dve"].n)
            p.wait("gps", "te", p.sems["te"].n) if k == 0 else None
            ldx = p.dma("gps", x_tm.rearrange("p (s d) -> p s d", s=2),
                        X.rearrange("(s n) d -> n s d", s=2)[k * TOK:(k + 1) * TOK], sem="dx")
            p.wait("dve", "dx", ldx)
            p.wait("act", "dx", ldx)
            asx["xnv"] = ln(xn1_fm, "ln1", k, x_tm)
        def s_xtp():
            asx["ln1tp"] = ln_tp(xn1_fm, asx["xnv"])
        def s_pa():
            proj_and_factor(WAT_HBM, a_re, a_im, "a", "wa", asx["ln1tp"],
                            [("wa", 2, WAT_HBM), ("wa", 3, WAT_HBM),
                             ("wx", 0, WXT_HBM), ("wx", 1, WXT_HBM)])
            if k == 0:
                tapdma("afac", a_re); tapdma("aim", a_im)
        def s_px():
            proj_and_factor(WXT_HBM, xc_re, xc_im, "x", "wx", asx["ln1tp"],
                            [("wx", 2, WXT_HBM), ("wx", 3, WXT_HBM),
                             ("wo", 0, WOUTP), ("wo", 1, WOUTP)])
            if k == 0:
                tapdma("xcre", xc_re); tapdma("xcim", xc_im)
        def s_xr():
            cin = (k % 2) * 2 * NCI
            cout = ((k + 1) % 2) * 2 * NCI
            q4b = lambda ap: ap.rearrange("p (c s u) -> p c s u", c=NCI, s=2)
            p.do("dve", lambda v: v.tensor_copy(q4b(xrh_re)[:, :, :, 1:TOK], q4(xc_re)[:, :, :, 0:TOK - 1]))
            p.do("dve", lambda v: v.tensor_scalar(q4b(xrh_iN)[:, :, :, 1:TOK],
                                                  q4(xc_im)[:, :, :, 0:TOK - 1], -1.0, None, ALU.mult))
            p.do("dve", lambda v, cin=cin: v.tensor_copy(q4b(xrh_re)[:, :, :, 0],
                                                         qcs(carry_re[:, cin:cin + 2 * NCI])))
            p.do("dve", lambda v, cin=cin: v.tensor_copy(q4b(xrh_iN)[:, :, :, 0],
                                                         qcs(carry_iN[:, cin:cin + 2 * NCI])))
            p.guard("dve")
            p.do("dve", lambda v: v.tensor_tensor(q4b(xrl_re)[:, :, :, 1:TOK],
                                                  q4(xc_re)[:, :, :, 0:TOK - 1],
                                                  q4b(xrh_re)[:, :, :, 1:TOK], ALU.subtract))
            p.do("dve", lambda v: v.scalar_tensor_tensor(q4b(xrl_iN)[:, :, :, 1:TOK],
                                                         q4(xc_im)[:, :, :, 0:TOK - 1], -1.0,
                                                         q4b(xrh_iN)[:, :, :, 1:TOK],
                                                         ALU.mult, ALU.subtract))
            p.do("dve", lambda v, cin=cin: v.tensor_tensor(q4b(xrl_re)[:, :, :, 0],
                                                           qcs(carry_re[:, cin:cin + 2 * NCI]),
                                                           q4b(xrh_re)[:, :, :, 0], ALU.subtract))
            p.do("dve", lambda v, cin=cin: v.tensor_tensor(q4b(xrl_iN)[:, :, :, 0],
                                                           qcs(carry_iN[:, cin:cin + 2 * NCI]),
                                                           q4b(xrh_iN)[:, :, :, 0], ALU.subtract))
            p.do("dve", lambda v, cout=cout: v.tensor_copy(qcs(carry_re[:, cout:cout + 2 * NCI]),
                                                           q4(xc_re)[:, :, :, TOK - 1]))
            p.do("dve", lambda v, cout=cout: v.tensor_scalar(qcs(carry_iN[:, cout:cout + 2 * NCI]),
                                                             q4(xc_im)[:, :, :, TOK - 1],
                                                             -1.0, None, ALU.mult))
            xrv = p.bump("dve", "dve")
            p.wait("te", "dve", xrv)

        def s_u():
            zr_pr = (zr_h, zr_l); zi_pr = (zi_h, zi_l)
            xre_pr = (xrh_re, xrl_re); xiN_pr = (xrh_iN, xrl_iN)
            for mc in range(NCI):
                terms_re = mm3([(prtile(zr_pr, kc, mc), xrpair(xre_pr, kc)) for kc in range(NCI)] +
                               [(prtile(zi_pr, kc, mc), xrpair(xiN_pr, kc)) for kc in range(NCI)])
                bre = p.take_banks(1)[0]
                mm_group(bre, T, terms_re)
                uv = p.bump("te", "te")
                p.wait("dve", "te", uv)
                sl = slice(mc * T, (mc + 1) * T)
                p.do("dve", lambda v, b=bre, sl=sl: v.tensor_copy(u_re[:, sl], pb[b][:, 0:T]))
                p.bump("dve", "dve")
                p.release_bank(bre, "dve")
                tP = mm3([(prtile(zi_pr, kc, mc), xrpair(xre_pr, kc)) for kc in range(NCI)])
                tQ = mm3([(prtile(zr_pr, kc, mc), xrpair(xiN_pr, kc)) for kc in range(NCI)])
                psub(tP, tQ, T,
                     lambda v, pP, pQ, sl=sl: v.tensor_tensor(u_im[:, sl], pP[:, 0:T], pQ[:, 0:T],
                                                              ALU.subtract))
            asx["uv"] = p.sems["dve"].n
            if k == 0:
                tapdma("ure", u_re); tapdma("uim", u_im)
        out = [(6, s_wd), (40, s_xln), (48, s_xtp), (65, s_pa), (98, s_px),
               (116, s_xr), (130, s_u)]
        scpts = [140, 145, 150, 155, 160, 165, 170, 175]
        nsl = len(scpts)
        for si in range(nsl):
            ta = (LC * si) // nsl
            tb = (LC * (si + 1)) // nsl
            out.append((scpts[si], lambda ta=ta, tb=tb: scan_inner(ta, tb, asx)))
        out.append((181, lambda: scan_chain(1, NJ // 2 + 1, asx)))
        out.append((186, lambda: scan_chain(NJ // 2 + 1, NJ + 1, asx)))
        out.append((191, lambda: scan_app(asx)))
        return out

    def stages_B(k):
        x_tm = xt_of(k)
        bs = {}
        def s_h():
            s_reh, s_rel = bpair(a_re)
            s_imh, s_iml = bpair(a_im)
            p.do("dve", lambda v: v.tensor_copy(s_reh, u_re))
            p.do("dve", lambda v: v.tensor_copy(s_imh, u_im))
            p.guard("dve")
            p.do("dve", lambda v: v.tensor_sub(s_rel, u_re, s_reh))
            p.do("dve", lambda v: v.tensor_sub(s_iml, u_im, s_imh))
            spv = p.bump("dve", "dve")
            p.wait("te", "dve", spv)
            vtr_pr = (vtr_h, vtr_l); vti_pr = (vti_h, vti_l)
            sre_pr = (s_reh, s_rel); sim_pr = (s_imh, s_iml)
            for mc in range(NCI):
                sl = slice(mc * T, (mc + 1) * T)
                tP = mm3([(prtile(vtr_pr, kc, mc), xrpair(sre_pr, kc)) for kc in range(NCI)])
                tQ = mm3([(prtile(vti_pr, kc, mc), xrpair(sim_pr, kc)) for kc in range(NCI)])
                def hevac(v, pP, pQ, sl=sl, mc=mc):
                    return v.tensor_tensor(xn_tm[:, mc * T:(mc + 1) * T], pP[:, 0:T], pQ[:, 0:T],
                                           ALU.subtract)
                psub(tP, tQ, T, hevac)
                p.do("dve", lambda v, sl=sl, mc=mc: v.tensor_add(xn_tm[:, mc * T:(mc + 1) * T],
                                                                 xn_tm[:, mc * T:(mc + 1) * T],
                                                                 xc_re[:, sl]))
                p.guard("dve")
                p.do("dve", lambda v, sl=sl, mc=mc: v.tensor_copy(hreh[:, sl], xn_tm[:, mc * T:(mc + 1) * T]))
                p.guard("dve")
                p.do("dve", lambda v, sl=sl, mc=mc: v.tensor_tensor(hrel[:, sl],
                                                                    xn_tm[:, mc * T:(mc + 1) * T],
                                                                    hreh[:, sl], ALU.subtract))
                p.bump("dve", "dve")
            bs["hpv"] = p.sems["dve"].n
            if k == 0:
                tapdma("sre", u_re); tapdma("sim", u_im)
        def s_op():
            p.wait("te", "dve", bs["hpv"])
            p.wait("te", *ws[("wo", 0)])
            p.wait("te", *ws[("wo", 1)])
            wov = watb.bitcast(mybir.dt.bfloat16)
            for j in range(2):
                for n in range(2):
                    b = p.take_banks(1)[0]
                    terms = []
                    for cc in range(NCI):
                        lh = hreh[:, cc * T + j * 128: cc * T + (j + 1) * 128]
                        ll = hrel[:, cc * T + j * 128: cc * T + (j + 1) * 128]
                        rh = wov[:, cc * 2048 + n * 512: cc * 2048 + (n + 1) * 512]
                        rl = wov[:, cc * 2048 + 1024 + n * 512: cc * 2048 + 1024 + (n + 1) * 512]
                        terms += [(lh, rh), (lh, rl), (ll, rh)]
                    mm_group(b, 512, terms)
                    ov = p.bump("te", "te")
                    p.wait("dve", "te", ov)
                    sl = slice(j * D + n * 512, j * D + (n + 1) * 512)
                    p.do("dve", lambda v, b=b, sl=sl: v.tensor_add(x_tm[:, sl], x_tm[:, sl], pb[b][:, 0:512]))
                    p.bump("dve", "dve")
                    p.release_bank(b, "dve")
            wst["wout_cons"] = p.sems["te"].n
            if k == 0:
                tapdma("x1", x_tm)
        def s_l2():
            bs["xn2v"] = ln(xn2_of(k), "ln2", k, x_tm)
        def s_l2tp():
            bs["x2v"] = ln_tp(xn2_of(k), bs["xn2v"])
        def s_g():
            gate_topk(k, xn2_of(k), bs["x2v"])
            bs["rwv"] = p.sems["dve"].n
            if k == 0:
                tapdma("rww", rww[:, 0:2 * E])
        def s_rb():
            p.wait("te", "dve", bs["rwv"])
            rwo = (k % 2) * 2 * E
            for j in range(2):
                b = p.take_banks(1)[0]
                p.do("te", lambda t, b=b, j=j, rwo=rwo: t.transpose(pb[b][0:E, 0:128],
                                                                    rww[:, rwo + j * E: rwo + (j + 1) * E],
                                                                    i128))
                tv = p.bump("te", "te")
                p.wait("dve", "te", tv)
                p.do("dve", lambda v, b=b, j=j: v.tensor_copy(rwf9[0:E, j * TOK:(j + 1) * TOK],
                                                              pb[b][0:E, 0:128]))
                p.bump("dve", "dve")
                p.release_bank(b, "dve")
            p.wait("te", "dve")
            for j in range(2):
                for n in range(2):
                    b = p.take_banks(1)[0]
                    p.do("te", lambda t, b=b, j=j, n=n:
                         t.matmul(pb[b][:, 0:512], rwf9[0:9, j * TOK:(j + 1) * TOK],
                                  b2sb[0:9, n * 512:(n + 1) * 512], start=True, stop=True))
                    tv = p.bump("te", "te")
                    p.wait("dve", "te", tv)
                    sl = slice(j * D + n * 512, j * D + (n + 1) * 512)
                    p.do("dve", lambda v, b=b, sl=sl: v.tensor_add(x_tm[:, sl], x_tm[:, sl],
                                                                   pb[b][:, 0:512]))
                    p.bump("dve", "dve")
                    p.release_bank(b, "dve")


        return [(208, s_h), (228, s_op), (242, s_l2), (250, s_l2tp),
                (260, s_g), (275, s_rb)]

    for _pt, f in stages_A(0):
        f()
    for _pt, f in stages_B(0):
        f()
    pend = {}
    for k in range(NCH):
        x_tm = xt_of(k)
        rwo = (k % 2) * 2 * E
        if k == 0:
            p.wait("sync", "dst", repack_done)
        units = []
        for e in range(9):
            W1s = W1[e] if e < E else SW1
            W3s = W3[e] if e < E else SW3
            W2s = W2[e] if e < E else SW2
            hstate = {}
            for hf in range(NH):
                def dma_fn(slot, sem, e=e, hf=hf):
                    return p.dma("sync", slot.bitcast(mybir.dt.bfloat16), W1P[e, hf], sem=sem)
                def comp_fn(slot, ld, hf=hf, e=e):
                    p.wait("te", ld[0], ld[1])
                    b = p.take_banks(1)[0]
                    sv = slot.bitcast(mybir.dt.bfloat16)
                    x2h, x2l = bpair(xn2_of(k))
                    terms = []
                    for kc in range(ND):
                        wh = sv[:, kc * 128:(kc + 1) * 128]
                        wl = sv[:, 1024 + kc * 128: 1024 + (kc + 1) * 128]
                        xh = x2h[:, kc * T:(kc + 1) * T]
                        xl = x2l[:, kc * T:(kc + 1) * T]
                        terms += [(wh, xh), (wh, xl), (wl, xh)]
                    mm_group(b, T, terms)
                    tv = p.bump("te", "te")
                    p.wait("act", "te", tv)
                    p.do("act", lambda s, b=b, hf=hf, e=e:
                         s.activation(hffn[:, hf * T:(hf + 1) * T], pb[b][:, 0:T], AF.Identity,
                                      bias=b1s[:, e * NH + hf: e * NH + hf + 1]))
                    p.bump("act", "act")
                    p.release_bank(b, "act")
                    return tv
                units.append((dma_fn, comp_fn))
            for hf in range(NH):
                def dma_fn(slot, sem, e=e, hf=hf):
                    return p.dma("sync", slot.bitcast(mybir.dt.bfloat16), W3P[e, hf], sem=sem)
                def comp_fn(slot, ld, hf=hf, e=e, hstate=hstate):
                    p.wait("te", ld[0], ld[1])
                    b = p.take_banks(1)[0]
                    sv = slot.bitcast(mybir.dt.bfloat16)
                    x2h, x2l = bpair(xn2_of(k))
                    terms = []
                    for kc in range(ND):
                        wh = sv[:, kc * 128:(kc + 1) * 128]
                        wl = sv[:, 1024 + kc * 128: 1024 + (kc + 1) * 128]
                        xh = x2h[:, kc * T:(kc + 1) * T]
                        xl = x2l[:, kc * T:(kc + 1) * T]
                        terms += [(wh, xh), (wh, xl), (wl, xh)]
                    mm_group(b, T, terms)
                    tv = p.bump("te", "te")
                    p.wait("dve", "te", tv)
                    p.wait("dve", "act", p.sems["act"].n)
                    if (hf % 4) in gsub:
                        p.wait("dve", "gps", gsub[hf % 4])
                    hs = hmul[:, (hf % 4) * T:((hf % 4) + 1) * T]
                    p.do("dve", lambda v, b=b, hf=hf, e=e, hs=hs:
                         v.scalar_tensor_tensor(hs, pb[b][:, 0:T],
                                                b3s[:, e * NH + hf: e * NH + hf + 1],
                                                hffn[:, hf * T:(hf + 1) * T], ALU.add, ALU.mult))
                    mv = p.bump("dve", "dve")
                    p.release_bank(b, "dve")
                    hh, hl = hpair(hf)
                    p.wait("act", "dve", mv)
                    p.do("act", lambda s, hs=hs: s.activation(hs, hs, AF.Silu))
                    p.guard("act")
                    p.do("act", lambda s, hh=hh, hs=hs: s.copy(hh, hs))
                    cv = p.bump("act", "act")
                    p.wait("gps", "act", cv)
                    p.do("gps", lambda g, hh=hh, hl=hl, hs=hs:
                         g.tensor_tensor(hl, hs, hh, ALU.subtract))
                    hstate[hf] = gsub[hf % 4] = p.bump("gps", "gps")
                    return tv
                units.append((dma_fn, comp_fn))
            ystate = {}
            for hf in range(NH):
                def dma_fn(slot, sem, e=e, hf=hf):
                    return p.dma("sync", slot.bitcast(mybir.dt.bfloat16), W2P[e, hf], sem=sem)
                def comp_fn(slot, ld, hf=hf, e=e, ystate=ystate, hstate=hstate):
                    hh, hl = hpair(hf)
                    if hf == 0:
                        ystate["banks"] = p.take_banks(4)
                        p.hold_banks(ystate["banks"])
                        p.wait("te", "gps", hstate[NH - 1])
                    p.wait("te", ld[0], ld[1])
                    sv = slot.bitcast(mybir.dt.bfloat16)
                    for j in range(2):
                        for n in range(2):
                            lh = hh[:, j * 128:(j + 1) * 128]
                            ll = hl[:, j * 128:(j + 1) * 128]
                            rh = sv[:, n * 512:(n + 1) * 512]
                            rl = sv[:, 1024 + n * 512: 1024 + (n + 1) * 512]
                            for ti, (lt, rr) in enumerate(((lh, rh), (lh, rl), (ll, rh))):
                                p.do("te", lambda t, b=ystate["banks"][2 * j + n], lt=lt, rr=rr,
                                     st=(hf == 0 and ti == 0), sp=(hf == NH - 1 and ti == 2):
                                     t.matmul(pb[b][:, 0:512], lt, rr, start=st, stop=sp))
                    tv = p.bump("te", "te")
                    if hf == NH - 1:
                        p.wait("dve", "te", tv)
                        for j in range(2):
                            for n in range(2):
                                b = ystate["banks"][2 * j + n]
                                sl = slice(j * D + n * 512, j * D + (n + 1) * 512)
                                sc = rww[:, rwo + j * E + e: rwo + j * E + e + 1] if e < E else 1.0
                                p.do("dve", lambda v, b=b, sl=sl, sc=sc, x_tm=x_tm:
                                     v.scalar_tensor_tensor(x_tm[:, sl], pb[b][:, 0:512], sc,
                                                            x_tm[:, sl], ALU.mult, ALU.add))
                                p.bump("dve", "dve")
                                p.release_bank(b, "dve")
                        p.unhold_banks(ystate["banks"])
                    return tv
                units.append((dma_fn, comp_fn))
        G = 4
        DEPTH = 6
        anchor = p.sems["te"].n
        for _i in range(DEPTH):
            wslot_free[_i] = max(wslot_free[_i], anchor)
        NU = len(units)
        if k + 1 < NCH:
            pend = {}
            for pt, f in stages_A(k + 1) + stages_B(k + 1):
                pend.setdefault(pt, []).append(f)
        else:
            pend = {}
        ngrp = (NU + G - 1) // G
        gsem = ["dwa" if m % 2 == 0 else "dwb" for m in range(ngrp)]
        gend = {}

        def slot_ap(i):
            return wbuf[:, (i % DEPTH) * 1024:(i % DEPTH) * 1024 + 1024]

        def issue(i):
            p.wait("sync", "te", wslot_free[i % DEPTH], force=True)
            m = i // G
            v = units[i][0](slot_ap(i), gsem[m])
            if i == NU - 1:
                dn = 0
                while (i + 1 + dn) % G != 0:
                    p.dma("sync", dumm[0:1, dn * 8:dn * 8 + 8], ONES[0:1, 0:8], sem=gsem[m])
                    v = p.sems[gsem[m]].n
                    dn += 1
                gend[m] = v
            elif i % G == G - 1:
                gend[m] = v

        for i in range(min(DEPTH, NU)):
            issue(i)
        for i in range(NU):
            m = i // G
            wslot_free[i % DEPTH] = units[i][1](slot_ap(i), (gsem[m], gend[m]))
            if i in pend:
                for f in pend.pop(i):
                    f()
            ip = i + DEPTH
            if ip < NU:
                issue(ip)

        if k == 0:
            tapdma("xout", x_tm)
        p.wait("gps", "dve", p.sems["dve"].n)
        p.dma("gps", OUT.rearrange("(s n) d -> n s d", s=2)[k * TOK:(k + 1) * TOK],
              x_tm.rearrange("p (s d) -> p s d", s=2), sem="dst")

    p.wait("gps", "dst", p.sems["dst"].n)

    import contextlib
    with contextlib.ExitStack() as st:
        for s in p.sems.values():
            s.h = st.enter_context(nc.semaphore("s_" + s.name))
        blk = st.enter_context(nc.Block())
        blk.tensor(lambda e: [f(e) for f in p.ops["te"]])
        blk.vector(lambda e: [f(e) for f in p.ops["dve"]])
        blk.scalar(lambda e: [f(e) for f in p.ops["act"]])
        blk.gpsimd(lambda e: [f(e) for f in p.ops["gps"]])
        blk.sync(lambda e: [f(e) for f in p.ops["sync"]])
    ninstr = {k: len(v) for k, v in p.ops.items()}
    return nc, ninstr


def host_inputs(inp, core, n_cores=8, rows_per_core=4096):
    seqs_per_core = 16 // n_cores
    x = np.ascontiguousarray(
        inp["L0_attn_x"].reshape(16, 2048, 1024)[core * seqs_per_core:(core + 1) * seqs_per_core]
    ).reshape(-1, D)[:rows_per_core]
    m = {
        "x": x,
        "w_a": inp["L0_attn_w_a"], "w_x": inp["L0_attn_w_x"], "w_out": inp["L0_attn_w_out"],
        "mat_v_real": inp["L0_attn_mat_v_real"], "mat_v_imag": inp["L0_attn_mat_v_imag"],
        "h0_re": inp["L0_attn_last_hidden_real"], "h0_im": inp["L0_attn_last_hidden_imag"],
        "gate_w": inp["L0_ffn_gate_w"],
        "gate_bb": np.ascontiguousarray(np.broadcast_to(inp["L0_ffn_gate_b"], (128, E))),
        "w1": inp["L0_ffn_w1"], "b1": inp["L0_ffn_b1"],
        "w3": inp["L0_ffn_w3"], "b3": inp["L0_ffn_b3"],
        "w2": inp["L0_ffn_w2"], "b2": inp["L0_ffn_b2"],
        "sw1": inp["L0_ffn_sw1"], "sb1": inp["L0_ffn_sb1"],
        "sw3": inp["L0_ffn_sw3"], "sb3": inp["L0_ffn_sb3"],
        "sw2": inp["L0_ffn_sw2"], "sb2": inp["L0_ffn_sb2"],
        "eye512": np.eye(INNER, dtype=np.float32),
        "i128": np.eye(128, dtype=np.float32),
        "iota8": np.ascontiguousarray(np.broadcast_to(np.arange(E, dtype=np.float32), (128, E))),
        "ones": np.ones((1, 128), dtype=np.float32),
    }
    return {k: np.ascontiguousarray(np.asarray(v, dtype=np.float32)) for k, v in m.items()}


from concourse.bass_utils import run_bass_kernel_spmd

_NC_CACHE = {}

def kernel(**inputs):
    n_cores = 8
    rows = 4096
    if "nc" not in _NC_CACHE:
        _NC_CACHE["nc"], _ = build(rows_per_core=rows, n_chunks=None, tap_names=(),
                                   newton_iters=4)
    nc = _NC_CACHE["nc"]
    in_maps = [host_inputs(inputs, core=c, n_cores=n_cores, rows_per_core=rows)
               for c in range(n_cores)]
    res = run_bass_kernel_spmd(nc, in_maps, core_ids=list(range(n_cores)))
    out = np.stack([res.results[c]["out"].reshape(2, 2048, D) for c in range(n_cores)])
    return out.reshape(16, 2048, D).astype(np.float32)


# version 18
# speedup vs baseline: 1.0221x; 1.0125x over previous
import numpy as np
import concourse.bass as bass
import concourse.mybir as mybir

F32 = mybir.dt.float32
AF = mybir.ActivationFunctionType
ALU = mybir.AluOpType
AX = mybir.AxisListType

D = 1024
INNER = 512
HID = 1408
E = 8
TOK = 128
T = 2 * TOK
ND = D // 128
NCI = INNER // 128
NH = HID // 128
LC = 16
NJ = TOK // LC
EPS = 1e-5
BIG = 1e9
ENGS = ("te", "dve", "act", "gps", "sync")
SEM_NAMES = ("te", "dve", "act", "gps", "dw", "dx", "dst", "dwa", "dwb", "dwc", "dwd")


class PSem:
    def __init__(self, name):
        self.name = name
        self.h = None
        self.n = 0


class Planner:
    def __init__(self):
        self.ops = {e: [] for e in ENGS}
        self.sems = {k: PSem(k) for k in SEM_NAMES}
        self.waited = {e: {} for e in ENGS}
        self.bank_free = [("te", 0)] * 8
        self.bank_rr = 0
        self.dirty = {e: False for e in ENGS}

    AUTOB = ("dve", "act", "gps")

    def do(self, eng, f, _raw=False):
        if _raw or eng not in self.AUTOB:
            self.ops[eng].append(f)
            return
        if self.dirty[eng]:
            self.dirty[eng] = False
            self.wait(eng, eng)
        s = self.sems[eng]
        s.n += 1
        def g(e, f=f, s=s):
            ins = f(e)
            ins.then_inc(s.h, 1)
            return ins
        self.ops[eng].append(g)
        self.dirty[eng] = True

    def bump(self, eng, sem_name, amt=1):
        if eng in self.AUTOB and sem_name == eng:
            return self.sems[sem_name].n
        s = self.sems[sem_name]
        s.n += amt
        last = self.ops[eng][-1]
        def g(e, last=last, s=s, amt=amt):
            ins = last(e)
            ins.then_inc(s.h, amt)
            return ins
        self.ops[eng][-1] = g
        return s.n

    def wait(self, eng, sem_name, val=None, force=False):
        s = self.sems[sem_name]
        val = s.n if val is None else val
        if val <= 0:
            return
        if not force and self.waited[eng].get(sem_name, -1) >= val:
            return
        assert val <= s.n, f"{eng} waits {sem_name}>={val} > planned {s.n}"
        self.waited[eng][sem_name] = val
        self.do(eng, lambda e, s=s, val=val: e.wait_ge(s.h, val), _raw=True)

    def dma(self, eng, out, in_, sem="dw"):
        if eng == "gps":
            self.wait(eng, sem, self.sems[sem].n)
        self.do(eng, lambda e, out=out, in_=in_: e.dma_start(out=out, in_=in_), _raw=True)
        return self.bump(eng, sem, 16)

    def take_banks(self, n):
        out = []
        tries = 0
        while len(out) < n:
            b = self.bank_rr % 8
            self.bank_rr += 1
            tries += 1
            assert tries < 64, "bank allocator livelock (too many held banks)"
            if b in getattr(self, "held", set()) or b in out:
                continue
            sname, v = self.bank_free[b]
            self.wait("te", sname, v)
            out.append(b)
        return out

    def hold_banks(self, banks):
        if not hasattr(self, "held"):
            self.held = set()
        self.held |= set(banks)

    def unhold_banks(self, banks):
        self.held -= set(banks)

    def release_bank(self, b, sem_name):
        self.bank_free[b] = (sem_name, self.sems[sem_name].n)

    def guard(self, eng):
        self.bump(eng, eng)
        self.wait(eng, eng)


def build(rows_per_core=4096, n_chunks=None, tap_names=(), newton_iters=4):
    p = Planner()
    nc = bass.Bass("TRN2", target_bir_lowering=False, debug=False)
    NCH = (rows_per_core // T) if n_chunks is None else n_chunks
    SEQ = rows_per_core // 2

    def din(name, shape):
        return nc.dram_tensor(name, shape, F32, kind="ExternalInput").ap()
    X = din("x", [rows_per_core, D])
    WA = din("w_a", [2 * INNER, D]); WXm = din("w_x", [2 * INNER, D])
    WOUT = din("w_out", [D, INNER])
    MVR = din("mat_v_real", [INNER, INNER]); MVI = din("mat_v_imag", [INNER, INNER])
    H0R = din("h0_re", [INNER]); H0I = din("h0_im", [INNER])
    GW = din("gate_w", [E, D]); GBB = din("gate_bb", [128, E])
    W1 = din("w1", [E, D, HID]); B1 = din("b1", [E, HID])
    W3 = din("w3", [E, D, HID]); B3 = din("b3", [E, HID])
    W2 = din("w2", [E, HID, D]); B2 = din("b2", [E, D])
    SW1 = din("sw1", [D, HID]); SB1 = din("sb1", [HID])
    SW3 = din("sw3", [D, HID]); SB3 = din("sb3", [HID])
    SW2 = din("sw2", [HID, D]); SB2 = din("sb2", [D])
    EYE = din("eye512", [INNER, INNER])
    I128D = din("i128", [128, 128])
    IOTA8 = din("iota8", [128, E])
    ONES = din("ones", [1, 128])
    OUT = nc.dram_tensor("out", [rows_per_core, D], F32, kind="ExternalOutput").ap()
    BF = mybir.dt.bfloat16
    WAT_HBM = nc.dram_tensor("waT_hbm", [D, 4 * INNER], BF, kind="Internal").ap()
    WXT_HBM = nc.dram_tensor("wxT_hbm", [D, 4 * INNER], BF, kind="Internal").ap()
    W1P = nc.dram_tensor("w1p_hbm", [9, NH, 128, 2 * D], BF, kind="Internal").ap()
    W3P = nc.dram_tensor("w3p_hbm", [9, NH, 128, 2 * D], BF, kind="Internal").ap()
    W2P = nc.dram_tensor("w2p_hbm", [9, NH, 128, 2 * D], BF, kind="Internal").ap()
    WOUTP = nc.dram_tensor("woutp_hbm", [INNER, 2 * D], BF, kind="Internal").ap()

    TAPCOLS = {"Zr": NCI * INNER, "Zi": NCI * INNER, "ln1": 2 * D, "afac": NCI * T,
               "aim": NCI * T, "xcre": NCI * T, "xcim": NCI * T, "ure": NCI * T,
               "uim": NCI * T, "sre": NCI * T, "sim": NCI * T, "hre": NCI * T,
               "x1": 2 * D, "ln2": 2 * D, "rww": 2 * E, "hffn0": NH * T, "xout": 2 * D}
    taps = {}
    for name in tap_names:
        taps[name] = nc.dram_tensor("tap_" + name, [128, TAPCOLS[name]], F32,
                                    kind="ExternalOutput").ap()

    def tapdma(name, src, eng_dep="dve"):
        if name not in taps:
            return
        p.wait("gps", eng_dep, p.sems[eng_dep].n)
        st = p.dma("gps", taps[name], src, sem="dst")
        p.wait("dve", "dst", st)
        p.wait("act", "dst", st)
        p.wait("te", "dst", st)

    def S(name, cols, dtype=F32):
        return nc.alloc_sbuf_tensor(name, [128, cols], dtype).ap()

    i128 = S("c_i128", 128)
    iota8 = S("c_iota8", E)
    gbb = S("c_gbb", E)
    ones = S("c_ones", 128)
    h0 = S("c_h0", 2 * NCI)
    gwT = S("r_gwT", ND * E)
    gwTp = S("r_gwTp", ND * E)
    b1s = S("r_b1", 9 * NH); b3s = S("r_b3", 9 * NH)
    b2sb = S("r_b2sb", D)
    rwf9 = S("r_rwf9", 2 * TOK)
    woutT = S("r_woutT", NCI * D)
    Zr = S("r_Zr", NCI * INNER); Zi = S("r_Zi", NCI * INNER); ZrN = S("r_ZrN", NCI * INNER)
    vT_r = S("r_vTr", NCI * INNER); vT_iN = S("r_vTiN", NCI * INNER)

    pool = S("pool", 11264)
    watb = S("watb", 4096)
    wbuf = S("wbuf", NH * D)
    x_tm = S("x_tm", 2 * D)
    xn_tm = S("xn_tm", 2 * D)
    scr = S("scr", 2 * D)
    scr2 = S("scr2", D)
    tmps = [S(f"tmp{i}", 128) for i in range(6)]
    gtmps = [S(f"gtmp{i}", 128) for i in range(4)]
    small = S("small", 64)
    carry_re = S("carry_re", 2 * 2 * NCI); carry_iN = S("carry_iN", 2 * 2 * NCI)
    scb_re = S("scb_re", 2 * NCI); scb_im = S("scb_im", 2 * NCI)
    scar_r = S("scar_r", NCI * 2 * NJ); scar_i = S("scar_i", NCI * 2 * NJ)
    gate = S("gate", 2 * 48)
    dumm = S("dumm", 64)
    rww = S("rww", 2 * 2 * E)

    def Rg(start, cols, t=pool):
        return t[:, start:start + cols]
    xn1_fm = Rg(0, ND * T)
    xr_re = Rg(0, NCI * T); xr_iN = Rg(NCI * T, NCI * T)
    a_re = scr[:, 0:NCI * T]; a_im = scr[:, NCI * T:2 * NCI * T]
    xn2_fm = Rg(2048, ND * T)
    xc_re = Rg(4096, NCI * T); xc_im = Rg(4096 + NCI * T, NCI * T)
    u_re = Rg(6144, NCI * T); u_im = Rg(6144 + NCI * T, NCI * T)
    P_re = Rg(8192, NCI * T); P_im = Rg(8192 + NCI * T, NCI * T)
    h_re = Rg(10240, NCI * T)
    hffn = woutT[:, 0:NH * T]
    hmul = S("hmul4", 4 * T)
    xn2_b = wbuf[:, 6144:8192]
    x_tm2 = wbuf[:, 8960:11008]
    v_r = Rg(0, 2048); v_iN = Rg(2048, 2048); vT_i = Rg(4096, 2048)
    Xr = Rg(6144, 2048); Xi = Rg(8192, 2048)
    Rr = x_tm[:, 0:2048]; Ri = xn_tm[:, 0:2048]; RiN = Rg(0, 2048, scr)
    eye = Rg(0, 2048, watb)
    BF16 = mybir.dt.bfloat16
    def bview(reg, half):
        return reg.bitcast(BF16)[:, half * 2048:(half + 1) * 2048]
    Xbr, Xbi = bview(Xr, 0), bview(Xr, 1)
    Zbr, Zbi = bview(Xi, 0), bview(Xi, 1)
    Rbr, Rbi = bview(Rr, 0), bview(Rr, 1)
    RbiN = bview(Ri, 0)
    vTbr, vTbi = bview(v_r, 0), bview(v_r, 1)
    vTbiN = bview(v_iN, 0)
    Xr_p = v_iN.bitcast(F32) if False else None
    wstage = watb

    pb = [nc.alloc_psum_tensor(f"pb{i}", [128, 512], F32).ap() for i in range(8)]

    def bpair(reg):
        n = reg.shape[1]
        bc = reg.bitcast(mybir.dt.bfloat16)
        return bc[:, 0:n], bc[:, n:2 * n]

    def hpair(hf):
        bc = hffn.bitcast(mybir.dt.bfloat16)
        return bc[:, hf * 2 * T: hf * 2 * T + T], bc[:, hf * 2 * T + T: hf * 2 * T + 2 * T]

    def mm3(terms):
        out = []
        for (lp, rp) in terms:
            lh, ll = lp
            rh, rl = rp
            out += [(lh, rh), (lh, rl), (ll, rh)]
        return out

    def psub(lhs_terms_p, lhs_terms_q, cols, evac):
        bP, bQ = p.take_banks(2)
        mm_group(bP, cols, lhs_terms_p)
        mm_group(bQ, cols, lhs_terms_q)
        tv = p.bump("te", "te")
        p.wait("dve", "te", tv)
        p.do("dve", lambda v, bP=bP, cols=cols: v.tensor_copy(scr2[:, 0:cols], pb[bP][:, 0:cols]))
        p.do("dve", lambda v, bQ=bQ, cols=cols: evac(v, scr2[:, 0:cols], pb[bQ]))
        p.bump("dve", "dve")
        p.release_bank(bP, "dve"); p.release_bank(bQ, "dve")

    xrh_re, xrl_re = bpair(xr_re)
    xrh_iN, xrl_iN = bpair(xr_iN)

    def prtile(pr, kc, mc):
        h, l = pr
        return (mtile(h, kc, mc), mtile(l, kc, mc))

    def xrpair(pr, kc):
        h, l = pr
        return (h[:, kc * T:(kc + 1) * T], l[:, kc * T:(kc + 1) * T])

    hreh, hrel = bpair(h_re)

    def mtile(msb, kc, mc):
        return msb[:, kc * INNER + mc * 128: kc * INNER + (mc + 1) * 128]

    def q5(ap):
        return ap.rearrange("p (c s j u) -> p c s j u", c=NCI, s=2, j=NJ)

    def q4(ap):
        return ap.rearrange("p (c s u) -> p c s u", c=NCI, s=2)

    def qcs(ap):
        return ap.rearrange("p (c s) -> p c s", c=NCI)

    p.dma("sync", i128, I128D)
    p.dma("sync", iota8, IOTA8)
    p.dma("sync", gbb, GBB)
    p.dma("sync", ones[0:1, :], ONES)
    p.dma("sync", h0[:, 0:NCI] if False else scr2[0:NCI, 0:128],
          H0R.rearrange("(c p) -> c p", p=128))
    p.dma("sync", scr2[32:32 + NCI, 0:128], H0I.rearrange("(c p) -> c p", p=128))
    p.dma("sync", b2sb[0:E, :], B2)
    p.dma("sync", b2sb[E:E + 1, :], SB2.unsqueeze(0))
    p.dma("sync", rwf9[E:E + 1, 0:TOK], ONES)
    p.dma("sync", rwf9[E:E + 1, TOK:2 * TOK], ONES)
    p.dma("sync", v_r.rearrange("p (c j) -> p c j", c=NCI),
          MVR.rearrange("(c p) j -> p c j", p=128))
    p.dma("sync", v_iN.rearrange("p (c j) -> p c j", c=NCI),
          MVI.rearrange("(c p) j -> p c j", p=128))
    p.dma("sync", scr[0:E, 0:D], GW)
    ld0 = p.dma("sync", wstage.rearrange("p (o j) -> p o j", o=ND),
                WOUT.rearrange("(o p) j -> p o j", p=128))

    p.wait("dve", "dw", ld0)
    p.do("dve", lambda v: v.memset(scb_re, 0.0))
    p.do("dve", lambda v: v.memset(scb_im, 0.0))
    p.bump("dve", "dve")

    def transp(src_ap, *evacs):
        kp, m = src_ap.shape[0], src_ap.shape[1]
        bp = src_ap.base_partition()
        b = p.take_banks(1)[0]
        p.do("te", lambda t, s=src_ap, b=b, kp=kp, m=m, bp=bp:
             t.transpose(pb[b][0:m, 0:kp], s, i128[bp:bp + kp, bp:bp + kp]))
        tv = p.bump("te", "te")
        p.wait("dve", "te", tv)
        for ev in evacs:
            p.do("dve", lambda v, b=b, ev=ev: ev(v, pb[b]))
        p.bump("dve", "dve")
        p.release_bank(b, "dve")

    p.wait("te", "dw", ld0)
    for (src, dst) in ((v_r, vT_r), (v_iN, vT_i)):
        for cj in range(NCI):
            for ci in range(NCI):
                transp(src[:, ci * INNER + cj * 128: ci * INNER + (cj + 1) * 128],
                       lambda v, ps, dst=dst, ci=ci, cj=cj:
                       v.tensor_copy(dst[:, cj * INNER + ci * 128: cj * INNER + (ci + 1) * 128],
                                     ps[:, 0:128]))
    p.do("dve", lambda v: v.tensor_scalar_mul(vT_iN, vT_i, -1.0))
    p.do("dve", lambda v: v.tensor_scalar_mul(v_iN, v_iN, -1.0))
    p.bump("dve", "dve")

    for c in range(NCI):
        for o in range(ND):
            transp(wstage[:, o * INNER + c * 128: o * INNER + (c + 1) * 128],
                   lambda v, ps, c=c, o=o:
                   v.tensor_copy(woutT[:, c * D + o * 128: c * D + (o + 1) * 128], ps[:, 0:128]))
    for c in range(ND):
        transp(scr[0:E, c * 128:(c + 1) * 128],
               lambda v, ps, c=c: v.tensor_copy(gwT[:, c * E:(c + 1) * E], ps[:, 0:E]))
    transp(scr2[0:NCI, 0:128],
           lambda v, ps: v.tensor_copy(h0[:, 0:NCI], ps[:, 0:NCI]))
    transp(scr2[32:32 + NCI, 0:128],
           lambda v, ps: v.tensor_copy(h0[:, NCI:2 * NCI], ps[:, 0:NCI]))
    p.do("dve", lambda v: v.tensor_copy(
        qcs(carry_re[:, 0:2 * NCI]), h0[:, 0:NCI].unsqueeze(2).to_broadcast([128, NCI, 2])))
    p.do("dve", lambda v: v.tensor_scalar(
        qcs(carry_iN[:, 0:2 * NCI]), h0[:, NCI:2 * NCI].unsqueeze(2).to_broadcast([128, NCI, 2]),
        -1.0, None, ALU.mult))
    p.bump("dve", "dve")
    for (BB, SBB, dst) in ((B1, SB1, b1s), (B3, SB3, b3s)):
        p.wait("sync", "dve", p.sems["dve"].n)
        p.dma("sync", scr[0:E, 0:HID], BB)
        ldb = p.dma("sync", scr[E:E + 1, 0:HID], SBB.unsqueeze(0))
        p.wait("te", "dw", ldb)
        for hc in range(NH):
            transp(scr[0:9, hc * 128:(hc + 1) * 128],
                   lambda v, ps, hc=hc, dst=dst: v.tensor_copy(
                       dst.rearrange("p (e h) -> p e h", e=9)[:, :, hc], ps[:, 0:9]))

    for (W_in, W_hbm) in ((WA, WAT_HBM), (WXm, WXT_HBM)):
        p.wait("sync", "te", p.sems["te"].n)
        p.wait("sync", "dve", p.sems["dve"].n)
        ld = p.dma("sync", wbuf[:, 0:ND * D].rearrange("p (o d) -> p o d", o=ND),
                   W_in.rearrange("(o p) d -> p o d", p=128))
        p.wait("te", "dw", ld)
        scb = scr.bitcast(mybir.dt.bfloat16)
        for dc in range(ND):
            for o in range(ND):
                b = p.take_banks(1)[0]
                p.do("te", lambda t, o=o, dc=dc, b=b:
                     t.transpose(pb[b][:, 0:128],
                                 wbuf[:, o * D + dc * 128: o * D + (dc + 1) * 128], i128))
                tv = p.bump("te", "te")
                p.wait("dve", "te", tv)
                p.do("dve", lambda v, b=b, o=o:
                     v.tensor_copy(scb[:, o * 64: o * 64 + 64], pb[b][:, 0:128:2]))
                p.do("dve", lambda v, b=b, o=o:
                     v.tensor_copy(scb[:, 512 + o * 64: 512 + o * 64 + 64], pb[b][:, 1:128:2]))
                p.guard("dve")
                p.do("dve", lambda v, b=b, o=o:
                     v.tensor_sub(scb[:, 1024 + o * 64: 1024 + o * 64 + 64], pb[b][:, 0:128:2],
                                  scb[:, o * 64: o * 64 + 64]))
                p.do("dve", lambda v, b=b, o=o:
                     v.tensor_sub(scb[:, 1536 + o * 64: 1536 + o * 64 + 64], pb[b][:, 1:128:2],
                                  scb[:, 512 + o * 64: 512 + o * 64 + 64]))
                p.bump("dve", "dve")
                p.release_bank(b, "dve")
            p.wait("gps", "dve", p.sems["dve"].n)
            st = p.dma("gps", W_hbm[dc * 128:(dc + 1) * 128, :], scb[:, 0:2048], sem="dst")
            p.wait("dve", "dst", st)
    watw_done = p.sems["dst"].n

    def repack(Wsrcs, Wdst, colblocks):
        p.wait("sync", "dw", p.sems["dw"].n)
        stg = watb[:, 2048:4096].bitcast(mybir.dt.bfloat16)
        cdim = ND if colblocks else NH
        for e in range(9):
            Ws = Wsrcs[0][e] if e < E else Wsrcs[1]
            p.wait("sync", "dst", p.sems["dst"].n)
            p.wait("sync", "gps", p.sems["gps"].n)
            ldw = p.dma("sync", wbuf.rearrange("p (c h) -> p c h", c=cdim),
                        Ws.rearrange("(c p) h -> p c h", p=128))
            p.wait("gps", "dw", ldw)
            for hf in range(NH):
                if colblocks:
                    blk = wbuf.rearrange("p (c h) -> p c h", c=ND)[:, :, hf * 128:(hf + 1) * 128]
                    sl = stg[:, 0:2048].rearrange("p (x c q) -> p x c q", x=2, c=ND)
                    hi, lo = sl[:, 0], sl[:, 1]
                else:
                    blk = wbuf[:, hf * D:(hf + 1) * D]
                    hi, lo = stg[:, 0:1024], stg[:, 1024:2048]
                p.wait("gps", "dst", p.sems["dst"].n)
                p.do("gps", lambda g, blk=blk, hi=hi: g.tensor_copy(hi, blk))
                p.guard("gps")
                p.do("gps", lambda g, blk=blk, hi=hi, lo=lo: g.tensor_tensor(lo, blk, hi, ALU.subtract))
                p.bump("gps", "gps")
                p.wait("gps", "gps", p.sems["gps"].n)
                p.dma("gps", Wdst[e, hf], stg[:, 0:2048] if not colblocks else None, sem="dst") \
                    if False else None
                p.dma("gps", Wdst[e, hf], stg[:, 0:2048], sem="dst")

    p.wait("sync", "dst", watw_done)
    ld_eye = p.dma("sync", eye.rearrange("p (c j) -> p c j", c=NCI),
                   EYE.rearrange("(c p) j -> p c j", p=128))
    repack((W1, SW1), W1P, True)
    repack((W3, SW3), W3P, True)
    repack((W2, SW2), W2P, False)
    repack_done = p.sems["dst"].n
    p.wait("dve", "dw", ld_eye)
    p.do("dve", lambda v: v.tensor_mul(scr[:, 0:2048], v_r, v_r))
    p.do("dve", lambda v: v.tensor_mul(scr2, v_iN[:, 0:D], v_iN[:, 0:D]))
    p.do("dve", lambda v: v.tensor_add(scr[:, 0:D], scr[:, 0:D], scr2))
    p.do("dve", lambda v: v.tensor_mul(scr2, v_iN[:, D:2 * D], v_iN[:, D:2 * D]))
    p.do("dve", lambda v: v.tensor_add(scr[:, D:2 * D], scr[:, D:2 * D], scr2))
    p.do("dve", lambda v: v.reduce_sum(small[:, 0:1], scr[:, 0:2048], axis=AX.X))
    p.do("dve", lambda v: v.memset(scr2[:, 0:1], 1.0))
    rsum_done = p.bump("dve", "dve")
    b = p.take_banks(1)[0]
    p.wait("te", "dve", rsum_done)
    p.do("te", lambda t, b=b: t.matmul(pb[b][0:1, 0:1], small[:, 0:1], scr2[:, 0:1],
                                       start=True, stop=True))
    tv = p.bump("te", "te")
    p.wait("dve", "te", tv)
    p.do("dve", lambda v, b=b: v.reciprocal(small[0:1, 2:3], pb[b][0:1, 0:1]))
    p.bump("dve", "dve")
    p.release_bank(b, "dve")
    b = p.take_banks(1)[0]
    p.wait("te", "dve")
    p.do("te", lambda t, b=b: t.matmul(pb[b][:, 0:1], ones[0:1, 0:128], small[0:1, 2:3],
                                       start=True, stop=True))
    tv = p.bump("te", "te")
    p.wait("dve", "te", tv)
    p.do("dve", lambda v, b=b: v.tensor_copy(small[:, 4:5], pb[b][:, 0:1]))
    p.bump("dve", "dve")
    p.release_bank(b, "dve")
    p.guard("dve")
    p.do("dve", lambda v: v.tensor_scalar(Xbr, vT_r, small[:, 4:5], None, ALU.mult))
    p.do("dve", lambda v: v.tensor_scalar(Xbi, vT_iN, small[:, 4:5], None, ALU.mult))
    p.do("dve", lambda v: v.tensor_scalar(Zbr, v_r, small[:, 4:5], None, ALU.mult))
    p.do("dve", lambda v: v.tensor_scalar(Zbi, v_iN, small[:, 4:5], None, ALU.mult))
    p.do("dve", lambda v: v.tensor_copy(vTbr, vT_r))
    p.do("dve", lambda v: v.tensor_copy(vTbi, vT_i))
    p.do("dve", lambda v: v.tensor_copy(vTbiN, vT_iN))
    nv = p.bump("dve", "dve")
    p.wait("te", "dve", nv)

    def mm_group(bank, out_cols, terms):
        n = len(terms)
        for i, (lt, rh) in enumerate(terms):
            p.do("te", lambda t, b=bank, lt=lt, rh=rh, i=i, n=n, oc=out_cols:
                 t.matmul(pb[b][:, 0:oc], lt, rh, start=(i == 0), stop=(i == n - 1)))

    def m512_terms(pairs, mc):
        out = []
        for (lsb, rsb) in pairs:
            for kc in range(NCI):
                out.append((mtile(lsb, kc, mc), rsb[:, kc * INNER:(kc + 1) * INNER]))
        return out

    def newton_iter(xr, xi, zr, zi, rr, ri, riN, vtr, vti, vtiN, cast):
        banks = p.take_banks(8)
        for mc in range(NCI):
            mm_group(banks[mc], INNER, m512_terms(((vtr, xr), (vtiN, xi)), mc))
            mm_group(banks[4 + mc], INNER, m512_terms(((vti, xr), (vtr, xi)), mc))
        t1v = p.bump("te", "te")
        p.wait("dve", "te", t1v)
        for mc in range(NCI):
            sl = slice(mc * INNER, (mc + 1) * INNER)
            p.do("dve", lambda v, b=banks[mc], sl=sl, rr=rr: v.tensor_sub(rr[:, sl], eye[:, sl], pb[b][:, 0:INNER]))
            p.do("dve", lambda v, b=banks[4 + mc], sl=sl, ri=ri: v.tensor_scalar_mul(ri[:, sl], pb[b][:, 0:INNER], -1.0))
            p.do("dve", lambda v, b=banks[4 + mc], sl=sl, riN=riN: v.tensor_copy(riN[:, sl], pb[b][:, 0:INNER]))
            p.bump("dve", "dve")
            p.release_bank(banks[mc], "dve"); p.release_bank(banks[4 + mc], "dve")
        p.wait("te", "dve")
        banks = p.take_banks(8)
        for mc in range(NCI):
            mm_group(banks[mc], INNER, m512_terms(((zr, rr), (zi, riN)), mc))
            mm_group(banks[4 + mc], INNER, m512_terms(((zr, ri), (zi, rr)), mc))
        dv = p.bump("te", "te")
        p.wait("dve", "te", dv)
        for mc in range(NCI):
            sl = slice(mc * INNER, (mc + 1) * INNER)
            p.do("dve", lambda v, b=banks[mc], sl=sl, xr=xr: v.tensor_add(xr[:, sl], xr[:, sl], pb[b][:, 0:INNER]))
            p.do("dve", lambda v, b=banks[4 + mc], sl=sl, xi=xi: v.tensor_add(xi[:, sl], xi[:, sl], pb[b][:, 0:INNER]))
            p.bump("dve", "dve")
            p.release_bank(banks[mc], "dve"); p.release_bank(banks[4 + mc], "dve")
        p.wait("te", "dve")
        banks = p.take_banks(8)
        for mc in range(NCI):
            mm_group(banks[mc], INNER, m512_terms(((rr, zr), (riN, zi)), mc))
            mm_group(banks[4 + mc], INNER, m512_terms(((ri, zr), (rr, zi)), mc))
        ev = p.bump("te", "te")
        p.wait("dve", "te", ev)
        for mc in range(NCI):
            sl = slice(mc * INNER, (mc + 1) * INNER)
            p.do("dve", lambda v, b=banks[mc], sl=sl, zr=zr: v.tensor_add(zr[:, sl], zr[:, sl], pb[b][:, 0:INNER]))
            p.do("dve", lambda v, b=banks[4 + mc], sl=sl, zi=zi: v.tensor_add(zi[:, sl], zi[:, sl], pb[b][:, 0:INNER]))
            p.bump("dve", "dve")
            p.release_bank(banks[mc], "dve"); p.release_bank(banks[4 + mc], "dve")
        p.wait("te", "dve")

    for it in range(34):
        newton_iter(Xbr, Xbi, Zbr, Zbi, Rbr, Rbi, RbiN, vTbr, vTbi, vTbiN, True)
    p.wait("dve", "te", p.sems["te"].n)
    p.do("dve", lambda v: v.tensor_copy(Zr, Zbr))
    p.do("dve", lambda v: v.tensor_copy(Zi, Zbi))
    p.do("dve", lambda v: v.tensor_copy(scr, Xbr))
    p.bump("dve", "dve")
    p.do("dve", lambda v: v.tensor_copy(Xi, Xbi))
    p.do("dve", lambda v: v.tensor_copy(Xr, scr))
    p.bump("dve", "dve")
    p.wait("te", "dve", p.sems["dve"].n)
    for it in range(newton_iters):
        newton_iter(Xr, Xi, Zr, Zi, Rr, Ri, RiN, vT_r, vT_i, vT_iN, False)

    tapdma("Zr", Zr); tapdma("Zi", Zi)
    zr_h, zr_l = bpair(ZrN)
    zi_h, zi_l = bpair(Zr)
    vtr_h, vtr_l = bpair(vT_iN)
    vti_h, vti_l = bpair(vT_r)
    gwh, gwl = bpair(gwTp)
    p.do("dve", lambda v: v.tensor_copy(zr_h, Zr))
    p.guard("dve")
    p.do("dve", lambda v: v.tensor_sub(zr_l, Zr, zr_h))
    p.do("dve", lambda v: v.tensor_copy(zi_h, Zi))
    p.guard("dve")
    p.do("dve", lambda v: v.tensor_sub(zi_l, Zi, zi_h))
    p.do("dve", lambda v: v.tensor_copy(vtr_h, vT_r))
    p.guard("dve")
    p.do("dve", lambda v: v.tensor_sub(vtr_l, vT_r, vtr_h))
    p.do("dve", lambda v: v.tensor_copy(vti_h, vT_i))
    p.guard("dve")
    p.do("dve", lambda v: v.tensor_sub(vti_l, vT_i, vti_h))
    p.do("dve", lambda v: v.tensor_copy(gwh, gwT))
    p.guard("dve")
    p.do("dve", lambda v: v.tensor_sub(gwl, gwT, gwh))
    p.bump("dve", "dve")
    scrb = scr.bitcast(mybir.dt.bfloat16)
    for c in range(NCI):
        p.wait("dve", "dst", p.sems["dst"].n)
        p.do("dve", lambda v, c=c: v.tensor_copy(scrb[:, 0:1024], woutT[:, c * D:(c + 1) * D]))
        p.guard("dve")
        p.do("dve", lambda v, c=c: v.tensor_sub(scrb[:, 1024:2048], woutT[:, c * D:(c + 1) * D],
                                                scrb[:, 0:1024]))
        p.bump("dve", "dve")
        p.wait("gps", "dve", p.sems["dve"].n)
        p.dma("gps", WOUTP[c * 128:(c + 1) * 128, :], scrb[:, 0:2048], sem="dst")
    p.wait("dve", "dst", p.sems["dst"].n)

    def ln(dst_fm, tapname, k, x_tm):
        p.wait("dve", "te", p.sems["te"].n)
        for j in range(2):
            p.do("dve", lambda v, j=j: v.reduce_sum(small[:, 8 + j:9 + j],
                                                    x_tm[:, j * D:(j + 1) * D], axis=AX.X))
        sv = p.bump("dve", "dve")
        p.wait("act", "dve", sv)
        for j in range(2):
            p.do("act", lambda s, j=j: s.activation(scr2, x_tm[:, j * D:(j + 1) * D], AF.Square,
                                                    accum_out=small[:, 10 + j:11 + j]))
        qv = p.bump("act", "act")
        p.wait("dve", "act", qv)
        for j in range(2):
            p.do("dve", lambda v, j=j: v.tensor_scalar_mul(small[:, 12 + j:13 + j],
                                                           small[:, 8 + j:9 + j], 1.0 / D))
            p.do("dve", lambda v, j=j: v.tensor_scalar_mul(small[:, 14 + j:15 + j],
                                                           small[:, 10 + j:11 + j], 1.0 / D))
        p.guard("dve")
        for j in range(2):
            p.do("dve", lambda v, j=j: v.tensor_scalar(small[:, 16 + j:17 + j],
                                                       small[:, 12 + j:13 + j],
                                                       small[:, 12 + j:13 + j], -1.0,
                                                       ALU.mult, ALU.mult))
        p.guard("dve")
        for j in range(2):
            p.do("dve", lambda v, j=j: v.tensor_add(small[:, 14 + j:15 + j],
                                                    small[:, 14 + j:15 + j], small[:, 16 + j:17 + j]))
            p.do("dve", lambda v, j=j: v.tensor_scalar_add(small[:, 14 + j:15 + j],
                                                           small[:, 14 + j:15 + j], EPS))
        p.guard("dve")
        for j in range(2):
            p.do("dve", lambda v, j=j: v.reciprocal(small[:, 18 + j:19 + j], small[:, 14 + j:15 + j]))
        rv = p.bump("dve", "dve")
        p.wait("act", "dve", rv)
        for j in range(2):
            p.do("act", lambda s, j=j: s.activation(small[:, 20 + j:21 + j],
                                                    small[:, 18 + j:19 + j], AF.Sqrt))
        rstdv = p.bump("act", "act")
        p.wait("dve", "act", rstdv)
        for j in range(2):
            p.do("dve", lambda v, j=j:
                 v.tensor_scalar(xn_tm[:, j * D:(j + 1) * D], x_tm[:, j * D:(j + 1) * D],
                                 small[:, 12 + j:13 + j], small[:, 20 + j:21 + j],
                                 ALU.subtract, ALU.mult))
        xnv = p.bump("dve", "dve")
        if k == 0:
            tapdma(tapname, xn_tm)
        return xnv

    def ln_tp(dst_fm, xnv):
        p.wait("te", "dve", xnv)
        dh, dl = bpair(dst_fm)
        for c in range(ND):
            for j in range(2):
                sl = slice(c * T + j * 128, c * T + (j + 1) * 128)
                transp(xn_tm[:, j * D + c * 128: j * D + (c + 1) * 128],
                       lambda v, ps, sl=sl: v.tensor_copy(dh[:, sl], ps[:, 0:128]),
                       lambda v, ps, sl=sl: v.tensor_tensor(dl[:, sl], ps[:, 0:128],
                                                            dh[:, sl], ALU.subtract))
        return p.sems["dve"].n

    def proj_and_factor(W_hbm, dst_re, dst_im, kind, key, xv, nxt):
        p.wait("te", "dve", xv)
        banks = p.take_banks(8)
        wbv = watb.bitcast(mybir.dt.bfloat16)
        x1h, x1l = bpair(xn1_fm)
        for tn in range(4):
            p.wait("te", *ws[(key, tn)])
            for m in range(8):
                for kcl in range(2):
                    kca = 2 * tn + kcl
                    base = (tn % 2) * 4096 + kcl * 2048
                    wh = wbv[:, base + m * 128: base + (m + 1) * 128]
                    wl = wbv[:, base + 1024 + m * 128: base + 1024 + (m + 1) * 128]
                    xh = x1h[:, kca * T:(kca + 1) * T]
                    xl = x1l[:, kca * T:(kca + 1) * T]
                    for ti, (lt, rh) in enumerate(((wh, xh), (wh, xl), (wl, xh))):
                        p.do("te", lambda t, b=banks[m], lt=lt, rh=rh, st=(kca == 0 and ti == 0),
                             sp=(kca == 7 and ti == 2):
                             t.matmul(pb[b][:, 0:T], lt, rh, start=st, stop=sp))
            pv = p.bump("te", "te")
            if nxt[tn] is not None:
                k2, t2, W2 = nxt[tn]
                wissue(k2, t2, W2, pv)
        p.wait("dve", "te", pv)
        fs = xn_tm
        for cc in range(NCI):
            p.do("dve", lambda v, b=banks[cc], cc=cc:
                 v.tensor_copy(fs[:, cc * 512: cc * 512 + T], pb[b][:, 0:T]))
            p.bump("dve", "dve")
            p.release_bank(banks[cc], "dve")
            p.do("dve", lambda v, b=banks[4 + cc], cc=cc:
                 v.tensor_copy(fs[:, cc * 512 + T:(cc + 1) * 512], pb[b][:, 0:T]))
            p.bump("dve", "dve")
            p.release_bank(banks[4 + cc], "dve")
        p.wait("act", "dve", p.sems["dve"].n)
        m2f = pool[:, 8192:10240]
        p.do("act", lambda s: s.activation(m2f, fs, AF.Square))
        sqv = p.bump("act", "act")
        p.wait("dve", "act", sqv)
        m2 = m2f.rearrange("p (c a u) -> p c a u", c=NCI, a=2)
        s2c = scr2.rearrange("p (c u) -> p c u", c=NCI)
        p.do("dve", lambda v: v.tensor_add(s2c, m2[:, :, 0, :], m2[:, :, 1, :]))
        m2v = p.bump("dve", "dve")
        p.wait("act", "dve", m2v)
        if kind == "a":
            p.do("act", lambda s: s.activation(m2f[:, 0:D], scr2, AF.Sqrt))
            rtv = p.bump("act", "act")
            p.wait("dve", "act", rtv)
            p.do("dve", lambda v: v.tensor_scalar_add(scr2, scr2, 1.0))
            p.guard("dve")
            p.do("dve", lambda v: v.reciprocal(scr2, scr2))
            p.guard("dve")
            p.do("dve", lambda v: v.tensor_mul(scr2, scr2, m2f[:, 0:D]))
            p.guard("dve")
        else:
            p.do("act", lambda s: s.activation(scr2, scr2, AF.Sqrt))
            rtv = p.bump("act", "act")
            p.wait("dve", "act", rtv)
        fre = fs.rearrange("p (c a u) -> p c a u", c=NCI, a=2)
        p.do("dve", lambda v: v.tensor_mul(dst_re.rearrange("p (c u) -> p c u", c=NCI),
                                           fre[:, :, 0, :], s2c))
        p.do("dve", lambda v: v.tensor_mul(dst_im.rearrange("p (c u) -> p c u", c=NCI),
                                           fre[:, :, 1, :], s2c))
        p.bump("dve", "dve")

    def scan_inner(ta, tb, st):
        qa_r, qa_i = q5(a_re), q5(a_im)
        qu_r, qu_i = q5(u_re), q5(u_im)
        qP_r, qP_i = q5(P_re), q5(P_im)
        def t128(i):
            return tmps[i][:, 0:NCI * 2 * NJ].rearrange("p (c s j) -> p c s j", c=NCI, s=2)
        def g128(i):
            return gtmps[i][:, 0:NCI * 2 * NJ].rearrange("p (c s j) -> p c s j", c=NCI, s=2)
        t1_, t2_, t3_, t4_, t5_, t6_ = (t128(i) for i in range(6))
        g1, g2, g3, g4 = (g128(i) for i in range(4))
        t1, t2, t3, t4, t5, t6 = t1_, t2_, t3_, t4_, t5_, t6_
        p.wait("gps", "dve", st["uv"])
        for t in range(ta, tb):
            ar, ai = qa_r[:, :, :, :, t], qa_i[:, :, :, :, t]
            ur, ui = qu_r[:, :, :, :, t], qu_i[:, :, :, :, t]
            pr, pi = qP_r[:, :, :, :, t], qP_i[:, :, :, :, t]
            if t == 0:
                p.do("dve", lambda v, ar=ar, ur=ur: v.tensor_mul(t1, ar, ur))
                p.do("dve", lambda v, ai=ai, ui=ui: v.tensor_mul(t2, ai, ui))
                p.do("dve", lambda v, ar=ar, ui=ui: v.tensor_mul(t3, ar, ui))
                p.do("dve", lambda v, ai=ai, ur=ur: v.tensor_mul(t4, ai, ur))
                p.do("dve", lambda v, ur=ur: v.tensor_sub(ur, t1, t2))
                p.do("dve", lambda v, ui=ui: v.tensor_add(ui, t3, t4))
                p.do("gps", lambda g, pr=pr, ar=ar: g.tensor_copy(pr, ar))
                p.do("gps", lambda g, pi=pi, ai=ai: g.tensor_copy(pi, ai))
            else:
                urp, uip = qu_r[:, :, :, :, t - 1], qu_i[:, :, :, :, t - 1]
                prp, pip = qP_r[:, :, :, :, t - 1], qP_i[:, :, :, :, t - 1]
                p.do("dve", lambda v, urp=urp, ur=ur: v.tensor_add(t5, urp, ur))
                p.do("dve", lambda v, uip=uip, ui=ui: v.tensor_add(t6, uip, ui))
                p.do("dve", lambda v, ar=ar: v.tensor_mul(t1, ar, t5))
                p.do("dve", lambda v, ai=ai: v.tensor_mul(t2, ai, t6))
                p.do("dve", lambda v, ur=ur: v.tensor_sub(ur, t1, t2))
                p.do("dve", lambda v, ar=ar: v.tensor_mul(t3, ar, t6))
                p.do("dve", lambda v, ai=ai: v.tensor_mul(t4, ai, t5))
                p.do("dve", lambda v, ui=ui: v.tensor_add(ui, t3, t4))
                p.do("gps", lambda g, ar=ar, prp=prp: g.tensor_mul(g1, ar, prp))
                p.do("gps", lambda g, ai=ai, pip=pip: g.tensor_mul(g2, ai, pip))
                p.do("gps", lambda g, pr=pr: g.tensor_sub(pr, g1, g2))
                p.do("gps", lambda g, ar=ar, pip=pip: g.tensor_mul(g3, ar, pip))
                p.do("gps", lambda g, ai=ai, prp=prp: g.tensor_mul(g4, ai, prp))
                p.do("gps", lambda g, pi=pi: g.tensor_add(pi, g3, g4))
        p.bump("dve", "dve")
        st["ptv"] = p.bump("gps", "gps")

    def scan_chain(ja, jb, st):
        qu_r, qu_i = q5(u_re), q5(u_im)
        qP_r, qP_i = q5(P_re), q5(P_im)
        qs_r = scar_r.rearrange("p (c s j) -> p c s j", c=NCI, s=2)
        qs_i = scar_i.rearrange("p (c s j) -> p c s j", c=NCI, s=2)
        def t8(i):
            return tmps[i][:, 0:8].rearrange("p (c s) -> p c s", c=NCI)
        t1b, t2b, t3b, t4b, t5b = (t8(i) for i in range(5))
        p.wait("dve", "gps", st["ptv"])
        if ja == 1:
            p.do("dve", lambda v: v.tensor_copy(qs_r[:, :, :, 0], qcs(scb_re)))
            p.do("dve", lambda v: v.tensor_copy(qs_i[:, :, :, 0], qcs(scb_im)))
        for j in range(ja, jb):
            Ar = qP_r[:, :, :, j - 1, LC - 1]; Ai = qP_i[:, :, :, j - 1, LC - 1]
            Br = qu_r[:, :, :, j - 1, LC - 1]; Bi = qu_i[:, :, :, j - 1, LC - 1]
            Srp = qs_r[:, :, :, j - 1]; Sip = qs_i[:, :, :, j - 1]
            Sr = qs_r[:, :, :, j] if j < NJ else qcs(scb_re)
            Si = qs_i[:, :, :, j] if j < NJ else qcs(scb_im)
            p.do("dve", lambda v, Ar=Ar, Srp=Srp: v.tensor_mul(t1b, Ar, Srp))
            p.do("dve", lambda v, Ai=Ai, Sip=Sip: v.tensor_mul(t2b, Ai, Sip))
            p.do("dve", lambda v, Ar=Ar, Sip=Sip: v.tensor_mul(t3b, Ar, Sip))
            p.do("dve", lambda v, Ai=Ai, Srp=Srp: v.tensor_mul(t4b, Ai, Srp))
            p.do("dve", lambda v: v.tensor_sub(t5b, t1b, t2b))
            p.do("dve", lambda v, Sr=Sr, Br=Br: v.tensor_add(Sr, t5b, Br))
            p.do("dve", lambda v: v.tensor_add(t5b, t3b, t4b))
            p.do("dve", lambda v, Si=Si, Bi=Bi: v.tensor_add(Si, t5b, Bi))
        p.bump("dve", "dve")

    def scan_app(st):
        qP_r, qP_i = q5(P_re), q5(P_im)
        p.wait("dve", "gps", st["ptv"])
        qs_r = scar_r.rearrange("p (c s j) -> p c s j", c=NCI, s=2)
        qs_i = scar_i.rearrange("p (c s j) -> p c s j", c=NCI, s=2)
        Srb = qs_r.unsqueeze(4).to_broadcast([128, NCI, 2, NJ, LC])
        Sib = qs_i.unsqueeze(4).to_broadcast([128, NCI, 2, NJ, LC])
        sA = scr[:, 0:1024].rearrange("p (c s j u) -> p c s j u", c=NCI, s=2, j=NJ)
        sB = scr[:, 1024:2048].rearrange("p (c s j u) -> p c s j u", c=NCI, s=2, j=NJ)
        p.do("dve", lambda v: v.tensor_mul(sA, qP_r, Srb))
        p.do("dve", lambda v: v.tensor_mul(sB, qP_i, Sib))
        p.do("dve", lambda v: v.tensor_sub(scr[:, 0:1024], scr[:, 0:1024], scr[:, 1024:2048]))
        p.do("dve", lambda v: v.tensor_add(u_re, u_re, scr[:, 0:1024]))
        p.do("dve", lambda v: v.tensor_mul(sA, qP_r, Sib))
        p.do("dve", lambda v: v.tensor_mul(sB, qP_i, Srb))
        p.do("dve", lambda v: v.tensor_add(scr[:, 0:1024], scr[:, 0:1024], scr[:, 1024:2048]))
        p.do("dve", lambda v: v.tensor_add(u_im, u_im, scr[:, 0:1024]))
        p.bump("dve", "dve")

    def gate_topk(k, xn2q, x2v):
        rwo = (k % 2) * 2 * E
        p.wait("te", "dve", x2v)
        p.wait("dve", "te", p.sems["te"].n)
        x2h, x2l = bpair(xn2q)
        for j in range(2):
            b = p.take_banks(1)[0]
            terms = []
            for kc in range(ND):
                lh = x2h[:, kc * T + j * 128: kc * T + (j + 1) * 128]
                ll = x2l[:, kc * T + j * 128: kc * T + (j + 1) * 128]
                rh = gwh[:, kc * E:(kc + 1) * E]
                rl = gwl[:, kc * E:(kc + 1) * E]
                terms += [(lh, rh), (lh, rl), (ll, rh)]
            mm_group(b, E, terms)
            tv = p.bump("te", "te")
            p.wait("dve", "te", tv)
            p.do("dve", lambda v, b=b, j=j: v.tensor_copy(gate[:, j * 48: j * 48 + E], pb[b][:, 0:E]))
            p.bump("dve", "dve")
            p.release_bank(b, "dve")
        def g(j, off, n=E):
            return gate[:, j * 48 + off: j * 48 + off + n]
        for j in range(2):
            p.do("dve", lambda v, j=j: v.reduce_max(small[:, 24 + j:25 + j], g(j, 0), axis=AX.X))
        p.guard("dve")
        for j in range(2):
            p.do("dve", lambda v, j=j: v.tensor_scalar_mul(small[:, 26 + j:27 + j],
                                                           small[:, 24 + j:25 + j], -1.0))
        ngv = p.bump("dve", "dve")
        p.wait("act", "dve", ngv)
        for j in range(2):
            p.do("act", lambda s, j=j: s.activation(g(j, 8), g(j, 0), AF.Exp,
                                                    bias=small[:, 26 + j:27 + j]))
        ev = p.bump("act", "act")
        p.wait("dve", "act", ev)
        for j in range(2):
            p.do("dve", lambda v, j=j: v.reduce_sum(small[:, 28 + j:29 + j], g(j, 8), axis=AX.X))
        p.guard("dve")
        for j in range(2):
            p.do("dve", lambda v, j=j: v.reciprocal(small[:, 30 + j:31 + j], small[:, 28 + j:29 + j]))
        p.guard("dve")
        for j in range(2):
            p.do("dve", lambda v, j=j: v.tensor_scalar(g(j, 8), g(j, 8), small[:, 30 + j:31 + j],
                                                       None, ALU.mult))
            p.do("dve", lambda v, j=j: v.tensor_add(g(j, 16), g(j, 8), gbb))
        p.guard("dve")
        for pass_i in range(2):
            for j in range(2):
                p.do("dve", lambda v, j=j: v.reduce_max(small[:, 32 + j:33 + j], g(j, 16), axis=AX.X))
            p.guard("dve")
            for j in range(2):
                p.do("dve", lambda v, j=j: v.tensor_scalar(g(j, 24), g(j, 16),
                                                           small[:, 32 + j:33 + j], None, ALU.is_ge))
            p.guard("dve")
            for j in range(2):
                p.do("dve", lambda v, j=j: v.tensor_scalar(g(j, 32), g(j, 24), -BIG, BIG,
                                                           ALU.mult, ALU.add))
                p.do("dve", lambda v, j=j: v.tensor_add(g(j, 32), g(j, 32), iota8))
            p.guard("dve")
            for j in range(2):
                p.do("dve", lambda v, j=j: v.tensor_reduce(small[:, 34 + j:35 + j], g(j, 32),
                                                           AX.X, ALU.min))
            p.guard("dve")
            for j in range(2):
                p.do("dve", lambda v, j=j: v.tensor_scalar(g(j, 24), iota8,
                                                           small[:, 34 + j:35 + j], None, ALU.is_equal))
            p.guard("dve")
            for j in range(2):
                p.do("dve", lambda v, j=j: v.tensor_mul(g(j, 40), g(j, 8), g(j, 24)))
            p.guard("dve")
            for j in range(2):
                p.do("dve", lambda v, j=j, pi=pass_i:
                     v.reduce_sum(small[:, 36 + 2 * j + pi:37 + 2 * j + pi], g(j, 40), axis=AX.X))
            p.guard("dve")
            for j in range(2):
                if pass_i == 0:
                    p.do("dve", lambda v, j=j, rwo=rwo: v.tensor_scalar(
                        rww[:, rwo + j * E: rwo + (j + 1) * E], g(j, 24),
                        small[:, 36 + 2 * j:37 + 2 * j], None, ALU.mult))
                    p.do("dve", lambda v, j=j: v.tensor_scalar(g(j, 40), g(j, 24), BIG, None, ALU.mult))
                    p.do("dve", lambda v, j=j: v.tensor_sub(g(j, 16), g(j, 16), g(j, 40)))
                else:
                    p.do("dve", lambda v, j=j: v.tensor_scalar(g(j, 40), g(j, 24),
                                                               small[:, 37 + 2 * j:38 + 2 * j],
                                                               None, ALU.mult))
                    p.do("dve", lambda v, j=j, rwo=rwo: v.tensor_add(
                        rww[:, rwo + j * E: rwo + (j + 1) * E],
                        rww[:, rwo + j * E: rwo + (j + 1) * E], g(j, 40)))
            p.guard("dve")

    wslot_free = [p.sems["te"].n] * 6
    ffn_base = [0]

    x_tm_a = x_tm
    def xt_of(k):
        return x_tm_a if k % 2 == 0 else x_tm2

    def xn2_of(k):
        return xn2_fm if k % 2 == 0 else xn2_b

    ws = {}
    wst = {"wout_cons": 0}
    gsub = {}
    wbv16 = watb.bitcast(mybir.dt.bfloat16)

    def wissue(key, t, W_hbm, anchor, k0=False):
        if k0:
            p.wait("act", "dst", p.sems["dst"].n, force=True)
        else:
            p.wait("act", "te", max(anchor, 1), force=True)
        dst = wbv16[:, (t % 2) * 4096:(t % 2) * 4096 + 4096].rearrange("p (c f) -> p c f", c=2)
        src = W_hbm.rearrange("(c p) f -> p c f", p=128)[:, 2 * t:2 * t + 2, :]
        sem = "dwc" if t % 2 == 0 else "dwd"
        ws[(key, t)] = (sem, p.dma("act", dst, src, sem=sem))

    def stages_A(k):
        x_tm = xt_of(k)
        asx = {}
        def s_wd():
            wissue("wa", 0, WAT_HBM, wst["wout_cons"], k0=(k == 0))
            wissue("wa", 1, WAT_HBM, wst["wout_cons"], k0=(k == 0))
        def s_xln():
            p.wait("gps", "dst", p.sems["dst"].n)
            p.wait("gps", "dve", p.sems["dve"].n)
            p.wait("gps", "te", p.sems["te"].n) if k == 0 else None
            ldx = p.dma("gps", x_tm.rearrange("p (s d) -> p s d", s=2),
                        X.rearrange("(s n) d -> n s d", s=2)[k * TOK:(k + 1) * TOK], sem="dx")
            p.wait("dve", "dx", ldx)
            p.wait("act", "dx", ldx)
            asx["xnv"] = ln(xn1_fm, "ln1", k, x_tm)
        def s_xtp():
            asx["ln1tp"] = ln_tp(xn1_fm, asx["xnv"])
        def s_pa():
            proj_and_factor(WAT_HBM, a_re, a_im, "a", "wa", asx["ln1tp"],
                            [("wa", 2, WAT_HBM), ("wa", 3, WAT_HBM),
                             ("wx", 0, WXT_HBM), ("wx", 1, WXT_HBM)])
            if k == 0:
                tapdma("afac", a_re); tapdma("aim", a_im)
        def s_px():
            proj_and_factor(WXT_HBM, xc_re, xc_im, "x", "wx", asx["ln1tp"],
                            [("wx", 2, WXT_HBM), ("wx", 3, WXT_HBM),
                             ("wo", 0, WOUTP), ("wo", 1, WOUTP)])
            if k == 0:
                tapdma("xcre", xc_re); tapdma("xcim", xc_im)
        def s_xr():
            cin = (k % 2) * 2 * NCI
            cout = ((k + 1) % 2) * 2 * NCI
            q4b = lambda ap: ap.rearrange("p (c s u) -> p c s u", c=NCI, s=2)
            p.do("dve", lambda v: v.tensor_copy(q4b(xrh_re)[:, :, :, 1:TOK], q4(xc_re)[:, :, :, 0:TOK - 1]))
            p.do("dve", lambda v: v.tensor_scalar(q4b(xrh_iN)[:, :, :, 1:TOK],
                                                  q4(xc_im)[:, :, :, 0:TOK - 1], -1.0, None, ALU.mult))
            p.do("dve", lambda v, cin=cin: v.tensor_copy(q4b(xrh_re)[:, :, :, 0],
                                                         qcs(carry_re[:, cin:cin + 2 * NCI])))
            p.do("dve", lambda v, cin=cin: v.tensor_copy(q4b(xrh_iN)[:, :, :, 0],
                                                         qcs(carry_iN[:, cin:cin + 2 * NCI])))
            p.guard("dve")
            p.do("dve", lambda v: v.tensor_tensor(q4b(xrl_re)[:, :, :, 1:TOK],
                                                  q4(xc_re)[:, :, :, 0:TOK - 1],
                                                  q4b(xrh_re)[:, :, :, 1:TOK], ALU.subtract))
            p.do("dve", lambda v: v.scalar_tensor_tensor(q4b(xrl_iN)[:, :, :, 1:TOK],
                                                         q4(xc_im)[:, :, :, 0:TOK - 1], -1.0,
                                                         q4b(xrh_iN)[:, :, :, 1:TOK],
                                                         ALU.mult, ALU.subtract))
            p.do("dve", lambda v, cin=cin: v.tensor_tensor(q4b(xrl_re)[:, :, :, 0],
                                                           qcs(carry_re[:, cin:cin + 2 * NCI]),
                                                           q4b(xrh_re)[:, :, :, 0], ALU.subtract))
            p.do("dve", lambda v, cin=cin: v.tensor_tensor(q4b(xrl_iN)[:, :, :, 0],
                                                           qcs(carry_iN[:, cin:cin + 2 * NCI]),
                                                           q4b(xrh_iN)[:, :, :, 0], ALU.subtract))
            p.do("dve", lambda v, cout=cout: v.tensor_copy(qcs(carry_re[:, cout:cout + 2 * NCI]),
                                                           q4(xc_re)[:, :, :, TOK - 1]))
            p.do("dve", lambda v, cout=cout: v.tensor_scalar(qcs(carry_iN[:, cout:cout + 2 * NCI]),
                                                             q4(xc_im)[:, :, :, TOK - 1],
                                                             -1.0, None, ALU.mult))
            xrv = p.bump("dve", "dve")
            p.wait("te", "dve", xrv)

        def s_u():
            zr_pr = (zr_h, zr_l); zi_pr = (zi_h, zi_l)
            xre_pr = (xrh_re, xrl_re); xiN_pr = (xrh_iN, xrl_iN)
            for mc in range(NCI):
                terms_re = mm3([(prtile(zr_pr, kc, mc), xrpair(xre_pr, kc)) for kc in range(NCI)] +
                               [(prtile(zi_pr, kc, mc), xrpair(xiN_pr, kc)) for kc in range(NCI)])
                bre = p.take_banks(1)[0]
                mm_group(bre, T, terms_re)
                uv = p.bump("te", "te")
                p.wait("dve", "te", uv)
                sl = slice(mc * T, (mc + 1) * T)
                p.do("dve", lambda v, b=bre, sl=sl: v.tensor_copy(u_re[:, sl], pb[b][:, 0:T]))
                p.bump("dve", "dve")
                p.release_bank(bre, "dve")
                tP = mm3([(prtile(zi_pr, kc, mc), xrpair(xre_pr, kc)) for kc in range(NCI)])
                tQ = mm3([(prtile(zr_pr, kc, mc), xrpair(xiN_pr, kc)) for kc in range(NCI)])
                psub(tP, tQ, T,
                     lambda v, pP, pQ, sl=sl: v.tensor_tensor(u_im[:, sl], pP[:, 0:T], pQ[:, 0:T],
                                                              ALU.subtract))
            asx["uv"] = p.sems["dve"].n
            if k == 0:
                tapdma("ure", u_re); tapdma("uim", u_im)
        out = [(6, s_wd), (40, s_xln), (48, s_xtp), (65, s_pa), (98, s_px),
               (116, s_xr), (130, s_u)]
        scpts = [140, 145, 150, 155, 160, 165, 170, 175]
        nsl = len(scpts)
        for si in range(nsl):
            ta = (LC * si) // nsl
            tb = (LC * (si + 1)) // nsl
            out.append((scpts[si], lambda ta=ta, tb=tb: scan_inner(ta, tb, asx)))
        out.append((181, lambda: scan_chain(1, NJ // 2 + 1, asx)))
        out.append((186, lambda: scan_chain(NJ // 2 + 1, NJ + 1, asx)))
        out.append((191, lambda: scan_app(asx)))
        return out

    def stages_B(k):
        x_tm = xt_of(k)
        bs = {}
        def s_h():
            s_reh, s_rel = bpair(a_re)
            s_imh, s_iml = bpair(a_im)
            p.do("dve", lambda v: v.tensor_copy(s_reh, u_re))
            p.do("dve", lambda v: v.tensor_copy(s_imh, u_im))
            p.guard("dve")
            p.do("dve", lambda v: v.tensor_sub(s_rel, u_re, s_reh))
            p.do("dve", lambda v: v.tensor_sub(s_iml, u_im, s_imh))
            spv = p.bump("dve", "dve")
            p.wait("te", "dve", spv)
            vtr_pr = (vtr_h, vtr_l); vti_pr = (vti_h, vti_l)
            sre_pr = (s_reh, s_rel); sim_pr = (s_imh, s_iml)
            for mc in range(NCI):
                sl = slice(mc * T, (mc + 1) * T)
                tP = mm3([(prtile(vtr_pr, kc, mc), xrpair(sre_pr, kc)) for kc in range(NCI)])
                tQ = mm3([(prtile(vti_pr, kc, mc), xrpair(sim_pr, kc)) for kc in range(NCI)])
                def hevac(v, pP, pQ, sl=sl, mc=mc):
                    return v.tensor_tensor(xn_tm[:, mc * T:(mc + 1) * T], pP[:, 0:T], pQ[:, 0:T],
                                           ALU.subtract)
                psub(tP, tQ, T, hevac)
                p.do("dve", lambda v, sl=sl, mc=mc: v.tensor_add(xn_tm[:, mc * T:(mc + 1) * T],
                                                                 xn_tm[:, mc * T:(mc + 1) * T],
                                                                 xc_re[:, sl]))
                p.guard("dve")
                p.do("dve", lambda v, sl=sl, mc=mc: v.tensor_copy(hreh[:, sl], xn_tm[:, mc * T:(mc + 1) * T]))
                p.guard("dve")
                p.do("dve", lambda v, sl=sl, mc=mc: v.tensor_tensor(hrel[:, sl],
                                                                    xn_tm[:, mc * T:(mc + 1) * T],
                                                                    hreh[:, sl], ALU.subtract))
                p.bump("dve", "dve")
            bs["hpv"] = p.sems["dve"].n
            if k == 0:
                tapdma("sre", u_re); tapdma("sim", u_im)
        def s_op():
            p.wait("te", "dve", bs["hpv"])
            p.wait("te", *ws[("wo", 0)])
            p.wait("te", *ws[("wo", 1)])
            wov = watb.bitcast(mybir.dt.bfloat16)
            for j in range(2):
                for n in range(2):
                    b = p.take_banks(1)[0]
                    terms = []
                    for cc in range(NCI):
                        lh = hreh[:, cc * T + j * 128: cc * T + (j + 1) * 128]
                        ll = hrel[:, cc * T + j * 128: cc * T + (j + 1) * 128]
                        rh = wov[:, cc * 2048 + n * 512: cc * 2048 + (n + 1) * 512]
                        rl = wov[:, cc * 2048 + 1024 + n * 512: cc * 2048 + 1024 + (n + 1) * 512]
                        terms += [(lh, rh), (lh, rl), (ll, rh)]
                    mm_group(b, 512, terms)
                    ov = p.bump("te", "te")
                    p.wait("dve", "te", ov)
                    sl = slice(j * D + n * 512, j * D + (n + 1) * 512)
                    p.do("dve", lambda v, b=b, sl=sl: v.tensor_add(x_tm[:, sl], x_tm[:, sl], pb[b][:, 0:512]))
                    p.bump("dve", "dve")
                    p.release_bank(b, "dve")
            wst["wout_cons"] = p.sems["te"].n
            if k == 0:
                tapdma("x1", x_tm)
        def s_l2():
            bs["xn2v"] = ln(xn2_of(k), "ln2", k, x_tm)
        def s_l2tp():
            bs["x2v"] = ln_tp(xn2_of(k), bs["xn2v"])
        def s_g():
            gate_topk(k, xn2_of(k), bs["x2v"])
            bs["rwv"] = p.sems["dve"].n
            if k == 0:
                tapdma("rww", rww[:, 0:2 * E])
        def s_rb():
            p.wait("te", "dve", bs["rwv"])
            rwo = (k % 2) * 2 * E
            for j in range(2):
                b = p.take_banks(1)[0]
                p.do("te", lambda t, b=b, j=j, rwo=rwo: t.transpose(pb[b][0:E, 0:128],
                                                                    rww[:, rwo + j * E: rwo + (j + 1) * E],
                                                                    i128))
                tv = p.bump("te", "te")
                p.wait("dve", "te", tv)
                p.do("dve", lambda v, b=b, j=j: v.tensor_copy(rwf9[0:E, j * TOK:(j + 1) * TOK],
                                                              pb[b][0:E, 0:128]))
                p.bump("dve", "dve")
                p.release_bank(b, "dve")
            p.wait("te", "dve")
            for j in range(2):
                for n in range(2):
                    b = p.take_banks(1)[0]
                    p.do("te", lambda t, b=b, j=j, n=n:
                         t.matmul(pb[b][:, 0:512], rwf9[0:9, j * TOK:(j + 1) * TOK],
                                  b2sb[0:9, n * 512:(n + 1) * 512], start=True, stop=True))
                    tv = p.bump("te", "te")
                    p.wait("dve", "te", tv)
                    sl = slice(j * D + n * 512, j * D + (n + 1) * 512)
                    p.do("dve", lambda v, b=b, sl=sl: v.tensor_add(x_tm[:, sl], x_tm[:, sl],
                                                                   pb[b][:, 0:512]))
                    p.bump("dve", "dve")
                    p.release_bank(b, "dve")


        return [(208, s_h), (228, s_op), (242, s_l2), (250, s_l2tp),
                (260, s_g), (275, s_rb)]

    for _pt, f in stages_A(0):
        f()
    for _pt, f in stages_B(0):
        f()
    pend = {}
    for k in range(NCH):
        x_tm = xt_of(k)
        rwo = (k % 2) * 2 * E
        if k == 0:
            p.wait("sync", "dst", repack_done)
        units = []
        for e in range(9):
            W1s = W1[e] if e < E else SW1
            W3s = W3[e] if e < E else SW3
            W2s = W2[e] if e < E else SW2
            hstate = {}
            for hf in range(NH):
                def dma_fn(slot, sem, e=e, hf=hf):
                    return p.dma("sync", slot.bitcast(mybir.dt.bfloat16), W1P[e, hf], sem=sem)
                def comp_fn(slot, ld, hf=hf, e=e):
                    p.wait("te", ld[0], ld[1])
                    b = p.take_banks(1)[0]
                    sv = slot.bitcast(mybir.dt.bfloat16)
                    x2h, x2l = bpair(xn2_of(k))
                    terms = []
                    for kc in range(ND):
                        wh = sv[:, kc * 128:(kc + 1) * 128]
                        wl = sv[:, 1024 + kc * 128: 1024 + (kc + 1) * 128]
                        xh = x2h[:, kc * T:(kc + 1) * T]
                        xl = x2l[:, kc * T:(kc + 1) * T]
                        terms += [(wh, xh), (wh, xl), (wl, xh)]
                    mm_group(b, T, terms)
                    tv = p.bump("te", "te")
                    p.wait("act", "te", tv)
                    p.do("act", lambda s, b=b, hf=hf, e=e:
                         s.activation(hffn[:, hf * T:(hf + 1) * T], pb[b][:, 0:T], AF.Identity,
                                      bias=b1s[:, e * NH + hf: e * NH + hf + 1]))
                    p.bump("act", "act")
                    p.release_bank(b, "act")
                    return tv
                units.append((dma_fn, comp_fn))
            for hf in range(NH):
                def dma_fn(slot, sem, e=e, hf=hf):
                    return p.dma("sync", slot.bitcast(mybir.dt.bfloat16), W3P[e, hf], sem=sem)
                def comp_fn(slot, ld, hf=hf, e=e, hstate=hstate):
                    p.wait("te", ld[0], ld[1])
                    b = p.take_banks(1)[0]
                    sv = slot.bitcast(mybir.dt.bfloat16)
                    x2h, x2l = bpair(xn2_of(k))
                    terms = []
                    for kc in range(ND):
                        wh = sv[:, kc * 128:(kc + 1) * 128]
                        wl = sv[:, 1024 + kc * 128: 1024 + (kc + 1) * 128]
                        xh = x2h[:, kc * T:(kc + 1) * T]
                        xl = x2l[:, kc * T:(kc + 1) * T]
                        terms += [(wh, xh), (wh, xl), (wl, xh)]
                    mm_group(b, T, terms)
                    tv = p.bump("te", "te")
                    p.wait("dve", "te", tv)
                    p.wait("dve", "act", p.sems["act"].n)
                    hs = hmul[:, (hf % 4) * T:((hf % 4) + 1) * T]
                    p.do("dve", lambda v, b=b, hf=hf, e=e, hs=hs:
                         v.scalar_tensor_tensor(hs, pb[b][:, 0:T],
                                                b3s[:, e * NH + hf: e * NH + hf + 1],
                                                hffn[:, hf * T:(hf + 1) * T], ALU.add, ALU.mult))
                    mv = p.bump("dve", "dve")
                    p.release_bank(b, "dve")
                    hh, hl = hpair(hf)
                    p.wait("act", "dve", mv)
                    p.do("act", lambda s, hs=hs: s.activation(hs, hs, AF.Silu))
                    p.guard("act")
                    p.do("act", lambda s, hh=hh, hs=hs: s.copy(hh, hs))
                    cv = p.bump("act", "act")
                    p.wait("dve", "act", cv)
                    p.do("dve", lambda v, hh=hh, hl=hl, hs=hs: v.tensor_sub(hl, hs, hh))
                    hstate[hf] = p.bump("dve", "dve")
                    return tv
                units.append((dma_fn, comp_fn))
            ystate = {}
            for hf in range(NH):
                def dma_fn(slot, sem, e=e, hf=hf):
                    return p.dma("sync", slot.bitcast(mybir.dt.bfloat16), W2P[e, hf], sem=sem)
                def comp_fn(slot, ld, hf=hf, e=e, ystate=ystate, hstate=hstate):
                    hh, hl = hpair(hf)
                    if hf == 0:
                        ystate["banks"] = p.take_banks(4)
                        p.hold_banks(ystate["banks"])
                        p.wait("te", "dve", hstate[NH - 1])
                    p.wait("te", ld[0], ld[1])
                    sv = slot.bitcast(mybir.dt.bfloat16)
                    for j in range(2):
                        for n in range(2):
                            lh = hh[:, j * 128:(j + 1) * 128]
                            ll = hl[:, j * 128:(j + 1) * 128]
                            rh = sv[:, n * 512:(n + 1) * 512]
                            rl = sv[:, 1024 + n * 512: 1024 + (n + 1) * 512]
                            for ti, (lt, rr) in enumerate(((lh, rh), (lh, rl), (ll, rh))):
                                p.do("te", lambda t, b=ystate["banks"][2 * j + n], lt=lt, rr=rr,
                                     st=(hf == 0 and ti == 0), sp=(hf == NH - 1 and ti == 2):
                                     t.matmul(pb[b][:, 0:512], lt, rr, start=st, stop=sp))
                    tv = p.bump("te", "te")
                    if hf == NH - 1:
                        p.wait("dve", "te", tv)
                        for j in range(2):
                            for n in range(2):
                                b = ystate["banks"][2 * j + n]
                                sl = slice(j * D + n * 512, j * D + (n + 1) * 512)
                                sc = rww[:, rwo + j * E + e: rwo + j * E + e + 1] if e < E else 1.0
                                p.do("dve", lambda v, b=b, sl=sl, sc=sc, x_tm=x_tm:
                                     v.scalar_tensor_tensor(x_tm[:, sl], pb[b][:, 0:512], sc,
                                                            x_tm[:, sl], ALU.mult, ALU.add))
                                p.bump("dve", "dve")
                                p.release_bank(b, "dve")
                        p.unhold_banks(ystate["banks"])
                    return tv
                units.append((dma_fn, comp_fn))
        G = 4
        DEPTH = 6
        anchor = p.sems["te"].n
        for _i in range(DEPTH):
            wslot_free[_i] = max(wslot_free[_i], anchor)
        NU = len(units)
        if k + 1 < NCH:
            pend = {}
            for pt, f in stages_A(k + 1) + stages_B(k + 1):
                pend.setdefault(pt, []).append(f)
        else:
            pend = {}
        ngrp = (NU + G - 1) // G
        gsem = ["dwa" if m % 2 == 0 else "dwb" for m in range(ngrp)]
        gend = {}

        def slot_ap(i):
            return wbuf[:, (i % DEPTH) * 1024:(i % DEPTH) * 1024 + 1024]

        def issue(i):
            p.wait("sync", "te", wslot_free[i % DEPTH], force=True)
            m = i // G
            v = units[i][0](slot_ap(i), gsem[m])
            if i == NU - 1:
                dn = 0
                while (i + 1 + dn) % G != 0:
                    p.dma("sync", dumm[0:1, dn * 8:dn * 8 + 8], ONES[0:1, 0:8], sem=gsem[m])
                    v = p.sems[gsem[m]].n
                    dn += 1
                gend[m] = v
            elif i % G == G - 1:
                gend[m] = v

        for i in range(min(DEPTH, NU)):
            issue(i)
        for i in range(NU):
            m = i // G
            wslot_free[i % DEPTH] = units[i][1](slot_ap(i), (gsem[m], gend[m]))
            if i in pend:
                for f in pend.pop(i):
                    f()
            ip = i + DEPTH
            if ip < NU:
                issue(ip)

        if k == 0:
            tapdma("xout", x_tm)
        p.wait("gps", "dve", p.sems["dve"].n)
        p.dma("gps", OUT.rearrange("(s n) d -> n s d", s=2)[k * TOK:(k + 1) * TOK],
              x_tm.rearrange("p (s d) -> p s d", s=2), sem="dst")

    p.wait("gps", "dst", p.sems["dst"].n)

    import contextlib
    with contextlib.ExitStack() as st:
        for s in p.sems.values():
            s.h = st.enter_context(nc.semaphore("s_" + s.name))
        blk = st.enter_context(nc.Block())
        blk.tensor(lambda e: [f(e) for f in p.ops["te"]])
        blk.vector(lambda e: [f(e) for f in p.ops["dve"]])
        blk.scalar(lambda e: [f(e) for f in p.ops["act"]])
        blk.gpsimd(lambda e: [f(e) for f in p.ops["gps"]])
        blk.sync(lambda e: [f(e) for f in p.ops["sync"]])
    ninstr = {k: len(v) for k, v in p.ops.items()}
    return nc, ninstr


def host_inputs(inp, core, n_cores=8, rows_per_core=4096):
    seqs_per_core = 16 // n_cores
    x = np.ascontiguousarray(
        inp["L0_attn_x"].reshape(16, 2048, 1024)[core * seqs_per_core:(core + 1) * seqs_per_core]
    ).reshape(-1, D)[:rows_per_core]
    m = {
        "x": x,
        "w_a": inp["L0_attn_w_a"], "w_x": inp["L0_attn_w_x"], "w_out": inp["L0_attn_w_out"],
        "mat_v_real": inp["L0_attn_mat_v_real"], "mat_v_imag": inp["L0_attn_mat_v_imag"],
        "h0_re": inp["L0_attn_last_hidden_real"], "h0_im": inp["L0_attn_last_hidden_imag"],
        "gate_w": inp["L0_ffn_gate_w"],
        "gate_bb": np.ascontiguousarray(np.broadcast_to(inp["L0_ffn_gate_b"], (128, E))),
        "w1": inp["L0_ffn_w1"], "b1": inp["L0_ffn_b1"],
        "w3": inp["L0_ffn_w3"], "b3": inp["L0_ffn_b3"],
        "w2": inp["L0_ffn_w2"], "b2": inp["L0_ffn_b2"],
        "sw1": inp["L0_ffn_sw1"], "sb1": inp["L0_ffn_sb1"],
        "sw3": inp["L0_ffn_sw3"], "sb3": inp["L0_ffn_sb3"],
        "sw2": inp["L0_ffn_sw2"], "sb2": inp["L0_ffn_sb2"],
        "eye512": np.eye(INNER, dtype=np.float32),
        "i128": np.eye(128, dtype=np.float32),
        "iota8": np.ascontiguousarray(np.broadcast_to(np.arange(E, dtype=np.float32), (128, E))),
        "ones": np.ones((1, 128), dtype=np.float32),
    }
    return {k: np.ascontiguousarray(np.asarray(v, dtype=np.float32)) for k, v in m.items()}


from concourse.bass_utils import run_bass_kernel_spmd

_NC_CACHE = {}

def kernel(**inputs):
    n_cores = 8
    rows = 4096
    if "nc" not in _NC_CACHE:
        _NC_CACHE["nc"], _ = build(rows_per_core=rows, n_chunks=None, tap_names=(),
                                   newton_iters=4)
    nc = _NC_CACHE["nc"]
    in_maps = [host_inputs(inputs, core=c, n_cores=n_cores, rows_per_core=rows)
               for c in range(n_cores)]
    res = run_bass_kernel_spmd(nc, in_maps, core_ids=list(range(n_cores)))
    out = np.stack([res.results[c]["out"].reshape(2, 2048, D) for c in range(n_cores)])
    return out.reshape(16, 2048, D).astype(np.float32)


# version 19
# speedup vs baseline: 1.0547x; 1.0319x over previous
import numpy as np
import concourse.bass as bass
import concourse.mybir as mybir

F32 = mybir.dt.float32
AF = mybir.ActivationFunctionType
ALU = mybir.AluOpType
AX = mybir.AxisListType

D = 1024
INNER = 512
HID = 1408
E = 8
TOK = 128
T = 2 * TOK
ND = D // 128
NCI = INNER // 128
NH = HID // 128
LC = 16
NJ = TOK // LC
EPS = 1e-5
BIG = 1e9
ENGS = ("te", "dve", "act", "gps", "sync")
SEM_NAMES = ("te", "dve", "act", "gps", "dw", "dx", "dst", "dwa", "dwb", "dwc", "dwd")


class PSem:
    def __init__(self, name):
        self.name = name
        self.h = None
        self.n = 0


class Planner:
    def __init__(self):
        self.ops = {e: [] for e in ENGS}
        self.sems = {k: PSem(k) for k in SEM_NAMES}
        self.waited = {e: {} for e in ENGS}
        self.bank_free = [("te", 0)] * 8
        self.bank_rr = 0
        self.dirty = {e: False for e in ENGS}

    AUTOB = ("dve", "act", "gps")

    def do(self, eng, f, _raw=False):
        if _raw or eng not in self.AUTOB:
            self.ops[eng].append(f)
            return
        if self.dirty[eng]:
            self.dirty[eng] = False
            self.wait(eng, eng)
        s = self.sems[eng]
        s.n += 1
        def g(e, f=f, s=s):
            ins = f(e)
            ins.then_inc(s.h, 1)
            return ins
        self.ops[eng].append(g)
        self.dirty[eng] = True

    def bump(self, eng, sem_name, amt=1):
        if eng in self.AUTOB and sem_name == eng:
            return self.sems[sem_name].n
        s = self.sems[sem_name]
        s.n += amt
        last = self.ops[eng][-1]
        def g(e, last=last, s=s, amt=amt):
            ins = last(e)
            ins.then_inc(s.h, amt)
            return ins
        self.ops[eng][-1] = g
        return s.n

    def wait(self, eng, sem_name, val=None, force=False):
        s = self.sems[sem_name]
        val = s.n if val is None else val
        if val <= 0:
            return
        if not force and self.waited[eng].get(sem_name, -1) >= val:
            return
        assert val <= s.n, f"{eng} waits {sem_name}>={val} > planned {s.n}"
        self.waited[eng][sem_name] = val
        self.do(eng, lambda e, s=s, val=val: e.wait_ge(s.h, val), _raw=True)

    def dma(self, eng, out, in_, sem="dw"):
        if eng == "gps":
            self.wait(eng, sem, self.sems[sem].n)
        self.do(eng, lambda e, out=out, in_=in_: e.dma_start(out=out, in_=in_), _raw=True)
        return self.bump(eng, sem, 16)

    def take_banks(self, n):
        out = []
        tries = 0
        while len(out) < n:
            b = self.bank_rr % 8
            self.bank_rr += 1
            tries += 1
            assert tries < 64, "bank allocator livelock (too many held banks)"
            if b in getattr(self, "held", set()) or b in out:
                continue
            sname, v = self.bank_free[b]
            self.wait("te", sname, v)
            out.append(b)
        return out

    def hold_banks(self, banks):
        if not hasattr(self, "held"):
            self.held = set()
        self.held |= set(banks)

    def unhold_banks(self, banks):
        self.held -= set(banks)

    def release_bank(self, b, sem_name):
        self.bank_free[b] = (sem_name, self.sems[sem_name].n)

    def guard(self, eng):
        self.bump(eng, eng)
        self.wait(eng, eng)


def build(rows_per_core=4096, n_chunks=None, tap_names=(), newton_iters=4):
    p = Planner()
    nc = bass.Bass("TRN2", target_bir_lowering=False, debug=False)
    NCH = (rows_per_core // T) if n_chunks is None else n_chunks
    SEQ = rows_per_core // 2

    def din(name, shape):
        return nc.dram_tensor(name, shape, F32, kind="ExternalInput").ap()
    X = din("x", [rows_per_core, D])
    WA = din("w_a", [2 * INNER, D]); WXm = din("w_x", [2 * INNER, D])
    WOUT = din("w_out", [D, INNER])
    MVR = din("mat_v_real", [INNER, INNER]); MVI = din("mat_v_imag", [INNER, INNER])
    H0R = din("h0_re", [INNER]); H0I = din("h0_im", [INNER])
    GW = din("gate_w", [E, D]); GBB = din("gate_bb", [128, E])
    W1 = din("w1", [E, D, HID]); B1 = din("b1", [E, HID])
    W3 = din("w3", [E, D, HID]); B3 = din("b3", [E, HID])
    W2 = din("w2", [E, HID, D]); B2 = din("b2", [E, D])
    SW1 = din("sw1", [D, HID]); SB1 = din("sb1", [HID])
    SW3 = din("sw3", [D, HID]); SB3 = din("sb3", [HID])
    SW2 = din("sw2", [HID, D]); SB2 = din("sb2", [D])
    EYE = din("eye512", [INNER, INNER])
    I128D = din("i128", [128, 128])
    IOTA8 = din("iota8", [128, E])
    ONES = din("ones", [1, 128])
    OUT = nc.dram_tensor("out", [rows_per_core, D], F32, kind="ExternalOutput").ap()
    BF = mybir.dt.bfloat16
    WAT_HBM = nc.dram_tensor("waT_hbm", [D, 4 * INNER], BF, kind="Internal").ap()
    WXT_HBM = nc.dram_tensor("wxT_hbm", [D, 4 * INNER], BF, kind="Internal").ap()
    W1P = nc.dram_tensor("w1p_hbm", [9, NH, 128, 2 * D], BF, kind="Internal").ap()
    W3P = nc.dram_tensor("w3p_hbm", [9, NH, 128, 2 * D], BF, kind="Internal").ap()
    W2P = nc.dram_tensor("w2p_hbm", [9, NH, 128, 2 * D], BF, kind="Internal").ap()
    WOUTP = nc.dram_tensor("woutp_hbm", [INNER, 2 * D], BF, kind="Internal").ap()

    TAPCOLS = {"Zr": NCI * INNER, "Zi": NCI * INNER, "ln1": 2 * D, "afac": NCI * T,
               "aim": NCI * T, "xcre": NCI * T, "xcim": NCI * T, "ure": NCI * T,
               "uim": NCI * T, "sre": NCI * T, "sim": NCI * T, "hre": NCI * T,
               "x1": 2 * D, "ln2": 2 * D, "rww": 2 * E, "hffn0": NH * T, "xout": 2 * D}
    taps = {}
    for name in tap_names:
        taps[name] = nc.dram_tensor("tap_" + name, [128, TAPCOLS[name]], F32,
                                    kind="ExternalOutput").ap()

    def tapdma(name, src, eng_dep="dve"):
        if name not in taps:
            return
        p.wait("gps", eng_dep, p.sems[eng_dep].n)
        st = p.dma("gps", taps[name], src, sem="dst")
        p.wait("dve", "dst", st)
        p.wait("act", "dst", st)
        p.wait("te", "dst", st)
        p.wait("gps", "dst", st)

    def S(name, cols, dtype=F32):
        return nc.alloc_sbuf_tensor(name, [128, cols], dtype).ap()

    i128 = S("c_i128", 128)
    iota8 = S("c_iota8", E)
    gbb = S("c_gbb", E)
    ones = S("c_ones", 128)
    h0 = S("c_h0", 2 * NCI)
    gwT = S("r_gwT", ND * E)
    gwTp = S("r_gwTp", ND * E)
    b1s = S("r_b1", 9 * NH); b3s = S("r_b3", 9 * NH)
    b2sb = S("r_b2sb", D)
    rwf9 = S("r_rwf9", 2 * TOK)
    woutT = S("r_woutT", NCI * D)
    Zr = S("r_Zr", NCI * INNER); Zi = S("r_Zi", NCI * INNER); ZrN = S("r_ZrN", NCI * INNER)
    vT_r = S("r_vTr", NCI * INNER); vT_iN = S("r_vTiN", NCI * INNER)

    pool = S("pool", 11264)
    watb = S("watb", 4096)
    wbuf = S("wbuf", NH * D)
    x_tm = S("x_tm", 2 * D)
    xn_tm = S("xn_tm", 2 * D)
    scr = S("scr", 2 * D)
    scr2 = S("scr2", D)
    tmps = [S(f"tmp{i}", 128) for i in range(6)]
    gtmps = [S(f"gtmp{i}", 128) for i in range(4)]
    small = S("small", 64)
    carry_re = S("carry_re", 2 * 2 * NCI); carry_iN = S("carry_iN", 2 * 2 * NCI)
    scb_re = S("scb_re", 2 * NCI); scb_im = S("scb_im", 2 * NCI)
    scar_r = S("scar_r", NCI * 2 * NJ); scar_i = S("scar_i", NCI * 2 * NJ)
    gate = S("gate", 2 * 48)
    dumm = S("dumm", 64)
    rww = S("rww", 2 * 2 * E)

    def Rg(start, cols, t=pool):
        return t[:, start:start + cols]
    xn1_fm = Rg(0, ND * T)
    xr_re = Rg(0, NCI * T); xr_iN = Rg(NCI * T, NCI * T)
    a_re = scr[:, 0:NCI * T]; a_im = scr[:, NCI * T:2 * NCI * T]
    xn2_fm = Rg(2048, ND * T)
    xc_re = Rg(4096, NCI * T); xc_im = Rg(4096 + NCI * T, NCI * T)
    u_re = Rg(6144, NCI * T); u_im = Rg(6144 + NCI * T, NCI * T)
    P_re = Rg(8192, NCI * T); P_im = Rg(8192 + NCI * T, NCI * T)
    h_re = Rg(10240, NCI * T)
    hffn = woutT[:, 0:NH * T]
    hmul = S("hmul4", 4 * T)
    xn2_b = wbuf[:, 6144:8192]
    x_tm2 = wbuf[:, 8960:11008]
    v_r = Rg(0, 2048); v_iN = Rg(2048, 2048); vT_i = Rg(4096, 2048)
    Xr = Rg(6144, 2048); Xi = Rg(8192, 2048)
    Rr = x_tm[:, 0:2048]; Ri = xn_tm[:, 0:2048]; RiN = Rg(0, 2048, scr)
    eye = Rg(0, 2048, watb)
    BF16 = mybir.dt.bfloat16
    def bview(reg, half):
        return reg.bitcast(BF16)[:, half * 2048:(half + 1) * 2048]
    Xbr, Xbi = bview(Xr, 0), bview(Xr, 1)
    Zbr, Zbi = bview(Xi, 0), bview(Xi, 1)
    Rbr, Rbi = bview(Rr, 0), bview(Rr, 1)
    RbiN = bview(Ri, 0)
    vTbr, vTbi = bview(v_r, 0), bview(v_r, 1)
    vTbiN = bview(v_iN, 0)
    Xr_p = v_iN.bitcast(F32) if False else None
    wstage = watb

    pb = [nc.alloc_psum_tensor(f"pb{i}", [128, 512], F32).ap() for i in range(8)]

    def bpair(reg):
        n = reg.shape[1]
        bc = reg.bitcast(mybir.dt.bfloat16)
        return bc[:, 0:n], bc[:, n:2 * n]

    def hpair(hf):
        bc = hffn.bitcast(mybir.dt.bfloat16)
        return bc[:, hf * 2 * T: hf * 2 * T + T], bc[:, hf * 2 * T + T: hf * 2 * T + 2 * T]

    def mm3(terms):
        out = []
        for (lp, rp) in terms:
            lh, ll = lp
            rh, rl = rp
            out += [(lh, rh), (lh, rl), (ll, rh)]
        return out

    def psub(lhs_terms_p, lhs_terms_q, cols, evac):
        bP, bQ = p.take_banks(2)
        mm_group(bP, cols, lhs_terms_p)
        mm_group(bQ, cols, lhs_terms_q)
        tv = p.bump("te", "te")
        p.wait("dve", "te", tv)
        p.do("dve", lambda v, bP=bP, cols=cols: v.tensor_copy(scr2[:, 0:cols], pb[bP][:, 0:cols]))
        p.do("dve", lambda v, bQ=bQ, cols=cols: evac(v, scr2[:, 0:cols], pb[bQ]))
        p.bump("dve", "dve")
        p.release_bank(bP, "dve"); p.release_bank(bQ, "dve")

    xrh_re, xrl_re = bpair(xr_re)
    xrh_iN, xrl_iN = bpair(xr_iN)

    def prtile(pr, kc, mc):
        h, l = pr
        return (mtile(h, kc, mc), mtile(l, kc, mc))

    def xrpair(pr, kc):
        h, l = pr
        return (h[:, kc * T:(kc + 1) * T], l[:, kc * T:(kc + 1) * T])

    hreh, hrel = bpair(h_re)

    def mtile(msb, kc, mc):
        return msb[:, kc * INNER + mc * 128: kc * INNER + (mc + 1) * 128]

    def q5(ap):
        return ap.rearrange("p (c s j u) -> p c s j u", c=NCI, s=2, j=NJ)

    def q4(ap):
        return ap.rearrange("p (c s u) -> p c s u", c=NCI, s=2)

    def qcs(ap):
        return ap.rearrange("p (c s) -> p c s", c=NCI)

    p.dma("sync", i128, I128D)
    p.dma("sync", iota8, IOTA8)
    p.dma("sync", gbb, GBB)
    p.dma("sync", ones[0:1, :], ONES)
    p.dma("sync", h0[:, 0:NCI] if False else scr2[0:NCI, 0:128],
          H0R.rearrange("(c p) -> c p", p=128))
    p.dma("sync", scr2[32:32 + NCI, 0:128], H0I.rearrange("(c p) -> c p", p=128))
    p.dma("sync", b2sb[0:E, :], B2)
    p.dma("sync", b2sb[E:E + 1, :], SB2.unsqueeze(0))
    p.dma("sync", rwf9[E:E + 1, 0:TOK], ONES)
    p.dma("sync", rwf9[E:E + 1, TOK:2 * TOK], ONES)
    p.dma("sync", v_r.rearrange("p (c j) -> p c j", c=NCI),
          MVR.rearrange("(c p) j -> p c j", p=128))
    p.dma("sync", v_iN.rearrange("p (c j) -> p c j", c=NCI),
          MVI.rearrange("(c p) j -> p c j", p=128))
    p.dma("sync", scr[0:E, 0:D], GW)
    ld0 = p.dma("sync", wstage.rearrange("p (o j) -> p o j", o=ND),
                WOUT.rearrange("(o p) j -> p o j", p=128))

    p.wait("dve", "dw", ld0)
    p.do("dve", lambda v: v.memset(scb_re, 0.0))
    p.do("dve", lambda v: v.memset(scb_im, 0.0))
    p.bump("dve", "dve")

    def transp(src_ap, *evacs):
        kp, m = src_ap.shape[0], src_ap.shape[1]
        bp = src_ap.base_partition()
        b = p.take_banks(1)[0]
        p.do("te", lambda t, s=src_ap, b=b, kp=kp, m=m, bp=bp:
             t.transpose(pb[b][0:m, 0:kp], s, i128[bp:bp + kp, bp:bp + kp]))
        tv = p.bump("te", "te")
        p.wait("dve", "te", tv)
        for ev in evacs:
            p.do("dve", lambda v, b=b, ev=ev: ev(v, pb[b]))
        p.bump("dve", "dve")
        p.release_bank(b, "dve")

    p.wait("te", "dw", ld0)
    for (src, dst) in ((v_r, vT_r), (v_iN, vT_i)):
        for cj in range(NCI):
            for ci in range(NCI):
                transp(src[:, ci * INNER + cj * 128: ci * INNER + (cj + 1) * 128],
                       lambda v, ps, dst=dst, ci=ci, cj=cj:
                       v.tensor_copy(dst[:, cj * INNER + ci * 128: cj * INNER + (ci + 1) * 128],
                                     ps[:, 0:128]))
    p.do("dve", lambda v: v.tensor_scalar_mul(vT_iN, vT_i, -1.0))
    p.do("dve", lambda v: v.tensor_scalar_mul(v_iN, v_iN, -1.0))
    p.bump("dve", "dve")

    for c in range(NCI):
        for o in range(ND):
            transp(wstage[:, o * INNER + c * 128: o * INNER + (c + 1) * 128],
                   lambda v, ps, c=c, o=o:
                   v.tensor_copy(woutT[:, c * D + o * 128: c * D + (o + 1) * 128], ps[:, 0:128]))
    for c in range(ND):
        transp(scr[0:E, c * 128:(c + 1) * 128],
               lambda v, ps, c=c: v.tensor_copy(gwT[:, c * E:(c + 1) * E], ps[:, 0:E]))
    transp(scr2[0:NCI, 0:128],
           lambda v, ps: v.tensor_copy(h0[:, 0:NCI], ps[:, 0:NCI]))
    transp(scr2[32:32 + NCI, 0:128],
           lambda v, ps: v.tensor_copy(h0[:, NCI:2 * NCI], ps[:, 0:NCI]))
    p.do("dve", lambda v: v.tensor_copy(
        qcs(carry_re[:, 0:2 * NCI]), h0[:, 0:NCI].unsqueeze(2).to_broadcast([128, NCI, 2])))
    p.do("dve", lambda v: v.tensor_scalar(
        qcs(carry_iN[:, 0:2 * NCI]), h0[:, NCI:2 * NCI].unsqueeze(2).to_broadcast([128, NCI, 2]),
        -1.0, None, ALU.mult))
    p.bump("dve", "dve")
    for (BB, SBB, dst) in ((B1, SB1, b1s), (B3, SB3, b3s)):
        p.wait("sync", "dve", p.sems["dve"].n)
        p.dma("sync", scr[0:E, 0:HID], BB)
        ldb = p.dma("sync", scr[E:E + 1, 0:HID], SBB.unsqueeze(0))
        p.wait("te", "dw", ldb)
        for hc in range(NH):
            transp(scr[0:9, hc * 128:(hc + 1) * 128],
                   lambda v, ps, hc=hc, dst=dst: v.tensor_copy(
                       dst.rearrange("p (e h) -> p e h", e=9)[:, :, hc], ps[:, 0:9]))

    for (W_in, W_hbm) in ((WA, WAT_HBM), (WXm, WXT_HBM)):
        p.wait("sync", "te", p.sems["te"].n)
        p.wait("sync", "dve", p.sems["dve"].n)
        ld = p.dma("sync", wbuf[:, 0:ND * D].rearrange("p (o d) -> p o d", o=ND),
                   W_in.rearrange("(o p) d -> p o d", p=128))
        p.wait("te", "dw", ld)
        scb = scr.bitcast(mybir.dt.bfloat16)
        for dc in range(ND):
            for o in range(ND):
                b = p.take_banks(1)[0]
                p.do("te", lambda t, o=o, dc=dc, b=b:
                     t.transpose(pb[b][:, 0:128],
                                 wbuf[:, o * D + dc * 128: o * D + (dc + 1) * 128], i128))
                tv = p.bump("te", "te")
                p.wait("dve", "te", tv)
                p.do("dve", lambda v, b=b, o=o:
                     v.tensor_copy(scb[:, o * 64: o * 64 + 64], pb[b][:, 0:128:2]))
                p.do("dve", lambda v, b=b, o=o:
                     v.tensor_copy(scb[:, 512 + o * 64: 512 + o * 64 + 64], pb[b][:, 1:128:2]))
                p.guard("dve")
                p.do("dve", lambda v, b=b, o=o:
                     v.tensor_sub(scb[:, 1024 + o * 64: 1024 + o * 64 + 64], pb[b][:, 0:128:2],
                                  scb[:, o * 64: o * 64 + 64]))
                p.do("dve", lambda v, b=b, o=o:
                     v.tensor_sub(scb[:, 1536 + o * 64: 1536 + o * 64 + 64], pb[b][:, 1:128:2],
                                  scb[:, 512 + o * 64: 512 + o * 64 + 64]))
                p.bump("dve", "dve")
                p.release_bank(b, "dve")
            p.wait("gps", "dve", p.sems["dve"].n)
            st = p.dma("gps", W_hbm[dc * 128:(dc + 1) * 128, :], scb[:, 0:2048], sem="dst")
            p.wait("dve", "dst", st)
    watw_done = p.sems["dst"].n

    def repack(Wsrcs, Wdst, colblocks):
        p.wait("sync", "dw", p.sems["dw"].n)
        stg = watb[:, 2048:4096].bitcast(mybir.dt.bfloat16)
        cdim = ND if colblocks else NH
        for e in range(9):
            Ws = Wsrcs[0][e] if e < E else Wsrcs[1]
            p.wait("sync", "dst", p.sems["dst"].n)
            p.wait("sync", "gps", p.sems["gps"].n)
            ldw = p.dma("sync", wbuf.rearrange("p (c h) -> p c h", c=cdim),
                        Ws.rearrange("(c p) h -> p c h", p=128))
            p.wait("gps", "dw", ldw)
            for hf in range(NH):
                if colblocks:
                    blk = wbuf.rearrange("p (c h) -> p c h", c=ND)[:, :, hf * 128:(hf + 1) * 128]
                    sl = stg[:, 0:2048].rearrange("p (x c q) -> p x c q", x=2, c=ND)
                    hi, lo = sl[:, 0], sl[:, 1]
                else:
                    blk = wbuf[:, hf * D:(hf + 1) * D]
                    hi, lo = stg[:, 0:1024], stg[:, 1024:2048]
                p.wait("gps", "dst", p.sems["dst"].n)
                p.do("gps", lambda g, blk=blk, hi=hi: g.tensor_copy(hi, blk))
                p.guard("gps")
                p.do("gps", lambda g, blk=blk, hi=hi, lo=lo: g.tensor_tensor(lo, blk, hi, ALU.subtract))
                p.bump("gps", "gps")
                p.wait("gps", "gps", p.sems["gps"].n)
                p.dma("gps", Wdst[e, hf], stg[:, 0:2048] if not colblocks else None, sem="dst") \
                    if False else None
                p.dma("gps", Wdst[e, hf], stg[:, 0:2048], sem="dst")

    p.wait("sync", "dst", watw_done)
    ld_eye = p.dma("sync", eye.rearrange("p (c j) -> p c j", c=NCI),
                   EYE.rearrange("(c p) j -> p c j", p=128))
    repack((W1, SW1), W1P, True)
    repack((W3, SW3), W3P, True)
    repack((W2, SW2), W2P, False)
    repack_done = p.sems["dst"].n
    p.wait("dve", "dw", ld_eye)
    p.do("dve", lambda v: v.tensor_mul(scr[:, 0:2048], v_r, v_r))
    p.do("dve", lambda v: v.tensor_mul(scr2, v_iN[:, 0:D], v_iN[:, 0:D]))
    p.do("dve", lambda v: v.tensor_add(scr[:, 0:D], scr[:, 0:D], scr2))
    p.do("dve", lambda v: v.tensor_mul(scr2, v_iN[:, D:2 * D], v_iN[:, D:2 * D]))
    p.do("dve", lambda v: v.tensor_add(scr[:, D:2 * D], scr[:, D:2 * D], scr2))
    p.do("dve", lambda v: v.reduce_sum(small[:, 0:1], scr[:, 0:2048], axis=AX.X))
    p.do("dve", lambda v: v.memset(scr2[:, 0:1], 1.0))
    rsum_done = p.bump("dve", "dve")
    b = p.take_banks(1)[0]
    p.wait("te", "dve", rsum_done)
    p.do("te", lambda t, b=b: t.matmul(pb[b][0:1, 0:1], small[:, 0:1], scr2[:, 0:1],
                                       start=True, stop=True))
    tv = p.bump("te", "te")
    p.wait("dve", "te", tv)
    p.do("dve", lambda v, b=b: v.reciprocal(small[0:1, 2:3], pb[b][0:1, 0:1]))
    p.bump("dve", "dve")
    p.release_bank(b, "dve")
    b = p.take_banks(1)[0]
    p.wait("te", "dve")
    p.do("te", lambda t, b=b: t.matmul(pb[b][:, 0:1], ones[0:1, 0:128], small[0:1, 2:3],
                                       start=True, stop=True))
    tv = p.bump("te", "te")
    p.wait("dve", "te", tv)
    p.do("dve", lambda v, b=b: v.tensor_copy(small[:, 4:5], pb[b][:, 0:1]))
    p.bump("dve", "dve")
    p.release_bank(b, "dve")
    p.guard("dve")
    p.do("dve", lambda v: v.tensor_scalar(Xbr, vT_r, small[:, 4:5], None, ALU.mult))
    p.do("dve", lambda v: v.tensor_scalar(Xbi, vT_iN, small[:, 4:5], None, ALU.mult))
    p.do("dve", lambda v: v.tensor_scalar(Zbr, v_r, small[:, 4:5], None, ALU.mult))
    p.do("dve", lambda v: v.tensor_scalar(Zbi, v_iN, small[:, 4:5], None, ALU.mult))
    p.do("dve", lambda v: v.tensor_copy(vTbr, vT_r))
    p.do("dve", lambda v: v.tensor_copy(vTbi, vT_i))
    p.do("dve", lambda v: v.tensor_copy(vTbiN, vT_iN))
    nv = p.bump("dve", "dve")
    p.wait("te", "dve", nv)

    def mm_group(bank, out_cols, terms):
        n = len(terms)
        for i, (lt, rh) in enumerate(terms):
            p.do("te", lambda t, b=bank, lt=lt, rh=rh, i=i, n=n, oc=out_cols:
                 t.matmul(pb[b][:, 0:oc], lt, rh, start=(i == 0), stop=(i == n - 1)))

    def m512_terms(pairs, mc):
        out = []
        for (lsb, rsb) in pairs:
            for kc in range(NCI):
                out.append((mtile(lsb, kc, mc), rsb[:, kc * INNER:(kc + 1) * INNER]))
        return out

    def newton_iter(xr, xi, zr, zi, rr, ri, riN, vtr, vti, vtiN, cast):
        banks = p.take_banks(8)
        for mc in range(NCI):
            mm_group(banks[mc], INNER, m512_terms(((vtr, xr), (vtiN, xi)), mc))
            mm_group(banks[4 + mc], INNER, m512_terms(((vti, xr), (vtr, xi)), mc))
        t1v = p.bump("te", "te")
        p.wait("dve", "te", t1v)
        for mc in range(NCI):
            sl = slice(mc * INNER, (mc + 1) * INNER)
            p.do("dve", lambda v, b=banks[mc], sl=sl, rr=rr: v.tensor_sub(rr[:, sl], eye[:, sl], pb[b][:, 0:INNER]))
            p.do("dve", lambda v, b=banks[4 + mc], sl=sl, ri=ri: v.tensor_scalar_mul(ri[:, sl], pb[b][:, 0:INNER], -1.0))
            p.do("dve", lambda v, b=banks[4 + mc], sl=sl, riN=riN: v.tensor_copy(riN[:, sl], pb[b][:, 0:INNER]))
            p.bump("dve", "dve")
            p.release_bank(banks[mc], "dve"); p.release_bank(banks[4 + mc], "dve")
        p.wait("te", "dve")
        banks = p.take_banks(8)
        for mc in range(NCI):
            mm_group(banks[mc], INNER, m512_terms(((zr, rr), (zi, riN)), mc))
            mm_group(banks[4 + mc], INNER, m512_terms(((zr, ri), (zi, rr)), mc))
        dv = p.bump("te", "te")
        p.wait("dve", "te", dv)
        for mc in range(NCI):
            sl = slice(mc * INNER, (mc + 1) * INNER)
            p.do("dve", lambda v, b=banks[mc], sl=sl, xr=xr: v.tensor_add(xr[:, sl], xr[:, sl], pb[b][:, 0:INNER]))
            p.do("dve", lambda v, b=banks[4 + mc], sl=sl, xi=xi: v.tensor_add(xi[:, sl], xi[:, sl], pb[b][:, 0:INNER]))
            p.bump("dve", "dve")
            p.release_bank(banks[mc], "dve"); p.release_bank(banks[4 + mc], "dve")
        p.wait("te", "dve")
        banks = p.take_banks(8)
        for mc in range(NCI):
            mm_group(banks[mc], INNER, m512_terms(((rr, zr), (riN, zi)), mc))
            mm_group(banks[4 + mc], INNER, m512_terms(((ri, zr), (rr, zi)), mc))
        ev = p.bump("te", "te")
        p.wait("dve", "te", ev)
        for mc in range(NCI):
            sl = slice(mc * INNER, (mc + 1) * INNER)
            p.do("dve", lambda v, b=banks[mc], sl=sl, zr=zr: v.tensor_add(zr[:, sl], zr[:, sl], pb[b][:, 0:INNER]))
            p.do("dve", lambda v, b=banks[4 + mc], sl=sl, zi=zi: v.tensor_add(zi[:, sl], zi[:, sl], pb[b][:, 0:INNER]))
            p.bump("dve", "dve")
            p.release_bank(banks[mc], "dve"); p.release_bank(banks[4 + mc], "dve")
        p.wait("te", "dve")

    for it in range(34):
        newton_iter(Xbr, Xbi, Zbr, Zbi, Rbr, Rbi, RbiN, vTbr, vTbi, vTbiN, True)
    p.wait("dve", "te", p.sems["te"].n)
    p.do("dve", lambda v: v.tensor_copy(Zr, Zbr))
    p.do("dve", lambda v: v.tensor_copy(Zi, Zbi))
    p.do("dve", lambda v: v.tensor_copy(scr, Xbr))
    p.bump("dve", "dve")
    p.do("dve", lambda v: v.tensor_copy(Xi, Xbi))
    p.do("dve", lambda v: v.tensor_copy(Xr, scr))
    p.bump("dve", "dve")
    p.wait("te", "dve", p.sems["dve"].n)
    for it in range(newton_iters):
        newton_iter(Xr, Xi, Zr, Zi, Rr, Ri, RiN, vT_r, vT_i, vT_iN, False)

    tapdma("Zr", Zr); tapdma("Zi", Zi)
    zr_h, zr_l = bpair(ZrN)
    zi_h, zi_l = bpair(Zr)
    vtr_h, vtr_l = bpair(vT_iN)
    vti_h, vti_l = bpair(vT_r)
    gwh, gwl = bpair(gwTp)
    p.do("dve", lambda v: v.tensor_copy(zr_h, Zr))
    p.guard("dve")
    p.do("dve", lambda v: v.tensor_sub(zr_l, Zr, zr_h))
    p.do("dve", lambda v: v.tensor_copy(zi_h, Zi))
    p.guard("dve")
    p.do("dve", lambda v: v.tensor_sub(zi_l, Zi, zi_h))
    p.do("dve", lambda v: v.tensor_copy(vtr_h, vT_r))
    p.guard("dve")
    p.do("dve", lambda v: v.tensor_sub(vtr_l, vT_r, vtr_h))
    p.do("dve", lambda v: v.tensor_copy(vti_h, vT_i))
    p.guard("dve")
    p.do("dve", lambda v: v.tensor_sub(vti_l, vT_i, vti_h))
    p.do("dve", lambda v: v.tensor_copy(gwh, gwT))
    p.guard("dve")
    p.do("dve", lambda v: v.tensor_sub(gwl, gwT, gwh))
    p.bump("dve", "dve")
    scrb = scr.bitcast(mybir.dt.bfloat16)
    for c in range(NCI):
        p.wait("dve", "dst", p.sems["dst"].n)
        p.do("dve", lambda v, c=c: v.tensor_copy(scrb[:, 0:1024], woutT[:, c * D:(c + 1) * D]))
        p.guard("dve")
        p.do("dve", lambda v, c=c: v.tensor_sub(scrb[:, 1024:2048], woutT[:, c * D:(c + 1) * D],
                                                scrb[:, 0:1024]))
        p.bump("dve", "dve")
        p.wait("gps", "dve", p.sems["dve"].n)
        p.dma("gps", WOUTP[c * 128:(c + 1) * 128, :], scrb[:, 0:2048], sem="dst")
    p.wait("dve", "dst", p.sems["dst"].n)

    def ln(dst_fm, tapname, k, x_tm):
        p.wait("dve", "te", p.sems["te"].n)
        for j in range(2):
            p.do("dve", lambda v, j=j: v.reduce_sum(small[:, 8 + j:9 + j],
                                                    x_tm[:, j * D:(j + 1) * D], axis=AX.X))
        sv = p.bump("dve", "dve")
        p.wait("act", "dve", sv)
        for j in range(2):
            p.do("act", lambda s, j=j: s.activation(scr2, x_tm[:, j * D:(j + 1) * D], AF.Square,
                                                    accum_out=small[:, 10 + j:11 + j]))
        qv = p.bump("act", "act")
        p.wait("dve", "act", qv)
        for j in range(2):
            p.do("dve", lambda v, j=j: v.tensor_scalar_mul(small[:, 12 + j:13 + j],
                                                           small[:, 8 + j:9 + j], 1.0 / D))
            p.do("dve", lambda v, j=j: v.tensor_scalar_mul(small[:, 14 + j:15 + j],
                                                           small[:, 10 + j:11 + j], 1.0 / D))
        p.guard("dve")
        for j in range(2):
            p.do("dve", lambda v, j=j: v.tensor_scalar(small[:, 16 + j:17 + j],
                                                       small[:, 12 + j:13 + j],
                                                       small[:, 12 + j:13 + j], -1.0,
                                                       ALU.mult, ALU.mult))
        p.guard("dve")
        for j in range(2):
            p.do("dve", lambda v, j=j: v.tensor_add(small[:, 14 + j:15 + j],
                                                    small[:, 14 + j:15 + j], small[:, 16 + j:17 + j]))
            p.do("dve", lambda v, j=j: v.tensor_scalar_add(small[:, 14 + j:15 + j],
                                                           small[:, 14 + j:15 + j], EPS))
        p.guard("dve")
        for j in range(2):
            p.do("dve", lambda v, j=j: v.reciprocal(small[:, 18 + j:19 + j], small[:, 14 + j:15 + j]))
        rv = p.bump("dve", "dve")
        p.wait("act", "dve", rv)
        for j in range(2):
            p.do("act", lambda s, j=j: s.activation(small[:, 20 + j:21 + j],
                                                    small[:, 18 + j:19 + j], AF.Sqrt))
        rstdv = p.bump("act", "act")
        p.wait("dve", "act", rstdv)
        for j in range(2):
            p.do("dve", lambda v, j=j:
                 v.tensor_scalar(xn_tm[:, j * D:(j + 1) * D], x_tm[:, j * D:(j + 1) * D],
                                 small[:, 12 + j:13 + j], small[:, 20 + j:21 + j],
                                 ALU.subtract, ALU.mult))
        xnv = p.bump("dve", "dve")
        if k == 0:
            tapdma(tapname, xn_tm)
        return xnv

    def ln_tp(dst_fm, xnv):
        p.wait("te", "dve", xnv)
        dh, dl = bpair(dst_fm)
        for c in range(ND):
            for j in range(2):
                sl = slice(c * T + j * 128, c * T + (j + 1) * 128)
                transp(xn_tm[:, j * D + c * 128: j * D + (c + 1) * 128],
                       lambda v, ps, sl=sl: v.tensor_copy(dh[:, sl], ps[:, 0:128]),
                       lambda v, ps, sl=sl: v.tensor_tensor(dl[:, sl], ps[:, 0:128],
                                                            dh[:, sl], ALU.subtract))
        return p.sems["dve"].n

    def proj_and_factor(W_hbm, dst_re, dst_im, kind, key, xv, nxt):
        p.wait("te", "dve", xv)
        banks = p.take_banks(8)
        wbv = watb.bitcast(mybir.dt.bfloat16)
        x1h, x1l = bpair(xn1_fm)
        for tn in range(4):
            p.wait("te", *ws[(key, tn)])
            for m in range(8):
                for kcl in range(2):
                    kca = 2 * tn + kcl
                    base = (tn % 2) * 4096 + kcl * 2048
                    wh = wbv[:, base + m * 128: base + (m + 1) * 128]
                    wl = wbv[:, base + 1024 + m * 128: base + 1024 + (m + 1) * 128]
                    xh = x1h[:, kca * T:(kca + 1) * T]
                    xl = x1l[:, kca * T:(kca + 1) * T]
                    for ti, (lt, rh) in enumerate(((wh, xh), (wh, xl), (wl, xh))):
                        p.do("te", lambda t, b=banks[m], lt=lt, rh=rh, st=(kca == 0 and ti == 0),
                             sp=(kca == 7 and ti == 2):
                             t.matmul(pb[b][:, 0:T], lt, rh, start=st, stop=sp))
            pv = p.bump("te", "te")
            if nxt[tn] is not None:
                k2, t2, W2 = nxt[tn]
                wissue(k2, t2, W2, pv)
        p.wait("dve", "te", pv)
        fs = xn_tm
        for cc in range(NCI):
            p.do("dve", lambda v, b=banks[cc], cc=cc:
                 v.tensor_copy(fs[:, cc * 512: cc * 512 + T], pb[b][:, 0:T]))
            p.bump("dve", "dve")
            p.release_bank(banks[cc], "dve")
            p.do("dve", lambda v, b=banks[4 + cc], cc=cc:
                 v.tensor_copy(fs[:, cc * 512 + T:(cc + 1) * 512], pb[b][:, 0:T]))
            p.bump("dve", "dve")
            p.release_bank(banks[4 + cc], "dve")
        p.wait("act", "dve", p.sems["dve"].n)
        m2f = pool[:, 8192:10240]
        p.do("act", lambda s: s.activation(m2f, fs, AF.Square))
        sqv = p.bump("act", "act")
        p.wait("dve", "act", sqv)
        m2 = m2f.rearrange("p (c a u) -> p c a u", c=NCI, a=2)
        s2c = scr2.rearrange("p (c u) -> p c u", c=NCI)
        p.do("dve", lambda v: v.tensor_add(s2c, m2[:, :, 0, :], m2[:, :, 1, :]))
        m2v = p.bump("dve", "dve")
        p.wait("act", "dve", m2v)
        if kind == "a":
            p.do("act", lambda s: s.activation(m2f[:, 0:D], scr2, AF.Sqrt))
            rtv = p.bump("act", "act")
            p.wait("dve", "act", rtv)
            p.do("dve", lambda v: v.tensor_scalar_add(scr2, scr2, 1.0))
            p.guard("dve")
            p.do("dve", lambda v: v.reciprocal(scr2, scr2))
            p.guard("dve")
            p.do("dve", lambda v: v.tensor_mul(scr2, scr2, m2f[:, 0:D]))
            p.guard("dve")
        else:
            p.do("act", lambda s: s.activation(scr2, scr2, AF.Sqrt))
            rtv = p.bump("act", "act")
            p.wait("dve", "act", rtv)
        fre = fs.rearrange("p (c a u) -> p c a u", c=NCI, a=2)
        p.do("dve", lambda v: v.tensor_mul(dst_re.rearrange("p (c u) -> p c u", c=NCI),
                                           fre[:, :, 0, :], s2c))
        p.do("dve", lambda v: v.tensor_mul(dst_im.rearrange("p (c u) -> p c u", c=NCI),
                                           fre[:, :, 1, :], s2c))
        p.bump("dve", "dve")

    def scan_inner(ta, tb, st):
        qa_r, qa_i = q5(a_re), q5(a_im)
        qu_r, qu_i = q5(u_re), q5(u_im)
        qP_r, qP_i = q5(P_re), q5(P_im)
        def t128(i):
            return tmps[i][:, 0:NCI * 2 * NJ].rearrange("p (c s j) -> p c s j", c=NCI, s=2)
        def g128(i):
            return gtmps[i][:, 0:NCI * 2 * NJ].rearrange("p (c s j) -> p c s j", c=NCI, s=2)
        t1_, t2_, t3_, t4_, t5_, t6_ = (t128(i) for i in range(6))
        g1, g2, g3, g4 = (g128(i) for i in range(4))
        t1, t2, t3, t4, t5, t6 = t1_, t2_, t3_, t4_, t5_, t6_
        p.wait("gps", "dve", st["uv"])
        for t in range(ta, tb):
            ar, ai = qa_r[:, :, :, :, t], qa_i[:, :, :, :, t]
            ur, ui = qu_r[:, :, :, :, t], qu_i[:, :, :, :, t]
            pr, pi = qP_r[:, :, :, :, t], qP_i[:, :, :, :, t]
            if t == 0:
                p.do("gps", lambda g, ar=ar, ur=ur: g.tensor_mul(t1, ar, ur))
                p.do("gps", lambda g, ai=ai, ui=ui: g.tensor_mul(t2, ai, ui))
                p.do("gps", lambda g, ar=ar, ui=ui: g.tensor_mul(t3, ar, ui))
                p.do("gps", lambda g, ai=ai, ur=ur: g.tensor_mul(t4, ai, ur))
                p.do("gps", lambda g, ur=ur: g.tensor_sub(ur, t1, t2))
                p.do("gps", lambda g, ui=ui: g.tensor_add(ui, t3, t4))
                p.do("gps", lambda g, pr=pr, ar=ar: g.tensor_copy(pr, ar))
                p.do("gps", lambda g, pi=pi, ai=ai: g.tensor_copy(pi, ai))
            else:
                urp, uip = qu_r[:, :, :, :, t - 1], qu_i[:, :, :, :, t - 1]
                prp, pip = qP_r[:, :, :, :, t - 1], qP_i[:, :, :, :, t - 1]
                p.do("gps", lambda g, urp=urp, ur=ur: g.tensor_add(t5, urp, ur))
                p.do("gps", lambda g, uip=uip, ui=ui: g.tensor_add(t6, uip, ui))
                p.do("gps", lambda g, ar=ar: g.tensor_mul(t1, ar, t5))
                p.do("gps", lambda g, ai=ai: g.tensor_mul(t2, ai, t6))
                p.do("gps", lambda g, ur=ur: g.tensor_sub(ur, t1, t2))
                p.do("gps", lambda g, ar=ar: g.tensor_mul(t3, ar, t6))
                p.do("gps", lambda g, ai=ai: g.tensor_mul(t4, ai, t5))
                p.do("gps", lambda g, ui=ui: g.tensor_add(ui, t3, t4))
                p.do("gps", lambda g, ar=ar, prp=prp: g.tensor_mul(g1, ar, prp))
                p.do("gps", lambda g, ai=ai, pip=pip: g.tensor_mul(g2, ai, pip))
                p.do("gps", lambda g, pr=pr: g.tensor_sub(pr, g1, g2))
                p.do("gps", lambda g, ar=ar, pip=pip: g.tensor_mul(g3, ar, pip))
                p.do("gps", lambda g, ai=ai, prp=prp: g.tensor_mul(g4, ai, prp))
                p.do("gps", lambda g, pi=pi: g.tensor_add(pi, g3, g4))
        st["ptv"] = p.bump("gps", "gps")

    def scan_chain(ja, jb, st):
        qu_r, qu_i = q5(u_re), q5(u_im)
        qP_r, qP_i = q5(P_re), q5(P_im)
        qs_r = scar_r.rearrange("p (c s j) -> p c s j", c=NCI, s=2)
        qs_i = scar_i.rearrange("p (c s j) -> p c s j", c=NCI, s=2)
        def t8(i):
            return tmps[i][:, 0:8].rearrange("p (c s) -> p c s", c=NCI)
        t1b, t2b, t3b, t4b, t5b = (t8(i) for i in range(5))
        if ja == 1:
            p.do("gps", lambda g: g.tensor_copy(qs_r[:, :, :, 0], qcs(scb_re)))
            p.do("gps", lambda g: g.tensor_copy(qs_i[:, :, :, 0], qcs(scb_im)))
        for j in range(ja, jb):
            Ar = qP_r[:, :, :, j - 1, LC - 1]; Ai = qP_i[:, :, :, j - 1, LC - 1]
            Br = qu_r[:, :, :, j - 1, LC - 1]; Bi = qu_i[:, :, :, j - 1, LC - 1]
            Srp = qs_r[:, :, :, j - 1]; Sip = qs_i[:, :, :, j - 1]
            Sr = qs_r[:, :, :, j] if j < NJ else qcs(scb_re)
            Si = qs_i[:, :, :, j] if j < NJ else qcs(scb_im)
            p.do("gps", lambda g, Ar=Ar, Srp=Srp: g.tensor_mul(t1b, Ar, Srp))
            p.do("gps", lambda g, Ai=Ai, Sip=Sip: g.tensor_mul(t2b, Ai, Sip))
            p.do("gps", lambda g, Ar=Ar, Sip=Sip: g.tensor_mul(t3b, Ar, Sip))
            p.do("gps", lambda g, Ai=Ai, Srp=Srp: g.tensor_mul(t4b, Ai, Srp))
            p.do("gps", lambda g: g.tensor_sub(t5b, t1b, t2b))
            p.do("gps", lambda g, Sr=Sr, Br=Br: g.tensor_add(Sr, t5b, Br))
            p.do("gps", lambda g: g.tensor_add(t5b, t3b, t4b))
            p.do("gps", lambda g, Si=Si, Bi=Bi: g.tensor_add(Si, t5b, Bi))
        p.bump("gps", "gps")

    def scan_app(st):
        qP_r, qP_i = q5(P_re), q5(P_im)
        qs_r = scar_r.rearrange("p (c s j) -> p c s j", c=NCI, s=2)
        qs_i = scar_i.rearrange("p (c s j) -> p c s j", c=NCI, s=2)
        Srb = qs_r.unsqueeze(4).to_broadcast([128, NCI, 2, NJ, LC])
        Sib = qs_i.unsqueeze(4).to_broadcast([128, NCI, 2, NJ, LC])
        sA = scr[:, 0:1024].rearrange("p (c s j u) -> p c s j u", c=NCI, s=2, j=NJ)
        sB = scr[:, 1024:2048].rearrange("p (c s j u) -> p c s j u", c=NCI, s=2, j=NJ)
        p.do("gps", lambda g: g.tensor_mul(sA, qP_r, Srb))
        p.do("gps", lambda g: g.tensor_mul(sB, qP_i, Sib))
        p.do("gps", lambda g: g.tensor_sub(scr[:, 0:1024], scr[:, 0:1024], scr[:, 1024:2048]))
        p.do("gps", lambda g: g.tensor_add(u_re, u_re, scr[:, 0:1024]))
        p.do("gps", lambda g: g.tensor_mul(sA, qP_r, Sib))
        p.do("gps", lambda g: g.tensor_mul(sB, qP_i, Srb))
        p.do("gps", lambda g: g.tensor_add(scr[:, 0:1024], scr[:, 0:1024], scr[:, 1024:2048]))
        p.do("gps", lambda g: g.tensor_add(u_im, u_im, scr[:, 0:1024]))
        wst["scanv"] = p.bump("gps", "gps")

    def gate_topk(k, xn2q, x2v):
        rwo = (k % 2) * 2 * E
        p.wait("te", "dve", x2v)
        p.wait("dve", "te", p.sems["te"].n)
        x2h, x2l = bpair(xn2q)
        for j in range(2):
            b = p.take_banks(1)[0]
            terms = []
            for kc in range(ND):
                lh = x2h[:, kc * T + j * 128: kc * T + (j + 1) * 128]
                ll = x2l[:, kc * T + j * 128: kc * T + (j + 1) * 128]
                rh = gwh[:, kc * E:(kc + 1) * E]
                rl = gwl[:, kc * E:(kc + 1) * E]
                terms += [(lh, rh), (lh, rl), (ll, rh)]
            mm_group(b, E, terms)
            tv = p.bump("te", "te")
            p.wait("dve", "te", tv)
            p.do("dve", lambda v, b=b, j=j: v.tensor_copy(gate[:, j * 48: j * 48 + E], pb[b][:, 0:E]))
            p.bump("dve", "dve")
            p.release_bank(b, "dve")
        def g(j, off, n=E):
            return gate[:, j * 48 + off: j * 48 + off + n]
        for j in range(2):
            p.do("dve", lambda v, j=j: v.reduce_max(small[:, 24 + j:25 + j], g(j, 0), axis=AX.X))
        p.guard("dve")
        for j in range(2):
            p.do("dve", lambda v, j=j: v.tensor_scalar_mul(small[:, 26 + j:27 + j],
                                                           small[:, 24 + j:25 + j], -1.0))
        ngv = p.bump("dve", "dve")
        p.wait("act", "dve", ngv)
        for j in range(2):
            p.do("act", lambda s, j=j: s.activation(g(j, 8), g(j, 0), AF.Exp,
                                                    bias=small[:, 26 + j:27 + j]))
        ev = p.bump("act", "act")
        p.wait("dve", "act", ev)
        for j in range(2):
            p.do("dve", lambda v, j=j: v.reduce_sum(small[:, 28 + j:29 + j], g(j, 8), axis=AX.X))
        p.guard("dve")
        for j in range(2):
            p.do("dve", lambda v, j=j: v.reciprocal(small[:, 30 + j:31 + j], small[:, 28 + j:29 + j]))
        p.guard("dve")
        for j in range(2):
            p.do("dve", lambda v, j=j: v.tensor_scalar(g(j, 8), g(j, 8), small[:, 30 + j:31 + j],
                                                       None, ALU.mult))
            p.do("dve", lambda v, j=j: v.tensor_add(g(j, 16), g(j, 8), gbb))
        p.guard("dve")
        for pass_i in range(2):
            for j in range(2):
                p.do("dve", lambda v, j=j: v.reduce_max(small[:, 32 + j:33 + j], g(j, 16), axis=AX.X))
            p.guard("dve")
            for j in range(2):
                p.do("dve", lambda v, j=j: v.tensor_scalar(g(j, 24), g(j, 16),
                                                           small[:, 32 + j:33 + j], None, ALU.is_ge))
            p.guard("dve")
            for j in range(2):
                p.do("dve", lambda v, j=j: v.tensor_scalar(g(j, 32), g(j, 24), -BIG, BIG,
                                                           ALU.mult, ALU.add))
                p.do("dve", lambda v, j=j: v.tensor_add(g(j, 32), g(j, 32), iota8))
            p.guard("dve")
            for j in range(2):
                p.do("dve", lambda v, j=j: v.tensor_reduce(small[:, 34 + j:35 + j], g(j, 32),
                                                           AX.X, ALU.min))
            p.guard("dve")
            for j in range(2):
                p.do("dve", lambda v, j=j: v.tensor_scalar(g(j, 24), iota8,
                                                           small[:, 34 + j:35 + j], None, ALU.is_equal))
            p.guard("dve")
            for j in range(2):
                p.do("dve", lambda v, j=j: v.tensor_mul(g(j, 40), g(j, 8), g(j, 24)))
            p.guard("dve")
            for j in range(2):
                p.do("dve", lambda v, j=j, pi=pass_i:
                     v.reduce_sum(small[:, 36 + 2 * j + pi:37 + 2 * j + pi], g(j, 40), axis=AX.X))
            p.guard("dve")
            for j in range(2):
                if pass_i == 0:
                    p.do("dve", lambda v, j=j, rwo=rwo: v.tensor_scalar(
                        rww[:, rwo + j * E: rwo + (j + 1) * E], g(j, 24),
                        small[:, 36 + 2 * j:37 + 2 * j], None, ALU.mult))
                    p.do("dve", lambda v, j=j: v.tensor_scalar(g(j, 40), g(j, 24), BIG, None, ALU.mult))
                    p.do("dve", lambda v, j=j: v.tensor_sub(g(j, 16), g(j, 16), g(j, 40)))
                else:
                    p.do("dve", lambda v, j=j: v.tensor_scalar(g(j, 40), g(j, 24),
                                                               small[:, 37 + 2 * j:38 + 2 * j],
                                                               None, ALU.mult))
                    p.do("dve", lambda v, j=j, rwo=rwo: v.tensor_add(
                        rww[:, rwo + j * E: rwo + (j + 1) * E],
                        rww[:, rwo + j * E: rwo + (j + 1) * E], g(j, 40)))
            p.guard("dve")

    wslot_free = [p.sems["te"].n] * 6
    ffn_base = [0]

    x_tm_a = x_tm
    def xt_of(k):
        return x_tm_a if k % 2 == 0 else x_tm2

    def xn2_of(k):
        return xn2_fm if k % 2 == 0 else xn2_b

    ws = {}
    wst = {"wout_cons": 0}
    gsub = {}
    wbv16 = watb.bitcast(mybir.dt.bfloat16)

    def wissue(key, t, W_hbm, anchor, k0=False):
        if k0:
            p.wait("act", "dst", p.sems["dst"].n, force=True)
        else:
            p.wait("act", "te", max(anchor, 1), force=True)
        dst = wbv16[:, (t % 2) * 4096:(t % 2) * 4096 + 4096].rearrange("p (c f) -> p c f", c=2)
        src = W_hbm.rearrange("(c p) f -> p c f", p=128)[:, 2 * t:2 * t + 2, :]
        sem = "dwc" if t % 2 == 0 else "dwd"
        ws[(key, t)] = (sem, p.dma("act", dst, src, sem=sem))

    def stages_A(k):
        x_tm = xt_of(k)
        asx = {}
        def s_wd():
            wissue("wa", 0, WAT_HBM, wst["wout_cons"], k0=(k == 0))
            wissue("wa", 1, WAT_HBM, wst["wout_cons"], k0=(k == 0))
        def s_xln():
            p.wait("gps", "dst", p.sems["dst"].n)
            p.wait("gps", "dve", p.sems["dve"].n)
            p.wait("gps", "te", p.sems["te"].n) if k == 0 else None
            ldx = p.dma("gps", x_tm.rearrange("p (s d) -> p s d", s=2),
                        X.rearrange("(s n) d -> n s d", s=2)[k * TOK:(k + 1) * TOK], sem="dx")
            p.wait("dve", "dx", ldx)
            p.wait("act", "dx", ldx)
            asx["xnv"] = ln(xn1_fm, "ln1", k, x_tm)
        def s_xtp():
            asx["ln1tp"] = ln_tp(xn1_fm, asx["xnv"])
        def s_pa():
            proj_and_factor(WAT_HBM, a_re, a_im, "a", "wa", asx["ln1tp"],
                            [("wa", 2, WAT_HBM), ("wa", 3, WAT_HBM),
                             ("wx", 0, WXT_HBM), ("wx", 1, WXT_HBM)])
            if k == 0:
                tapdma("afac", a_re); tapdma("aim", a_im)
        def s_px():
            proj_and_factor(WXT_HBM, xc_re, xc_im, "x", "wx", asx["ln1tp"],
                            [("wx", 2, WXT_HBM), ("wx", 3, WXT_HBM),
                             ("wo", 0, WOUTP), ("wo", 1, WOUTP)])
            if k == 0:
                tapdma("xcre", xc_re); tapdma("xcim", xc_im)
        def s_xr():
            cin = (k % 2) * 2 * NCI
            cout = ((k + 1) % 2) * 2 * NCI
            q4b = lambda ap: ap.rearrange("p (c s u) -> p c s u", c=NCI, s=2)
            p.do("dve", lambda v: v.tensor_copy(q4b(xrh_re)[:, :, :, 1:TOK], q4(xc_re)[:, :, :, 0:TOK - 1]))
            p.do("dve", lambda v: v.tensor_scalar(q4b(xrh_iN)[:, :, :, 1:TOK],
                                                  q4(xc_im)[:, :, :, 0:TOK - 1], -1.0, None, ALU.mult))
            p.do("dve", lambda v, cin=cin: v.tensor_copy(q4b(xrh_re)[:, :, :, 0],
                                                         qcs(carry_re[:, cin:cin + 2 * NCI])))
            p.do("dve", lambda v, cin=cin: v.tensor_copy(q4b(xrh_iN)[:, :, :, 0],
                                                         qcs(carry_iN[:, cin:cin + 2 * NCI])))
            p.guard("dve")
            p.do("dve", lambda v: v.tensor_tensor(q4b(xrl_re)[:, :, :, 1:TOK],
                                                  q4(xc_re)[:, :, :, 0:TOK - 1],
                                                  q4b(xrh_re)[:, :, :, 1:TOK], ALU.subtract))
            p.do("dve", lambda v: v.scalar_tensor_tensor(q4b(xrl_iN)[:, :, :, 1:TOK],
                                                         q4(xc_im)[:, :, :, 0:TOK - 1], -1.0,
                                                         q4b(xrh_iN)[:, :, :, 1:TOK],
                                                         ALU.mult, ALU.subtract))
            p.do("dve", lambda v, cin=cin: v.tensor_tensor(q4b(xrl_re)[:, :, :, 0],
                                                           qcs(carry_re[:, cin:cin + 2 * NCI]),
                                                           q4b(xrh_re)[:, :, :, 0], ALU.subtract))
            p.do("dve", lambda v, cin=cin: v.tensor_tensor(q4b(xrl_iN)[:, :, :, 0],
                                                           qcs(carry_iN[:, cin:cin + 2 * NCI]),
                                                           q4b(xrh_iN)[:, :, :, 0], ALU.subtract))
            p.do("dve", lambda v, cout=cout: v.tensor_copy(qcs(carry_re[:, cout:cout + 2 * NCI]),
                                                           q4(xc_re)[:, :, :, TOK - 1]))
            p.do("dve", lambda v, cout=cout: v.tensor_scalar(qcs(carry_iN[:, cout:cout + 2 * NCI]),
                                                             q4(xc_im)[:, :, :, TOK - 1],
                                                             -1.0, None, ALU.mult))
            xrv = p.bump("dve", "dve")
            p.wait("te", "dve", xrv)

        def s_u():
            zr_pr = (zr_h, zr_l); zi_pr = (zi_h, zi_l)
            xre_pr = (xrh_re, xrl_re); xiN_pr = (xrh_iN, xrl_iN)
            for mc in range(NCI):
                terms_re = mm3([(prtile(zr_pr, kc, mc), xrpair(xre_pr, kc)) for kc in range(NCI)] +
                               [(prtile(zi_pr, kc, mc), xrpair(xiN_pr, kc)) for kc in range(NCI)])
                bre = p.take_banks(1)[0]
                mm_group(bre, T, terms_re)
                uv = p.bump("te", "te")
                p.wait("dve", "te", uv)
                sl = slice(mc * T, (mc + 1) * T)
                p.do("dve", lambda v, b=bre, sl=sl: v.tensor_copy(u_re[:, sl], pb[b][:, 0:T]))
                p.bump("dve", "dve")
                p.release_bank(bre, "dve")
                tP = mm3([(prtile(zi_pr, kc, mc), xrpair(xre_pr, kc)) for kc in range(NCI)])
                tQ = mm3([(prtile(zr_pr, kc, mc), xrpair(xiN_pr, kc)) for kc in range(NCI)])
                psub(tP, tQ, T,
                     lambda v, pP, pQ, sl=sl: v.tensor_tensor(u_im[:, sl], pP[:, 0:T], pQ[:, 0:T],
                                                              ALU.subtract))
            asx["uv"] = p.sems["dve"].n
            if k == 0:
                tapdma("ure", u_re); tapdma("uim", u_im)
        out = [(6, s_wd), (40, s_xln), (48, s_xtp), (65, s_pa), (98, s_px),
               (116, s_xr), (130, s_u)]
        scpts = [140, 145, 150, 155, 160, 165, 170, 175]
        nsl = len(scpts)
        for si in range(nsl):
            ta = (LC * si) // nsl
            tb = (LC * (si + 1)) // nsl
            out.append((scpts[si], lambda ta=ta, tb=tb: scan_inner(ta, tb, asx)))
        out.append((181, lambda: scan_chain(1, NJ // 2 + 1, asx)))
        out.append((186, lambda: scan_chain(NJ // 2 + 1, NJ + 1, asx)))
        out.append((191, lambda: scan_app(asx)))
        return out

    def stages_B(k):
        x_tm = xt_of(k)
        bs = {}
        def s_h():
            p.wait("dve", "gps", wst["scanv"])
            s_reh, s_rel = bpair(a_re)
            s_imh, s_iml = bpair(a_im)
            p.do("dve", lambda v: v.tensor_copy(s_reh, u_re))
            p.do("dve", lambda v: v.tensor_copy(s_imh, u_im))
            p.guard("dve")
            p.do("dve", lambda v: v.tensor_sub(s_rel, u_re, s_reh))
            p.do("dve", lambda v: v.tensor_sub(s_iml, u_im, s_imh))
            spv = p.bump("dve", "dve")
            p.wait("te", "dve", spv)
            vtr_pr = (vtr_h, vtr_l); vti_pr = (vti_h, vti_l)
            sre_pr = (s_reh, s_rel); sim_pr = (s_imh, s_iml)
            for mc in range(NCI):
                sl = slice(mc * T, (mc + 1) * T)
                tP = mm3([(prtile(vtr_pr, kc, mc), xrpair(sre_pr, kc)) for kc in range(NCI)])
                tQ = mm3([(prtile(vti_pr, kc, mc), xrpair(sim_pr, kc)) for kc in range(NCI)])
                def hevac(v, pP, pQ, sl=sl, mc=mc):
                    return v.tensor_tensor(xn_tm[:, mc * T:(mc + 1) * T], pP[:, 0:T], pQ[:, 0:T],
                                           ALU.subtract)
                psub(tP, tQ, T, hevac)
                p.do("dve", lambda v, sl=sl, mc=mc: v.tensor_add(xn_tm[:, mc * T:(mc + 1) * T],
                                                                 xn_tm[:, mc * T:(mc + 1) * T],
                                                                 xc_re[:, sl]))
                p.guard("dve")
                p.do("dve", lambda v, sl=sl, mc=mc: v.tensor_copy(hreh[:, sl], xn_tm[:, mc * T:(mc + 1) * T]))
                p.guard("dve")
                p.do("dve", lambda v, sl=sl, mc=mc: v.tensor_tensor(hrel[:, sl],
                                                                    xn_tm[:, mc * T:(mc + 1) * T],
                                                                    hreh[:, sl], ALU.subtract))
                p.bump("dve", "dve")
            bs["hpv"] = p.sems["dve"].n
            if k == 0:
                tapdma("sre", u_re, "gps"); tapdma("sim", u_im, "gps")
        def s_op():
            p.wait("te", "dve", bs["hpv"])
            p.wait("te", *ws[("wo", 0)])
            p.wait("te", *ws[("wo", 1)])
            wov = watb.bitcast(mybir.dt.bfloat16)
            for j in range(2):
                for n in range(2):
                    b = p.take_banks(1)[0]
                    terms = []
                    for cc in range(NCI):
                        lh = hreh[:, cc * T + j * 128: cc * T + (j + 1) * 128]
                        ll = hrel[:, cc * T + j * 128: cc * T + (j + 1) * 128]
                        rh = wov[:, cc * 2048 + n * 512: cc * 2048 + (n + 1) * 512]
                        rl = wov[:, cc * 2048 + 1024 + n * 512: cc * 2048 + 1024 + (n + 1) * 512]
                        terms += [(lh, rh), (lh, rl), (ll, rh)]
                    mm_group(b, 512, terms)
                    ov = p.bump("te", "te")
                    p.wait("dve", "te", ov)
                    sl = slice(j * D + n * 512, j * D + (n + 1) * 512)
                    p.do("dve", lambda v, b=b, sl=sl: v.tensor_add(x_tm[:, sl], x_tm[:, sl], pb[b][:, 0:512]))
                    p.bump("dve", "dve")
                    p.release_bank(b, "dve")
            wst["wout_cons"] = p.sems["te"].n
            if k == 0:
                tapdma("x1", x_tm)
        def s_l2():
            bs["xn2v"] = ln(xn2_of(k), "ln2", k, x_tm)
        def s_l2tp():
            bs["x2v"] = ln_tp(xn2_of(k), bs["xn2v"])
        def s_g():
            gate_topk(k, xn2_of(k), bs["x2v"])
            bs["rwv"] = p.sems["dve"].n
            if k == 0:
                tapdma("rww", rww[:, 0:2 * E])
        def s_rb():
            p.wait("te", "dve", bs["rwv"])
            rwo = (k % 2) * 2 * E
            for j in range(2):
                b = p.take_banks(1)[0]
                p.do("te", lambda t, b=b, j=j, rwo=rwo: t.transpose(pb[b][0:E, 0:128],
                                                                    rww[:, rwo + j * E: rwo + (j + 1) * E],
                                                                    i128))
                tv = p.bump("te", "te")
                p.wait("dve", "te", tv)
                p.do("dve", lambda v, b=b, j=j: v.tensor_copy(rwf9[0:E, j * TOK:(j + 1) * TOK],
                                                              pb[b][0:E, 0:128]))
                p.bump("dve", "dve")
                p.release_bank(b, "dve")
            p.wait("te", "dve")
            for j in range(2):
                for n in range(2):
                    b = p.take_banks(1)[0]
                    p.do("te", lambda t, b=b, j=j, n=n:
                         t.matmul(pb[b][:, 0:512], rwf9[0:9, j * TOK:(j + 1) * TOK],
                                  b2sb[0:9, n * 512:(n + 1) * 512], start=True, stop=True))
                    tv = p.bump("te", "te")
                    p.wait("dve", "te", tv)
                    sl = slice(j * D + n * 512, j * D + (n + 1) * 512)
                    p.do("dve", lambda v, b=b, sl=sl: v.tensor_add(x_tm[:, sl], x_tm[:, sl],
                                                                   pb[b][:, 0:512]))
                    p.bump("dve", "dve")
                    p.release_bank(b, "dve")


        return [(208, s_h), (228, s_op), (242, s_l2), (250, s_l2tp),
                (260, s_g), (275, s_rb)]

    for _pt, f in stages_A(0):
        f()
    for _pt, f in stages_B(0):
        f()
    pend = {}
    for k in range(NCH):
        x_tm = xt_of(k)
        rwo = (k % 2) * 2 * E
        if k == 0:
            p.wait("sync", "dst", repack_done)
        units = []
        for e in range(9):
            W1s = W1[e] if e < E else SW1
            W3s = W3[e] if e < E else SW3
            W2s = W2[e] if e < E else SW2
            hstate = {}
            for hf in range(NH):
                def dma_fn(slot, sem, e=e, hf=hf):
                    return p.dma("sync", slot.bitcast(mybir.dt.bfloat16), W1P[e, hf], sem=sem)
                def comp_fn(slot, ld, hf=hf, e=e):
                    p.wait("te", ld[0], ld[1])
                    b = p.take_banks(1)[0]
                    sv = slot.bitcast(mybir.dt.bfloat16)
                    x2h, x2l = bpair(xn2_of(k))
                    terms = []
                    for kc in range(ND):
                        wh = sv[:, kc * 128:(kc + 1) * 128]
                        wl = sv[:, 1024 + kc * 128: 1024 + (kc + 1) * 128]
                        xh = x2h[:, kc * T:(kc + 1) * T]
                        xl = x2l[:, kc * T:(kc + 1) * T]
                        terms += [(wh, xh), (wh, xl), (wl, xh)]
                    mm_group(b, T, terms)
                    tv = p.bump("te", "te")
                    p.wait("act", "te", tv)
                    p.do("act", lambda s, b=b, hf=hf, e=e:
                         s.activation(hffn[:, hf * T:(hf + 1) * T], pb[b][:, 0:T], AF.Identity,
                                      bias=b1s[:, e * NH + hf: e * NH + hf + 1]))
                    p.bump("act", "act")
                    p.release_bank(b, "act")
                    return tv
                units.append((dma_fn, comp_fn))
            for hf in range(NH):
                def dma_fn(slot, sem, e=e, hf=hf):
                    return p.dma("sync", slot.bitcast(mybir.dt.bfloat16), W3P[e, hf], sem=sem)
                def comp_fn(slot, ld, hf=hf, e=e, hstate=hstate):
                    p.wait("te", ld[0], ld[1])
                    b = p.take_banks(1)[0]
                    sv = slot.bitcast(mybir.dt.bfloat16)
                    x2h, x2l = bpair(xn2_of(k))
                    terms = []
                    for kc in range(ND):
                        wh = sv[:, kc * 128:(kc + 1) * 128]
                        wl = sv[:, 1024 + kc * 128: 1024 + (kc + 1) * 128]
                        xh = x2h[:, kc * T:(kc + 1) * T]
                        xl = x2l[:, kc * T:(kc + 1) * T]
                        terms += [(wh, xh), (wh, xl), (wl, xh)]
                    mm_group(b, T, terms)
                    tv = p.bump("te", "te")
                    p.wait("dve", "te", tv)
                    p.wait("dve", "act", p.sems["act"].n)
                    hs = hmul[:, (hf % 4) * T:((hf % 4) + 1) * T]
                    p.do("dve", lambda v, b=b, hf=hf, e=e, hs=hs:
                         v.scalar_tensor_tensor(hs, pb[b][:, 0:T],
                                                b3s[:, e * NH + hf: e * NH + hf + 1],
                                                hffn[:, hf * T:(hf + 1) * T], ALU.add, ALU.mult))
                    mv = p.bump("dve", "dve")
                    p.release_bank(b, "dve")
                    hh, hl = hpair(hf)
                    p.wait("act", "dve", mv)
                    p.do("act", lambda s, hs=hs: s.activation(hs, hs, AF.Silu))
                    p.guard("act")
                    p.do("act", lambda s, hh=hh, hs=hs: s.copy(hh, hs))
                    cv = p.bump("act", "act")
                    p.wait("dve", "act", cv)
                    p.do("dve", lambda v, hh=hh, hl=hl, hs=hs: v.tensor_sub(hl, hs, hh))
                    hstate[hf] = p.bump("dve", "dve")
                    return tv
                units.append((dma_fn, comp_fn))
            ystate = {}
            for hf in range(NH):
                def dma_fn(slot, sem, e=e, hf=hf):
                    return p.dma("sync", slot.bitcast(mybir.dt.bfloat16), W2P[e, hf], sem=sem)
                def comp_fn(slot, ld, hf=hf, e=e, ystate=ystate, hstate=hstate):
                    hh, hl = hpair(hf)
                    if hf == 0:
                        ystate["banks"] = p.take_banks(4)
                        p.hold_banks(ystate["banks"])
                        p.wait("te", "dve", hstate[NH - 1])
                    p.wait("te", ld[0], ld[1])
                    sv = slot.bitcast(mybir.dt.bfloat16)
                    for j in range(2):
                        for n in range(2):
                            lh = hh[:, j * 128:(j + 1) * 128]
                            ll = hl[:, j * 128:(j + 1) * 128]
                            rh = sv[:, n * 512:(n + 1) * 512]
                            rl = sv[:, 1024 + n * 512: 1024 + (n + 1) * 512]
                            for ti, (lt, rr) in enumerate(((lh, rh), (lh, rl), (ll, rh))):
                                p.do("te", lambda t, b=ystate["banks"][2 * j + n], lt=lt, rr=rr,
                                     st=(hf == 0 and ti == 0), sp=(hf == NH - 1 and ti == 2):
                                     t.matmul(pb[b][:, 0:512], lt, rr, start=st, stop=sp))
                    tv = p.bump("te", "te")
                    if hf == NH - 1:
                        p.wait("dve", "te", tv)
                        for j in range(2):
                            for n in range(2):
                                b = ystate["banks"][2 * j + n]
                                sl = slice(j * D + n * 512, j * D + (n + 1) * 512)
                                sc = rww[:, rwo + j * E + e: rwo + j * E + e + 1] if e < E else 1.0
                                p.do("dve", lambda v, b=b, sl=sl, sc=sc, x_tm=x_tm:
                                     v.scalar_tensor_tensor(x_tm[:, sl], pb[b][:, 0:512], sc,
                                                            x_tm[:, sl], ALU.mult, ALU.add))
                                p.bump("dve", "dve")
                                p.release_bank(b, "dve")
                        p.unhold_banks(ystate["banks"])
                    return tv
                units.append((dma_fn, comp_fn))
        G = 4
        DEPTH = 6
        anchor = p.sems["te"].n
        for _i in range(DEPTH):
            wslot_free[_i] = max(wslot_free[_i], anchor)
        NU = len(units)
        if k + 1 < NCH:
            pend = {}
            for pt, f in stages_A(k + 1) + stages_B(k + 1):
                pend.setdefault(pt, []).append(f)
        else:
            pend = {}
        ngrp = (NU + G - 1) // G
        gsem = ["dwa" if m % 2 == 0 else "dwb" for m in range(ngrp)]
        gend = {}

        def slot_ap(i):
            return wbuf[:, (i % DEPTH) * 1024:(i % DEPTH) * 1024 + 1024]

        def issue(i):
            p.wait("sync", "te", wslot_free[i % DEPTH], force=True)
            m = i // G
            v = units[i][0](slot_ap(i), gsem[m])
            if i == NU - 1:
                dn = 0
                while (i + 1 + dn) % G != 0:
                    p.dma("sync", dumm[0:1, dn * 8:dn * 8 + 8], ONES[0:1, 0:8], sem=gsem[m])
                    v = p.sems[gsem[m]].n
                    dn += 1
                gend[m] = v
            elif i % G == G - 1:
                gend[m] = v

        for i in range(min(DEPTH, NU)):
            issue(i)
        for i in range(NU):
            m = i // G
            wslot_free[i % DEPTH] = units[i][1](slot_ap(i), (gsem[m], gend[m]))
            if i in pend:
                for f in pend.pop(i):
                    f()
            ip = i + DEPTH
            if ip < NU:
                issue(ip)

        if k == 0:
            tapdma("xout", x_tm)
        p.wait("gps", "dve", p.sems["dve"].n)
        p.dma("gps", OUT.rearrange("(s n) d -> n s d", s=2)[k * TOK:(k + 1) * TOK],
              x_tm.rearrange("p (s d) -> p s d", s=2), sem="dst")

    p.wait("gps", "dst", p.sems["dst"].n)

    import contextlib
    with contextlib.ExitStack() as st:
        for s in p.sems.values():
            s.h = st.enter_context(nc.semaphore("s_" + s.name))
        blk = st.enter_context(nc.Block())
        blk.tensor(lambda e: [f(e) for f in p.ops["te"]])
        blk.vector(lambda e: [f(e) for f in p.ops["dve"]])
        blk.scalar(lambda e: [f(e) for f in p.ops["act"]])
        blk.gpsimd(lambda e: [f(e) for f in p.ops["gps"]])
        blk.sync(lambda e: [f(e) for f in p.ops["sync"]])
    ninstr = {k: len(v) for k, v in p.ops.items()}
    return nc, ninstr


def host_inputs(inp, core, n_cores=8, rows_per_core=4096):
    seqs_per_core = 16 // n_cores
    x = np.ascontiguousarray(
        inp["L0_attn_x"].reshape(16, 2048, 1024)[core * seqs_per_core:(core + 1) * seqs_per_core]
    ).reshape(-1, D)[:rows_per_core]
    m = {
        "x": x,
        "w_a": inp["L0_attn_w_a"], "w_x": inp["L0_attn_w_x"], "w_out": inp["L0_attn_w_out"],
        "mat_v_real": inp["L0_attn_mat_v_real"], "mat_v_imag": inp["L0_attn_mat_v_imag"],
        "h0_re": inp["L0_attn_last_hidden_real"], "h0_im": inp["L0_attn_last_hidden_imag"],
        "gate_w": inp["L0_ffn_gate_w"],
        "gate_bb": np.ascontiguousarray(np.broadcast_to(inp["L0_ffn_gate_b"], (128, E))),
        "w1": inp["L0_ffn_w1"], "b1": inp["L0_ffn_b1"],
        "w3": inp["L0_ffn_w3"], "b3": inp["L0_ffn_b3"],
        "w2": inp["L0_ffn_w2"], "b2": inp["L0_ffn_b2"],
        "sw1": inp["L0_ffn_sw1"], "sb1": inp["L0_ffn_sb1"],
        "sw3": inp["L0_ffn_sw3"], "sb3": inp["L0_ffn_sb3"],
        "sw2": inp["L0_ffn_sw2"], "sb2": inp["L0_ffn_sb2"],
        "eye512": np.eye(INNER, dtype=np.float32),
        "i128": np.eye(128, dtype=np.float32),
        "iota8": np.ascontiguousarray(np.broadcast_to(np.arange(E, dtype=np.float32), (128, E))),
        "ones": np.ones((1, 128), dtype=np.float32),
    }
    return {k: np.ascontiguousarray(np.asarray(v, dtype=np.float32)) for k, v in m.items()}


from concourse.bass_utils import run_bass_kernel_spmd

_NC_CACHE = {}

def kernel(**inputs):
    n_cores = 8
    rows = 4096
    if "nc" not in _NC_CACHE:
        _NC_CACHE["nc"], _ = build(rows_per_core=rows, n_chunks=None, tap_names=(),
                                   newton_iters=4)
    nc = _NC_CACHE["nc"]
    in_maps = [host_inputs(inputs, core=c, n_cores=n_cores, rows_per_core=rows)
               for c in range(n_cores)]
    res = run_bass_kernel_spmd(nc, in_maps, core_ids=list(range(n_cores)))
    out = np.stack([res.results[c]["out"].reshape(2, 2048, D) for c in range(n_cores)])
    return out.reshape(16, 2048, D).astype(np.float32)


# version 20
# speedup vs baseline: 1.1632x; 1.1028x over previous
import numpy as np
import concourse.bass as bass
import concourse.mybir as mybir

F32 = mybir.dt.float32
AF = mybir.ActivationFunctionType
ALU = mybir.AluOpType
AX = mybir.AxisListType

D = 1024
INNER = 512
HID = 1408
E = 8
TOK = 128
T = 2 * TOK
ND = D // 128
NCI = INNER // 128
NH = HID // 128
LC = 16
NJ = TOK // LC
EPS = 1e-5
BIG = 1e9
ENGS = ("te", "dve", "act", "gps", "sync")
SEM_NAMES = ("te", "dve", "act", "gps", "dw", "dx", "dst", "dwa", "dwb", "dwc", "dwd")


class PSem:
    def __init__(self, name):
        self.name = name
        self.h = None
        self.n = 0


class Planner:
    def __init__(self):
        self.ops = {e: [] for e in ENGS}
        self.sems = {k: PSem(k) for k in SEM_NAMES}
        self.waited = {e: {} for e in ENGS}
        self.bank_free = [("te", 0)] * 8
        self.bank_rr = 0
        self.dirty = {e: False for e in ENGS}

    AUTOB = ("dve", "act", "gps")

    def do(self, eng, f, _raw=False):
        if _raw or eng not in self.AUTOB:
            self.ops[eng].append(f)
            return
        if self.dirty[eng]:
            self.dirty[eng] = False
            self.wait(eng, eng)
        s = self.sems[eng]
        s.n += 1
        def g(e, f=f, s=s):
            ins = f(e)
            ins.then_inc(s.h, 1)
            return ins
        self.ops[eng].append(g)
        self.dirty[eng] = True

    def bump(self, eng, sem_name, amt=1):
        if eng in self.AUTOB and sem_name == eng:
            return self.sems[sem_name].n
        s = self.sems[sem_name]
        s.n += amt
        last = self.ops[eng][-1]
        def g(e, last=last, s=s, amt=amt):
            ins = last(e)
            ins.then_inc(s.h, amt)
            return ins
        self.ops[eng][-1] = g
        return s.n

    def wait(self, eng, sem_name, val=None, force=False):
        s = self.sems[sem_name]
        val = s.n if val is None else val
        if val <= 0:
            return
        if not force and self.waited[eng].get(sem_name, -1) >= val:
            return
        assert val <= s.n, f"{eng} waits {sem_name}>={val} > planned {s.n}"
        self.waited[eng][sem_name] = val
        self.do(eng, lambda e, s=s, val=val: e.wait_ge(s.h, val), _raw=True)

    def dma(self, eng, out, in_, sem="dw"):
        if eng == "gps":
            self.wait(eng, sem, self.sems[sem].n)
        self.do(eng, lambda e, out=out, in_=in_: e.dma_start(out=out, in_=in_), _raw=True)
        return self.bump(eng, sem, 16)

    def take_banks(self, n):
        out = []
        tries = 0
        while len(out) < n:
            b = self.bank_rr % 8
            self.bank_rr += 1
            tries += 1
            assert tries < 64, "bank allocator livelock (too many held banks)"
            if b in getattr(self, "held", set()) or b in out:
                continue
            sname, v = self.bank_free[b]
            self.wait("te", sname, v)
            out.append(b)
        return out

    def hold_banks(self, banks):
        if not hasattr(self, "held"):
            self.held = set()
        self.held |= set(banks)

    def unhold_banks(self, banks):
        self.held -= set(banks)

    def release_bank(self, b, sem_name):
        self.bank_free[b] = (sem_name, self.sems[sem_name].n)

    def guard(self, eng):
        self.bump(eng, eng)
        self.wait(eng, eng)


def build(rows_per_core=4096, n_chunks=None, tap_names=(), newton_iters=4):
    p = Planner()
    nc = bass.Bass("TRN2", target_bir_lowering=False, debug=False)
    NCH = (rows_per_core // T) if n_chunks is None else n_chunks
    SEQ = rows_per_core // 2

    def din(name, shape):
        return nc.dram_tensor(name, shape, F32, kind="ExternalInput").ap()
    X = din("x", [rows_per_core, D])
    WA = din("w_a", [2 * INNER, D]); WXm = din("w_x", [2 * INNER, D])
    WOUT = din("w_out", [D, INNER])
    MVR = din("mat_v_real", [INNER, INNER]); MVI = din("mat_v_imag", [INNER, INNER])
    H0R = din("h0_re", [INNER]); H0I = din("h0_im", [INNER])
    GW = din("gate_w", [E, D]); GBB = din("gate_bb", [128, E])
    W1 = din("w1", [E, D, HID]); B1 = din("b1", [E, HID])
    W3 = din("w3", [E, D, HID]); B3 = din("b3", [E, HID])
    W2 = din("w2", [E, HID, D]); B2 = din("b2", [E, D])
    SW1 = din("sw1", [D, HID]); SB1 = din("sb1", [HID])
    SW3 = din("sw3", [D, HID]); SB3 = din("sb3", [HID])
    SW2 = din("sw2", [HID, D]); SB2 = din("sb2", [D])
    EYE = din("eye512", [INNER, INNER])
    I128D = din("i128", [128, 128])
    IOTA8 = din("iota8", [128, E])
    ONES = din("ones", [1, 128])
    OUT = nc.dram_tensor("out", [rows_per_core, D], F32, kind="ExternalOutput").ap()
    BF = mybir.dt.bfloat16
    WAT_HBM = nc.dram_tensor("waT_hbm", [D, 4 * INNER], BF, kind="Internal").ap()
    WXT_HBM = nc.dram_tensor("wxT_hbm", [D, 4 * INNER], BF, kind="Internal").ap()
    W1P = nc.dram_tensor("w1p_hbm", [9, NH, 128, 2 * D], BF, kind="Internal").ap()
    W3P = nc.dram_tensor("w3p_hbm", [9, NH, 128, 2 * D], BF, kind="Internal").ap()
    W2P = nc.dram_tensor("w2p_hbm", [9, NH, 128, 2 * D], BF, kind="Internal").ap()
    WOUTP = nc.dram_tensor("woutp_hbm", [INNER, 2 * D], BF, kind="Internal").ap()

    TAPCOLS = {"Zr": NCI * INNER, "Zi": NCI * INNER, "ln1": 2 * D, "afac": NCI * T,
               "aim": NCI * T, "xcre": NCI * T, "xcim": NCI * T, "ure": NCI * T,
               "uim": NCI * T, "sre": NCI * T, "sim": NCI * T, "hre": NCI * T,
               "x1": 2 * D, "ln2": 2 * D, "rww": 2 * E, "hffn0": NH * T, "xout": 2 * D}
    taps = {}
    for name in tap_names:
        taps[name] = nc.dram_tensor("tap_" + name, [128, TAPCOLS[name]], F32,
                                    kind="ExternalOutput").ap()

    def tapdma(name, src, eng_dep="dve"):
        if name not in taps:
            return
        p.wait("gps", eng_dep, p.sems[eng_dep].n)
        st = p.dma("gps", taps[name], src, sem="dst")
        p.wait("dve", "dst", st)
        p.wait("act", "dst", st)
        p.wait("te", "dst", st)
        p.wait("gps", "dst", st)

    def S(name, cols, dtype=F32):
        return nc.alloc_sbuf_tensor(name, [128, cols], dtype).ap()

    i128 = S("c_i128", 128)
    iota8 = S("c_iota8", E)
    gbb = S("c_gbb", E)
    ones = S("c_ones", 128)
    h0 = S("c_h0", 2 * NCI)
    gwT = S("r_gwT", ND * E)
    gwTp = S("r_gwTp", ND * E)
    b1s = S("r_b1", 9 * NH); b3s = S("r_b3", 9 * NH)
    b2sb = S("r_b2sb", D)
    rwf9 = S("r_rwf9", 2 * TOK)
    woutT = S("r_woutT", NCI * D)
    Zr = S("r_Zr", NCI * INNER); Zi = S("r_Zi", NCI * INNER); ZrN = S("r_ZrN", NCI * INNER)
    vT_r = S("r_vTr", NCI * INNER); vT_iN = S("r_vTiN", NCI * INNER)

    pool = S("pool", 11264)
    watb = S("watb", 4096)
    wbuf = S("wbuf", NH * D)
    x_tm = S("x_tm", 2 * D)
    xn_tm = S("xn_tm", 2 * D)
    scr = S("scr", 2 * D)
    scr2 = S("scr2", D)
    tmps = [S(f"tmp{i}", 128) for i in range(6)]
    gtmps = [S(f"gtmp{i}", 128) for i in range(4)]
    small = S("small", 64)
    carry_re = S("carry_re", 2 * 2 * NCI); carry_iN = S("carry_iN", 2 * 2 * NCI)
    scb_re = S("scb_re", 2 * NCI); scb_im = S("scb_im", 2 * NCI)
    scar_r = S("scar_r", NCI * 2 * NJ); scar_i = S("scar_i", NCI * 2 * NJ)
    gate = S("gate", 2 * 48)
    dumm = S("dumm", 64)
    rww = S("rww", 2 * 2 * E)

    def Rg(start, cols, t=pool):
        return t[:, start:start + cols]
    xn1_fm = Rg(0, ND * T)
    xr_re = Rg(0, NCI * T); xr_iN = Rg(NCI * T, NCI * T)
    a_re = scr[:, 0:NCI * T]; a_im = scr[:, NCI * T:2 * NCI * T]
    xn2_fm = Rg(2048, ND * T)
    xc_re = Rg(4096, NCI * T); xc_im = Rg(4096 + NCI * T, NCI * T)
    u_re = Rg(6144, NCI * T); u_im = Rg(6144 + NCI * T, NCI * T)
    P_re = Rg(8192, NCI * T); P_im = Rg(8192 + NCI * T, NCI * T)
    h_re = Rg(10240, NCI * T)
    hffn = woutT[:, 0:NH * T]
    hmul = S("hmul4", 4 * T)
    xn2_b = wbuf[:, 6144:8192]
    x_tm2 = wbuf[:, 8960:11008]
    v_r = Rg(0, 2048); v_iN = Rg(2048, 2048); vT_i = Rg(4096, 2048)
    Xr = Rg(6144, 2048); Xi = Rg(8192, 2048)
    Rr = x_tm[:, 0:2048]; Ri = xn_tm[:, 0:2048]; RiN = Rg(0, 2048, scr)
    eye = Rg(0, 2048, watb)
    BF16 = mybir.dt.bfloat16
    def bview(reg, half):
        return reg.bitcast(BF16)[:, half * 2048:(half + 1) * 2048]
    Xbr, Xbi = bview(Xr, 0), bview(Xr, 1)
    Zbr, Zbi = bview(Xi, 0), bview(Xi, 1)
    Rbr, Rbi = bview(Rr, 0), bview(Rr, 1)
    RbiN = bview(Ri, 0)
    vTbr, vTbi = bview(v_r, 0), bview(v_r, 1)
    vTbiN = bview(v_iN, 0)
    Xr_p = v_iN.bitcast(F32) if False else None
    wstage = watb

    pb = [nc.alloc_psum_tensor(f"pb{i}", [128, 512], F32).ap() for i in range(8)]

    def bpair(reg):
        n = reg.shape[1]
        bc = reg.bitcast(mybir.dt.bfloat16)
        return bc[:, 0:n], bc[:, n:2 * n]

    def hpair(hf):
        bc = hffn.bitcast(mybir.dt.bfloat16)
        return bc[:, hf * 2 * T: hf * 2 * T + T], bc[:, hf * 2 * T + T: hf * 2 * T + 2 * T]

    def mm3(terms):
        out = []
        for (lp, rp) in terms:
            lh, ll = lp
            rh, rl = rp
            out += [(lh, rh), (lh, rl), (ll, rh)]
        return out

    def psub(lhs_terms_p, lhs_terms_q, cols, evac):
        bP, bQ = p.take_banks(2)
        mm_group(bP, cols, lhs_terms_p)
        mm_group(bQ, cols, lhs_terms_q)
        tv = p.bump("te", "te")
        p.wait("dve", "te", tv)
        p.do("dve", lambda v, bP=bP, cols=cols: v.tensor_copy(scr2[:, 0:cols], pb[bP][:, 0:cols]))
        p.do("dve", lambda v, bQ=bQ, cols=cols: evac(v, scr2[:, 0:cols], pb[bQ]))
        p.bump("dve", "dve")
        p.release_bank(bP, "dve"); p.release_bank(bQ, "dve")

    xrh_re, xrl_re = bpair(xr_re)
    xrh_iN, xrl_iN = bpair(xr_iN)

    def prtile(pr, kc, mc):
        h, l = pr
        return (mtile(h, kc, mc), mtile(l, kc, mc))

    def xrpair(pr, kc):
        h, l = pr
        return (h[:, kc * T:(kc + 1) * T], l[:, kc * T:(kc + 1) * T])

    hreh, hrel = bpair(h_re)

    def mtile(msb, kc, mc):
        return msb[:, kc * INNER + mc * 128: kc * INNER + (mc + 1) * 128]

    def q5(ap):
        return ap.rearrange("p (c s j u) -> p c s j u", c=NCI, s=2, j=NJ)

    def q4(ap):
        return ap.rearrange("p (c s u) -> p c s u", c=NCI, s=2)

    def qcs(ap):
        return ap.rearrange("p (c s) -> p c s", c=NCI)

    p.dma("sync", i128, I128D)
    p.dma("sync", iota8, IOTA8)
    p.dma("sync", gbb, GBB)
    p.dma("sync", ones[0:1, :], ONES)
    p.dma("sync", h0[:, 0:NCI] if False else scr2[0:NCI, 0:128],
          H0R.rearrange("(c p) -> c p", p=128))
    p.dma("sync", scr2[32:32 + NCI, 0:128], H0I.rearrange("(c p) -> c p", p=128))
    p.dma("sync", b2sb[0:E, :], B2)
    p.dma("sync", b2sb[E:E + 1, :], SB2.unsqueeze(0))
    p.dma("sync", rwf9[E:E + 1, 0:TOK], ONES)
    p.dma("sync", rwf9[E:E + 1, TOK:2 * TOK], ONES)
    p.dma("sync", v_r.rearrange("p (c j) -> p c j", c=NCI),
          MVR.rearrange("(c p) j -> p c j", p=128))
    p.dma("sync", v_iN.rearrange("p (c j) -> p c j", c=NCI),
          MVI.rearrange("(c p) j -> p c j", p=128))
    p.dma("sync", scr[0:E, 0:D], GW)
    ld0 = p.dma("sync", wstage.rearrange("p (o j) -> p o j", o=ND),
                WOUT.rearrange("(o p) j -> p o j", p=128))

    p.wait("dve", "dw", ld0)
    p.do("dve", lambda v: v.memset(scb_re, 0.0))
    p.do("dve", lambda v: v.memset(scb_im, 0.0))
    p.bump("dve", "dve")

    def transp(src_ap, *evacs):
        kp, m = src_ap.shape[0], src_ap.shape[1]
        bp = src_ap.base_partition()
        b = p.take_banks(1)[0]
        p.do("te", lambda t, s=src_ap, b=b, kp=kp, m=m, bp=bp:
             t.transpose(pb[b][0:m, 0:kp], s, i128[bp:bp + kp, bp:bp + kp]))
        tv = p.bump("te", "te")
        p.wait("dve", "te", tv)
        for ev in evacs:
            p.do("dve", lambda v, b=b, ev=ev: ev(v, pb[b]))
        p.bump("dve", "dve")
        p.release_bank(b, "dve")

    p.wait("te", "dw", ld0)
    for (src, dst) in ((v_r, vT_r), (v_iN, vT_i)):
        for cj in range(NCI):
            for ci in range(NCI):
                transp(src[:, ci * INNER + cj * 128: ci * INNER + (cj + 1) * 128],
                       lambda v, ps, dst=dst, ci=ci, cj=cj:
                       v.tensor_copy(dst[:, cj * INNER + ci * 128: cj * INNER + (ci + 1) * 128],
                                     ps[:, 0:128]))
    p.do("dve", lambda v: v.tensor_scalar_mul(vT_iN, vT_i, -1.0))
    p.do("dve", lambda v: v.tensor_scalar_mul(v_iN, v_iN, -1.0))
    p.bump("dve", "dve")

    for c in range(NCI):
        for o in range(ND):
            transp(wstage[:, o * INNER + c * 128: o * INNER + (c + 1) * 128],
                   lambda v, ps, c=c, o=o:
                   v.tensor_copy(woutT[:, c * D + o * 128: c * D + (o + 1) * 128], ps[:, 0:128]))
    for c in range(ND):
        transp(scr[0:E, c * 128:(c + 1) * 128],
               lambda v, ps, c=c: v.tensor_copy(gwT[:, c * E:(c + 1) * E], ps[:, 0:E]))
    transp(scr2[0:NCI, 0:128],
           lambda v, ps: v.tensor_copy(h0[:, 0:NCI], ps[:, 0:NCI]))
    transp(scr2[32:32 + NCI, 0:128],
           lambda v, ps: v.tensor_copy(h0[:, NCI:2 * NCI], ps[:, 0:NCI]))
    p.do("dve", lambda v: v.tensor_copy(
        qcs(carry_re[:, 0:2 * NCI]), h0[:, 0:NCI].unsqueeze(2).to_broadcast([128, NCI, 2])))
    p.do("dve", lambda v: v.tensor_scalar(
        qcs(carry_iN[:, 0:2 * NCI]), h0[:, NCI:2 * NCI].unsqueeze(2).to_broadcast([128, NCI, 2]),
        -1.0, None, ALU.mult))
    p.bump("dve", "dve")
    for (BB, SBB, dst) in ((B1, SB1, b1s), (B3, SB3, b3s)):
        p.wait("sync", "dve", p.sems["dve"].n)
        p.dma("sync", scr[0:E, 0:HID], BB)
        ldb = p.dma("sync", scr[E:E + 1, 0:HID], SBB.unsqueeze(0))
        p.wait("te", "dw", ldb)
        for hc in range(NH):
            transp(scr[0:9, hc * 128:(hc + 1) * 128],
                   lambda v, ps, hc=hc, dst=dst: v.tensor_copy(
                       dst.rearrange("p (e h) -> p e h", e=9)[:, :, hc], ps[:, 0:9]))

    for (W_in, W_hbm) in ((WA, WAT_HBM), (WXm, WXT_HBM)):
        p.wait("sync", "te", p.sems["te"].n)
        p.wait("sync", "dve", p.sems["dve"].n)
        ld = p.dma("sync", wbuf[:, 0:ND * D].rearrange("p (o d) -> p o d", o=ND),
                   W_in.rearrange("(o p) d -> p o d", p=128))
        p.wait("te", "dw", ld)
        scb = scr.bitcast(mybir.dt.bfloat16)
        for dc in range(ND):
            for o in range(ND):
                b = p.take_banks(1)[0]
                p.do("te", lambda t, o=o, dc=dc, b=b:
                     t.transpose(pb[b][:, 0:128],
                                 wbuf[:, o * D + dc * 128: o * D + (dc + 1) * 128], i128))
                tv = p.bump("te", "te")
                p.wait("dve", "te", tv)
                p.do("dve", lambda v, b=b, o=o:
                     v.tensor_copy(scb[:, o * 64: o * 64 + 64], pb[b][:, 0:128:2]))
                p.do("dve", lambda v, b=b, o=o:
                     v.tensor_copy(scb[:, 512 + o * 64: 512 + o * 64 + 64], pb[b][:, 1:128:2]))
                p.guard("dve")
                p.do("dve", lambda v, b=b, o=o:
                     v.tensor_sub(scb[:, 1024 + o * 64: 1024 + o * 64 + 64], pb[b][:, 0:128:2],
                                  scb[:, o * 64: o * 64 + 64]))
                p.do("dve", lambda v, b=b, o=o:
                     v.tensor_sub(scb[:, 1536 + o * 64: 1536 + o * 64 + 64], pb[b][:, 1:128:2],
                                  scb[:, 512 + o * 64: 512 + o * 64 + 64]))
                p.bump("dve", "dve")
                p.release_bank(b, "dve")
            p.wait("gps", "dve", p.sems["dve"].n)
            st = p.dma("gps", W_hbm[dc * 128:(dc + 1) * 128, :], scb[:, 0:2048], sem="dst")
            p.wait("dve", "dst", st)
    watw_done = p.sems["dst"].n

    def repack(Wsrcs, Wdst, colblocks):
        p.wait("sync", "dw", p.sems["dw"].n)
        stg = watb[:, 2048:4096].bitcast(mybir.dt.bfloat16)
        cdim = ND if colblocks else NH
        for e in range(9):
            Ws = Wsrcs[0][e] if e < E else Wsrcs[1]
            p.wait("sync", "dst", p.sems["dst"].n)
            p.wait("sync", "gps", p.sems["gps"].n)
            ldw = p.dma("sync", wbuf.rearrange("p (c h) -> p c h", c=cdim),
                        Ws.rearrange("(c p) h -> p c h", p=128))
            p.wait("gps", "dw", ldw)
            for hf in range(NH):
                if colblocks:
                    blk = wbuf.rearrange("p (c h) -> p c h", c=ND)[:, :, hf * 128:(hf + 1) * 128]
                    sl = stg[:, 0:2048].rearrange("p (x c q) -> p x c q", x=2, c=ND)
                    hi, lo = sl[:, 0], sl[:, 1]
                else:
                    blk = wbuf[:, hf * D:(hf + 1) * D]
                    hi, lo = stg[:, 0:1024], stg[:, 1024:2048]
                p.wait("gps", "dst", p.sems["dst"].n)
                p.do("gps", lambda g, blk=blk, hi=hi: g.tensor_copy(hi, blk))
                p.guard("gps")
                p.do("gps", lambda g, blk=blk, hi=hi, lo=lo: g.tensor_tensor(lo, blk, hi, ALU.subtract))
                p.bump("gps", "gps")
                p.wait("gps", "gps", p.sems["gps"].n)
                p.dma("gps", Wdst[e, hf], stg[:, 0:2048] if not colblocks else None, sem="dst") \
                    if False else None
                p.dma("gps", Wdst[e, hf], stg[:, 0:2048], sem="dst")

    p.wait("sync", "dst", watw_done)
    ld_eye = p.dma("sync", eye.rearrange("p (c j) -> p c j", c=NCI),
                   EYE.rearrange("(c p) j -> p c j", p=128))
    repack((W1, SW1), W1P, True)
    repack((W3, SW3), W3P, True)
    repack((W2, SW2), W2P, False)
    repack_done = p.sems["dst"].n
    p.wait("dve", "dw", ld_eye)
    p.do("dve", lambda v: v.tensor_mul(scr[:, 0:2048], v_r, v_r))
    p.do("dve", lambda v: v.tensor_mul(scr2, v_iN[:, 0:D], v_iN[:, 0:D]))
    p.do("dve", lambda v: v.tensor_add(scr[:, 0:D], scr[:, 0:D], scr2))
    p.do("dve", lambda v: v.tensor_mul(scr2, v_iN[:, D:2 * D], v_iN[:, D:2 * D]))
    p.do("dve", lambda v: v.tensor_add(scr[:, D:2 * D], scr[:, D:2 * D], scr2))
    p.do("dve", lambda v: v.reduce_sum(small[:, 0:1], scr[:, 0:2048], axis=AX.X))
    p.do("dve", lambda v: v.memset(scr2[:, 0:1], 1.0))
    rsum_done = p.bump("dve", "dve")
    b = p.take_banks(1)[0]
    p.wait("te", "dve", rsum_done)
    p.do("te", lambda t, b=b: t.matmul(pb[b][0:1, 0:1], small[:, 0:1], scr2[:, 0:1],
                                       start=True, stop=True))
    tv = p.bump("te", "te")
    p.wait("dve", "te", tv)
    p.do("dve", lambda v, b=b: v.reciprocal(small[0:1, 2:3], pb[b][0:1, 0:1]))
    p.bump("dve", "dve")
    p.release_bank(b, "dve")
    b = p.take_banks(1)[0]
    p.wait("te", "dve")
    p.do("te", lambda t, b=b: t.matmul(pb[b][:, 0:1], ones[0:1, 0:128], small[0:1, 2:3],
                                       start=True, stop=True))
    tv = p.bump("te", "te")
    p.wait("dve", "te", tv)
    p.do("dve", lambda v, b=b: v.tensor_copy(small[:, 4:5], pb[b][:, 0:1]))
    p.bump("dve", "dve")
    p.release_bank(b, "dve")
    p.guard("dve")
    p.do("dve", lambda v: v.tensor_scalar(Xbr, vT_r, small[:, 4:5], None, ALU.mult))
    p.do("dve", lambda v: v.tensor_scalar(Xbi, vT_iN, small[:, 4:5], None, ALU.mult))
    p.do("dve", lambda v: v.tensor_scalar(Zbr, v_r, small[:, 4:5], None, ALU.mult))
    p.do("dve", lambda v: v.tensor_scalar(Zbi, v_iN, small[:, 4:5], None, ALU.mult))
    p.do("dve", lambda v: v.tensor_copy(vTbr, vT_r))
    p.do("dve", lambda v: v.tensor_copy(vTbi, vT_i))
    p.do("dve", lambda v: v.tensor_copy(vTbiN, vT_iN))
    nv = p.bump("dve", "dve")
    p.wait("te", "dve", nv)

    def mm_group(bank, out_cols, terms):
        n = len(terms)
        for i, (lt, rh) in enumerate(terms):
            p.do("te", lambda t, b=bank, lt=lt, rh=rh, i=i, n=n, oc=out_cols:
                 t.matmul(pb[b][:, 0:oc], lt, rh, start=(i == 0), stop=(i == n - 1)))

    def m512_terms(pairs, mc):
        out = []
        for (lsb, rsb) in pairs:
            for kc in range(NCI):
                out.append((mtile(lsb, kc, mc), rsb[:, kc * INNER:(kc + 1) * INNER]))
        return out

    def newton_iter(xr, xi, zr, zi, rr, ri, riN, vtr, vti, vtiN, cast):
        banks = p.take_banks(8)
        for mc in range(NCI):
            mm_group(banks[mc], INNER, m512_terms(((vtr, xr), (vtiN, xi)), mc))
            mm_group(banks[4 + mc], INNER, m512_terms(((vti, xr), (vtr, xi)), mc))
        t1v = p.bump("te", "te")
        p.wait("dve", "te", t1v)
        for mc in range(NCI):
            sl = slice(mc * INNER, (mc + 1) * INNER)
            p.do("dve", lambda v, b=banks[mc], sl=sl, rr=rr: v.tensor_sub(rr[:, sl], eye[:, sl], pb[b][:, 0:INNER]))
            p.do("dve", lambda v, b=banks[4 + mc], sl=sl, ri=ri: v.tensor_scalar_mul(ri[:, sl], pb[b][:, 0:INNER], -1.0))
            p.do("dve", lambda v, b=banks[4 + mc], sl=sl, riN=riN: v.tensor_copy(riN[:, sl], pb[b][:, 0:INNER]))
            p.bump("dve", "dve")
            p.release_bank(banks[mc], "dve"); p.release_bank(banks[4 + mc], "dve")
        p.wait("te", "dve")
        banks = p.take_banks(8)
        for mc in range(NCI):
            mm_group(banks[mc], INNER, m512_terms(((zr, rr), (zi, riN)), mc))
            mm_group(banks[4 + mc], INNER, m512_terms(((zr, ri), (zi, rr)), mc))
        dv = p.bump("te", "te")
        p.wait("dve", "te", dv)
        for mc in range(NCI):
            sl = slice(mc * INNER, (mc + 1) * INNER)
            p.do("dve", lambda v, b=banks[mc], sl=sl, xr=xr: v.tensor_add(xr[:, sl], xr[:, sl], pb[b][:, 0:INNER]))
            p.do("dve", lambda v, b=banks[4 + mc], sl=sl, xi=xi: v.tensor_add(xi[:, sl], xi[:, sl], pb[b][:, 0:INNER]))
            p.bump("dve", "dve")
            p.release_bank(banks[mc], "dve"); p.release_bank(banks[4 + mc], "dve")
        p.wait("te", "dve")
        banks = p.take_banks(8)
        for mc in range(NCI):
            mm_group(banks[mc], INNER, m512_terms(((rr, zr), (riN, zi)), mc))
            mm_group(banks[4 + mc], INNER, m512_terms(((ri, zr), (rr, zi)), mc))
        ev = p.bump("te", "te")
        p.wait("dve", "te", ev)
        for mc in range(NCI):
            sl = slice(mc * INNER, (mc + 1) * INNER)
            p.do("dve", lambda v, b=banks[mc], sl=sl, zr=zr: v.tensor_add(zr[:, sl], zr[:, sl], pb[b][:, 0:INNER]))
            p.do("dve", lambda v, b=banks[4 + mc], sl=sl, zi=zi: v.tensor_add(zi[:, sl], zi[:, sl], pb[b][:, 0:INNER]))
            p.bump("dve", "dve")
            p.release_bank(banks[mc], "dve"); p.release_bank(banks[4 + mc], "dve")
        p.wait("te", "dve")

    for it in range(34):
        newton_iter(Xbr, Xbi, Zbr, Zbi, Rbr, Rbi, RbiN, vTbr, vTbi, vTbiN, True)
    p.wait("dve", "te", p.sems["te"].n)
    p.do("dve", lambda v: v.tensor_copy(Zr, Zbr))
    p.do("dve", lambda v: v.tensor_copy(Zi, Zbi))
    p.do("dve", lambda v: v.tensor_copy(scr, Xbr))
    p.bump("dve", "dve")
    p.do("dve", lambda v: v.tensor_copy(Xi, Xbi))
    p.do("dve", lambda v: v.tensor_copy(Xr, scr))
    p.bump("dve", "dve")
    p.wait("te", "dve", p.sems["dve"].n)
    for it in range(newton_iters):
        newton_iter(Xr, Xi, Zr, Zi, Rr, Ri, RiN, vT_r, vT_i, vT_iN, False)

    tapdma("Zr", Zr); tapdma("Zi", Zi)
    zr_h, zr_l = bpair(ZrN)
    zi_h, zi_l = bpair(Zr)
    vtr_h, vtr_l = bpair(vT_iN)
    vti_h, vti_l = bpair(vT_r)
    gwh, gwl = bpair(gwTp)
    p.do("dve", lambda v: v.tensor_copy(zr_h, Zr))
    p.guard("dve")
    p.do("dve", lambda v: v.tensor_sub(zr_l, Zr, zr_h))
    p.do("dve", lambda v: v.tensor_copy(zi_h, Zi))
    p.guard("dve")
    p.do("dve", lambda v: v.tensor_sub(zi_l, Zi, zi_h))
    p.do("dve", lambda v: v.tensor_copy(vtr_h, vT_r))
    p.guard("dve")
    p.do("dve", lambda v: v.tensor_sub(vtr_l, vT_r, vtr_h))
    p.do("dve", lambda v: v.tensor_copy(vti_h, vT_i))
    p.guard("dve")
    p.do("dve", lambda v: v.tensor_sub(vti_l, vT_i, vti_h))
    p.do("dve", lambda v: v.tensor_copy(gwh, gwT))
    p.guard("dve")
    p.do("dve", lambda v: v.tensor_sub(gwl, gwT, gwh))
    p.bump("dve", "dve")
    scrb = scr.bitcast(mybir.dt.bfloat16)
    for c in range(NCI):
        p.wait("dve", "dst", p.sems["dst"].n)
        p.do("dve", lambda v, c=c: v.tensor_copy(scrb[:, 0:1024], woutT[:, c * D:(c + 1) * D]))
        p.guard("dve")
        p.do("dve", lambda v, c=c: v.tensor_sub(scrb[:, 1024:2048], woutT[:, c * D:(c + 1) * D],
                                                scrb[:, 0:1024]))
        p.bump("dve", "dve")
        p.wait("gps", "dve", p.sems["dve"].n)
        p.dma("gps", WOUTP[c * 128:(c + 1) * 128, :], scrb[:, 0:2048], sem="dst")
    p.wait("dve", "dst", p.sems["dst"].n)

    def ln(dst_fm, tapname, k, x_tm):
        p.wait("dve", "te", p.sems["te"].n)
        for j in range(2):
            p.do("dve", lambda v, j=j: v.reduce_sum(small[:, 8 + j:9 + j],
                                                    x_tm[:, j * D:(j + 1) * D], axis=AX.X))
        sv = p.bump("dve", "dve")
        p.wait("act", "dve", sv)
        for j in range(2):
            p.do("act", lambda s, j=j: s.activation(scr2, x_tm[:, j * D:(j + 1) * D], AF.Square,
                                                    accum_out=small[:, 10 + j:11 + j]))
        qv = p.bump("act", "act")
        p.wait("dve", "act", qv)
        for j in range(2):
            p.do("dve", lambda v, j=j: v.tensor_scalar_mul(small[:, 12 + j:13 + j],
                                                           small[:, 8 + j:9 + j], 1.0 / D))
            p.do("dve", lambda v, j=j: v.tensor_scalar_mul(small[:, 14 + j:15 + j],
                                                           small[:, 10 + j:11 + j], 1.0 / D))
        p.guard("dve")
        for j in range(2):
            p.do("dve", lambda v, j=j: v.tensor_scalar(small[:, 16 + j:17 + j],
                                                       small[:, 12 + j:13 + j],
                                                       small[:, 12 + j:13 + j], -1.0,
                                                       ALU.mult, ALU.mult))
        p.guard("dve")
        for j in range(2):
            p.do("dve", lambda v, j=j: v.tensor_add(small[:, 14 + j:15 + j],
                                                    small[:, 14 + j:15 + j], small[:, 16 + j:17 + j]))
            p.do("dve", lambda v, j=j: v.tensor_scalar_add(small[:, 14 + j:15 + j],
                                                           small[:, 14 + j:15 + j], EPS))
        p.guard("dve")
        for j in range(2):
            p.do("dve", lambda v, j=j: v.reciprocal(small[:, 18 + j:19 + j], small[:, 14 + j:15 + j]))
        rv = p.bump("dve", "dve")
        p.wait("act", "dve", rv)
        for j in range(2):
            p.do("act", lambda s, j=j: s.activation(small[:, 20 + j:21 + j],
                                                    small[:, 18 + j:19 + j], AF.Sqrt))
        rstdv = p.bump("act", "act")
        p.wait("dve", "act", rstdv)
        for j in range(2):
            p.do("dve", lambda v, j=j:
                 v.tensor_scalar(xn_tm[:, j * D:(j + 1) * D], x_tm[:, j * D:(j + 1) * D],
                                 small[:, 12 + j:13 + j], small[:, 20 + j:21 + j],
                                 ALU.subtract, ALU.mult))
        xnv = p.bump("dve", "dve")
        if k == 0:
            tapdma(tapname, xn_tm)
        return xnv

    def ln_tp(dst_fm, xnv):
        p.wait("te", "dve", xnv)
        dh, dl = bpair(dst_fm)
        for c in range(ND):
            for j in range(2):
                sl = slice(c * T + j * 128, c * T + (j + 1) * 128)
                transp(xn_tm[:, j * D + c * 128: j * D + (c + 1) * 128],
                       lambda v, ps, sl=sl: v.tensor_copy(dh[:, sl], ps[:, 0:128]),
                       lambda v, ps, sl=sl: v.tensor_tensor(dl[:, sl], ps[:, 0:128],
                                                            dh[:, sl], ALU.subtract))
        return p.sems["dve"].n

    def proj_and_factor(W_hbm, dst_re, dst_im, kind, key, xv, nxt):
        p.wait("te", "dve", xv)
        banks = p.take_banks(8)
        wbv = watb.bitcast(mybir.dt.bfloat16)
        x1h, x1l = bpair(xn1_fm)
        for tn in range(4):
            p.wait("te", *ws[(key, tn)])
            for m in range(8):
                for kcl in range(2):
                    kca = 2 * tn + kcl
                    base = (tn % 2) * 4096 + kcl * 2048
                    wh = wbv[:, base + m * 128: base + (m + 1) * 128]
                    wl = wbv[:, base + 1024 + m * 128: base + 1024 + (m + 1) * 128]
                    xh = x1h[:, kca * T:(kca + 1) * T]
                    xl = x1l[:, kca * T:(kca + 1) * T]
                    for ti, (lt, rh) in enumerate(((wh, xh), (wh, xl), (wl, xh))):
                        p.do("te", lambda t, b=banks[m], lt=lt, rh=rh, st=(kca == 0 and ti == 0),
                             sp=(kca == 7 and ti == 2):
                             t.matmul(pb[b][:, 0:T], lt, rh, start=st, stop=sp))
            pv = p.bump("te", "te")
            if nxt[tn] is not None:
                k2, t2, W2 = nxt[tn]
                wissue(k2, t2, W2, pv)
        p.wait("dve", "te", pv)
        fs = xn_tm
        for cc in range(NCI):
            p.do("dve", lambda v, b=banks[cc], cc=cc:
                 v.tensor_copy(fs[:, cc * 512: cc * 512 + T], pb[b][:, 0:T]))
            p.bump("dve", "dve")
            p.release_bank(banks[cc], "dve")
            p.do("dve", lambda v, b=banks[4 + cc], cc=cc:
                 v.tensor_copy(fs[:, cc * 512 + T:(cc + 1) * 512], pb[b][:, 0:T]))
            p.bump("dve", "dve")
            p.release_bank(banks[4 + cc], "dve")
        p.wait("act", "dve", p.sems["dve"].n)
        m2f = pool[:, 8192:10240]
        p.do("act", lambda s: s.activation(m2f, fs, AF.Square))
        sqv = p.bump("act", "act")
        p.wait("dve", "act", sqv)
        m2 = m2f.rearrange("p (c a u) -> p c a u", c=NCI, a=2)
        s2c = scr2.rearrange("p (c u) -> p c u", c=NCI)
        p.do("dve", lambda v: v.tensor_add(s2c, m2[:, :, 0, :], m2[:, :, 1, :]))
        m2v = p.bump("dve", "dve")
        p.wait("act", "dve", m2v)
        if kind == "a":
            p.do("act", lambda s: s.activation(m2f[:, 0:D], scr2, AF.Sqrt))
            rtv = p.bump("act", "act")
            p.wait("dve", "act", rtv)
            p.do("dve", lambda v: v.tensor_scalar_add(scr2, scr2, 1.0))
            p.guard("dve")
            p.do("dve", lambda v: v.reciprocal(scr2, scr2))
            p.guard("dve")
            p.do("dve", lambda v: v.tensor_mul(scr2, scr2, m2f[:, 0:D]))
            p.guard("dve")
        else:
            p.do("act", lambda s: s.activation(scr2, scr2, AF.Sqrt))
            rtv = p.bump("act", "act")
            p.wait("dve", "act", rtv)
        fre = fs.rearrange("p (c a u) -> p c a u", c=NCI, a=2)
        p.do("dve", lambda v: v.tensor_mul(dst_re.rearrange("p (c u) -> p c u", c=NCI),
                                           fre[:, :, 0, :], s2c))
        p.do("dve", lambda v: v.tensor_mul(dst_im.rearrange("p (c u) -> p c u", c=NCI),
                                           fre[:, :, 1, :], s2c))
        p.bump("dve", "dve")

    def scan_inner(ta, tb, st):
        qa_r, qa_i = q5(a_re), q5(a_im)
        qu_r, qu_i = q5(u_re), q5(u_im)
        qP_r, qP_i = q5(P_re), q5(P_im)
        def t128(i):
            return tmps[i][:, 0:NCI * 2 * NJ].rearrange("p (c s j) -> p c s j", c=NCI, s=2)
        def g128(i):
            return gtmps[i][:, 0:NCI * 2 * NJ].rearrange("p (c s j) -> p c s j", c=NCI, s=2)
        t1_, t2_, t3_, t4_, t5_, t6_ = (t128(i) for i in range(6))
        g1, g2, g3, g4 = (g128(i) for i in range(4))
        t1, t2, t3, t4, t5, t6 = t1_, t2_, t3_, t4_, t5_, t6_
        p.wait("gps", "dve", st["uv"])
        for t in range(ta, tb):
            ar, ai = qa_r[:, :, :, :, t], qa_i[:, :, :, :, t]
            ur, ui = qu_r[:, :, :, :, t], qu_i[:, :, :, :, t]
            pr, pi = qP_r[:, :, :, :, t], qP_i[:, :, :, :, t]
            if t == 0:
                p.do("gps", lambda g, ar=ar, ur=ur: g.tensor_mul(t1, ar, ur))
                p.do("gps", lambda g, ai=ai, ui=ui: g.tensor_mul(t2, ai, ui))
                p.do("gps", lambda g, ar=ar, ui=ui: g.tensor_mul(t3, ar, ui))
                p.do("gps", lambda g, ai=ai, ur=ur: g.tensor_mul(t4, ai, ur))
                p.do("gps", lambda g, ur=ur: g.tensor_sub(ur, t1, t2))
                p.do("gps", lambda g, ui=ui: g.tensor_add(ui, t3, t4))
                p.do("gps", lambda g, pr=pr, ar=ar: g.tensor_copy(pr, ar))
                p.do("gps", lambda g, pi=pi, ai=ai: g.tensor_copy(pi, ai))
            else:
                urp, uip = qu_r[:, :, :, :, t - 1], qu_i[:, :, :, :, t - 1]
                prp, pip = qP_r[:, :, :, :, t - 1], qP_i[:, :, :, :, t - 1]
                p.do("gps", lambda g, urp=urp, ur=ur: g.tensor_add(t5, urp, ur))
                p.do("gps", lambda g, uip=uip, ui=ui: g.tensor_add(t6, uip, ui))
                p.do("gps", lambda g, ar=ar: g.tensor_mul(t1, ar, t5))
                p.do("gps", lambda g, ai=ai: g.tensor_mul(t2, ai, t6))
                p.do("gps", lambda g, ur=ur: g.tensor_sub(ur, t1, t2))
                p.do("gps", lambda g, ar=ar: g.tensor_mul(t3, ar, t6))
                p.do("gps", lambda g, ai=ai: g.tensor_mul(t4, ai, t5))
                p.do("gps", lambda g, ui=ui: g.tensor_add(ui, t3, t4))
                p.do("gps", lambda g, ar=ar, prp=prp: g.tensor_mul(g1, ar, prp))
                p.do("gps", lambda g, ai=ai, pip=pip: g.tensor_mul(g2, ai, pip))
                p.do("gps", lambda g, pr=pr: g.tensor_sub(pr, g1, g2))
                p.do("gps", lambda g, ar=ar, pip=pip: g.tensor_mul(g3, ar, pip))
                p.do("gps", lambda g, ai=ai, prp=prp: g.tensor_mul(g4, ai, prp))
                p.do("gps", lambda g, pi=pi: g.tensor_add(pi, g3, g4))
        st["ptv"] = p.bump("gps", "gps")

    def scan_chain(ja, jb, st):
        qu_r, qu_i = q5(u_re), q5(u_im)
        qP_r, qP_i = q5(P_re), q5(P_im)
        qs_r = scar_r.rearrange("p (c s j) -> p c s j", c=NCI, s=2)
        qs_i = scar_i.rearrange("p (c s j) -> p c s j", c=NCI, s=2)
        def t8(i):
            return tmps[i][:, 0:8].rearrange("p (c s) -> p c s", c=NCI)
        t1b, t2b, t3b, t4b, t5b = (t8(i) for i in range(5))
        if ja == 1:
            p.do("gps", lambda g: g.tensor_copy(qs_r[:, :, :, 0], qcs(scb_re)))
            p.do("gps", lambda g: g.tensor_copy(qs_i[:, :, :, 0], qcs(scb_im)))
        for j in range(ja, jb):
            Ar = qP_r[:, :, :, j - 1, LC - 1]; Ai = qP_i[:, :, :, j - 1, LC - 1]
            Br = qu_r[:, :, :, j - 1, LC - 1]; Bi = qu_i[:, :, :, j - 1, LC - 1]
            Srp = qs_r[:, :, :, j - 1]; Sip = qs_i[:, :, :, j - 1]
            Sr = qs_r[:, :, :, j] if j < NJ else qcs(scb_re)
            Si = qs_i[:, :, :, j] if j < NJ else qcs(scb_im)
            p.do("gps", lambda g, Ar=Ar, Srp=Srp: g.tensor_mul(t1b, Ar, Srp))
            p.do("gps", lambda g, Ai=Ai, Sip=Sip: g.tensor_mul(t2b, Ai, Sip))
            p.do("gps", lambda g, Ar=Ar, Sip=Sip: g.tensor_mul(t3b, Ar, Sip))
            p.do("gps", lambda g, Ai=Ai, Srp=Srp: g.tensor_mul(t4b, Ai, Srp))
            p.do("gps", lambda g: g.tensor_sub(t5b, t1b, t2b))
            p.do("gps", lambda g, Sr=Sr, Br=Br: g.tensor_add(Sr, t5b, Br))
            p.do("gps", lambda g: g.tensor_add(t5b, t3b, t4b))
            p.do("gps", lambda g, Si=Si, Bi=Bi: g.tensor_add(Si, t5b, Bi))
        p.bump("gps", "gps")

    def scan_app(st):
        qP_r, qP_i = q5(P_re), q5(P_im)
        qs_r = scar_r.rearrange("p (c s j) -> p c s j", c=NCI, s=2)
        qs_i = scar_i.rearrange("p (c s j) -> p c s j", c=NCI, s=2)
        Srb = qs_r.unsqueeze(4).to_broadcast([128, NCI, 2, NJ, LC])
        Sib = qs_i.unsqueeze(4).to_broadcast([128, NCI, 2, NJ, LC])
        sA = scr[:, 0:1024].rearrange("p (c s j u) -> p c s j u", c=NCI, s=2, j=NJ)
        sB = scr[:, 1024:2048].rearrange("p (c s j u) -> p c s j u", c=NCI, s=2, j=NJ)
        p.do("gps", lambda g: g.tensor_mul(sA, qP_r, Srb))
        p.do("gps", lambda g: g.tensor_mul(sB, qP_i, Sib))
        p.do("gps", lambda g: g.tensor_sub(scr[:, 0:1024], scr[:, 0:1024], scr[:, 1024:2048]))
        p.do("gps", lambda g: g.tensor_add(u_re, u_re, scr[:, 0:1024]))
        p.do("gps", lambda g: g.tensor_mul(sA, qP_r, Sib))
        p.do("gps", lambda g: g.tensor_mul(sB, qP_i, Srb))
        p.do("gps", lambda g: g.tensor_add(scr[:, 0:1024], scr[:, 0:1024], scr[:, 1024:2048]))
        p.do("gps", lambda g: g.tensor_add(u_im, u_im, scr[:, 0:1024]))
        wst["scanv"] = p.bump("gps", "gps")

    def gate_topk(k, xn2q, x2v):
        rwo = (k % 2) * 2 * E
        p.wait("te", "dve", x2v)
        p.wait("dve", "te", p.sems["te"].n)
        x2h, x2l = bpair(xn2q)
        for j in range(2):
            b = p.take_banks(1)[0]
            terms = []
            for kc in range(ND):
                lh = x2h[:, kc * T + j * 128: kc * T + (j + 1) * 128]
                ll = x2l[:, kc * T + j * 128: kc * T + (j + 1) * 128]
                rh = gwh[:, kc * E:(kc + 1) * E]
                rl = gwl[:, kc * E:(kc + 1) * E]
                terms += [(lh, rh), (lh, rl), (ll, rh)]
            mm_group(b, E, terms)
            tv = p.bump("te", "te")
            p.wait("dve", "te", tv)
            p.do("dve", lambda v, b=b, j=j: v.tensor_copy(gate[:, j * 48: j * 48 + E], pb[b][:, 0:E]))
            p.bump("dve", "dve")
            p.release_bank(b, "dve")
        def g(j, off, n=E):
            return gate[:, j * 48 + off: j * 48 + off + n]
        for j in range(2):
            p.do("dve", lambda v, j=j: v.reduce_max(small[:, 24 + j:25 + j], g(j, 0), axis=AX.X))
        p.guard("dve")
        for j in range(2):
            p.do("dve", lambda v, j=j: v.tensor_scalar_mul(small[:, 26 + j:27 + j],
                                                           small[:, 24 + j:25 + j], -1.0))
        ngv = p.bump("dve", "dve")
        p.wait("act", "dve", ngv)
        for j in range(2):
            p.do("act", lambda s, j=j: s.activation(g(j, 8), g(j, 0), AF.Exp,
                                                    bias=small[:, 26 + j:27 + j]))
        ev = p.bump("act", "act")
        p.wait("dve", "act", ev)
        for j in range(2):
            p.do("dve", lambda v, j=j: v.reduce_sum(small[:, 28 + j:29 + j], g(j, 8), axis=AX.X))
        p.guard("dve")
        for j in range(2):
            p.do("dve", lambda v, j=j: v.reciprocal(small[:, 30 + j:31 + j], small[:, 28 + j:29 + j]))
        p.guard("dve")
        for j in range(2):
            p.do("dve", lambda v, j=j: v.tensor_scalar(g(j, 8), g(j, 8), small[:, 30 + j:31 + j],
                                                       None, ALU.mult))
            p.do("dve", lambda v, j=j: v.tensor_add(g(j, 16), g(j, 8), gbb))
        p.guard("dve")
        for pass_i in range(2):
            for j in range(2):
                p.do("dve", lambda v, j=j: v.reduce_max(small[:, 32 + j:33 + j], g(j, 16), axis=AX.X))
            p.guard("dve")
            for j in range(2):
                p.do("dve", lambda v, j=j: v.tensor_scalar(g(j, 24), g(j, 16),
                                                           small[:, 32 + j:33 + j], None, ALU.is_ge))
            p.guard("dve")
            for j in range(2):
                p.do("dve", lambda v, j=j: v.tensor_scalar(g(j, 32), g(j, 24), -BIG, BIG,
                                                           ALU.mult, ALU.add))
                p.do("dve", lambda v, j=j: v.tensor_add(g(j, 32), g(j, 32), iota8))
            p.guard("dve")
            for j in range(2):
                p.do("dve", lambda v, j=j: v.tensor_reduce(small[:, 34 + j:35 + j], g(j, 32),
                                                           AX.X, ALU.min))
            p.guard("dve")
            for j in range(2):
                p.do("dve", lambda v, j=j: v.tensor_scalar(g(j, 24), iota8,
                                                           small[:, 34 + j:35 + j], None, ALU.is_equal))
            p.guard("dve")
            for j in range(2):
                p.do("dve", lambda v, j=j: v.tensor_mul(g(j, 40), g(j, 8), g(j, 24)))
            p.guard("dve")
            for j in range(2):
                p.do("dve", lambda v, j=j, pi=pass_i:
                     v.reduce_sum(small[:, 36 + 2 * j + pi:37 + 2 * j + pi], g(j, 40), axis=AX.X))
            p.guard("dve")
            for j in range(2):
                if pass_i == 0:
                    p.do("dve", lambda v, j=j, rwo=rwo: v.tensor_scalar(
                        rww[:, rwo + j * E: rwo + (j + 1) * E], g(j, 24),
                        small[:, 36 + 2 * j:37 + 2 * j], None, ALU.mult))
                    p.do("dve", lambda v, j=j: v.tensor_scalar(g(j, 40), g(j, 24), BIG, None, ALU.mult))
                    p.do("dve", lambda v, j=j: v.tensor_sub(g(j, 16), g(j, 16), g(j, 40)))
                else:
                    p.do("dve", lambda v, j=j: v.tensor_scalar(g(j, 40), g(j, 24),
                                                               small[:, 37 + 2 * j:38 + 2 * j],
                                                               None, ALU.mult))
                    p.do("dve", lambda v, j=j, rwo=rwo: v.tensor_add(
                        rww[:, rwo + j * E: rwo + (j + 1) * E],
                        rww[:, rwo + j * E: rwo + (j + 1) * E], g(j, 40)))
            p.guard("dve")

    wslot_free = [p.sems["te"].n] * 6
    ffn_base = [0]
    gctr = [0]

    x_tm_a = x_tm
    def xt_of(k):
        return x_tm_a if k % 2 == 0 else x_tm2

    def xn2_of(k):
        return xn2_fm if k % 2 == 0 else xn2_b

    ws = {}
    wst = {"wout_cons": 0}
    gsub = {}
    wbv16 = watb.bitcast(mybir.dt.bfloat16)

    def wissue(key, t, W_hbm, anchor, k0=False):
        if k0:
            p.wait("act", "dst", p.sems["dst"].n, force=True)
        else:
            p.wait("act", "te", max(anchor, 1), force=True)
        dst = wbv16[:, (t % 2) * 4096:(t % 2) * 4096 + 4096].rearrange("p (c f) -> p c f", c=2)
        src = W_hbm.rearrange("(c p) f -> p c f", p=128)[:, 2 * t:2 * t + 2, :]
        sem = "dwc" if t % 2 == 0 else "dwd"
        ws[(key, t)] = (sem, p.dma("act", dst, src, sem=sem))

    def stages_A(k):
        x_tm = xt_of(k)
        asx = {}
        def s_wd():
            wissue("wa", 0, WAT_HBM, wst["wout_cons"], k0=(k == 0))
            wissue("wa", 1, WAT_HBM, wst["wout_cons"], k0=(k == 0))
        def s_xln():
            p.wait("gps", "dst", p.sems["dst"].n)
            p.wait("gps", "dve", p.sems["dve"].n)
            p.wait("gps", "te", p.sems["te"].n) if k == 0 else None
            ldx = p.dma("gps", x_tm.rearrange("p (s d) -> p s d", s=2),
                        X.rearrange("(s n) d -> n s d", s=2)[k * TOK:(k + 1) * TOK], sem="dx")
            p.wait("dve", "dx", ldx)
            p.wait("act", "dx", ldx)
            asx["xnv"] = ln(xn1_fm, "ln1", k, x_tm)
        def s_xtp():
            asx["ln1tp"] = ln_tp(xn1_fm, asx["xnv"])
        def s_pa():
            proj_and_factor(WAT_HBM, a_re, a_im, "a", "wa", asx["ln1tp"],
                            [("wa", 2, WAT_HBM), ("wa", 3, WAT_HBM),
                             ("wx", 0, WXT_HBM), ("wx", 1, WXT_HBM)])
            if k == 0:
                tapdma("afac", a_re); tapdma("aim", a_im)
        def s_px():
            proj_and_factor(WXT_HBM, xc_re, xc_im, "x", "wx", asx["ln1tp"],
                            [("wx", 2, WXT_HBM), ("wx", 3, WXT_HBM),
                             ("wo", 0, WOUTP), ("wo", 1, WOUTP)])
            if k == 0:
                tapdma("xcre", xc_re); tapdma("xcim", xc_im)
        def s_xr():
            cin = (k % 2) * 2 * NCI
            cout = ((k + 1) % 2) * 2 * NCI
            q4b = lambda ap: ap.rearrange("p (c s u) -> p c s u", c=NCI, s=2)
            p.do("dve", lambda v: v.tensor_copy(q4b(xrh_re)[:, :, :, 1:TOK], q4(xc_re)[:, :, :, 0:TOK - 1]))
            p.do("dve", lambda v: v.tensor_scalar(q4b(xrh_iN)[:, :, :, 1:TOK],
                                                  q4(xc_im)[:, :, :, 0:TOK - 1], -1.0, None, ALU.mult))
            p.do("dve", lambda v, cin=cin: v.tensor_copy(q4b(xrh_re)[:, :, :, 0],
                                                         qcs(carry_re[:, cin:cin + 2 * NCI])))
            p.do("dve", lambda v, cin=cin: v.tensor_copy(q4b(xrh_iN)[:, :, :, 0],
                                                         qcs(carry_iN[:, cin:cin + 2 * NCI])))
            p.guard("dve")
            p.do("dve", lambda v: v.tensor_tensor(q4b(xrl_re)[:, :, :, 1:TOK],
                                                  q4(xc_re)[:, :, :, 0:TOK - 1],
                                                  q4b(xrh_re)[:, :, :, 1:TOK], ALU.subtract))
            p.do("dve", lambda v: v.scalar_tensor_tensor(q4b(xrl_iN)[:, :, :, 1:TOK],
                                                         q4(xc_im)[:, :, :, 0:TOK - 1], -1.0,
                                                         q4b(xrh_iN)[:, :, :, 1:TOK],
                                                         ALU.mult, ALU.subtract))
            p.do("dve", lambda v, cin=cin: v.tensor_tensor(q4b(xrl_re)[:, :, :, 0],
                                                           qcs(carry_re[:, cin:cin + 2 * NCI]),
                                                           q4b(xrh_re)[:, :, :, 0], ALU.subtract))
            p.do("dve", lambda v, cin=cin: v.tensor_tensor(q4b(xrl_iN)[:, :, :, 0],
                                                           qcs(carry_iN[:, cin:cin + 2 * NCI]),
                                                           q4b(xrh_iN)[:, :, :, 0], ALU.subtract))
            p.do("dve", lambda v, cout=cout: v.tensor_copy(qcs(carry_re[:, cout:cout + 2 * NCI]),
                                                           q4(xc_re)[:, :, :, TOK - 1]))
            p.do("dve", lambda v, cout=cout: v.tensor_scalar(qcs(carry_iN[:, cout:cout + 2 * NCI]),
                                                             q4(xc_im)[:, :, :, TOK - 1],
                                                             -1.0, None, ALU.mult))
            xrv = p.bump("dve", "dve")
            p.wait("te", "dve", xrv)

        def s_u():
            zr_pr = (zr_h, zr_l); zi_pr = (zi_h, zi_l)
            xre_pr = (xrh_re, xrl_re); xiN_pr = (xrh_iN, xrl_iN)
            for mc in range(NCI):
                terms_re = mm3([(prtile(zr_pr, kc, mc), xrpair(xre_pr, kc)) for kc in range(NCI)] +
                               [(prtile(zi_pr, kc, mc), xrpair(xiN_pr, kc)) for kc in range(NCI)])
                bre = p.take_banks(1)[0]
                mm_group(bre, T, terms_re)
                uv = p.bump("te", "te")
                p.wait("dve", "te", uv)
                sl = slice(mc * T, (mc + 1) * T)
                p.do("dve", lambda v, b=bre, sl=sl: v.tensor_copy(u_re[:, sl], pb[b][:, 0:T]))
                p.bump("dve", "dve")
                p.release_bank(bre, "dve")
                tP = mm3([(prtile(zi_pr, kc, mc), xrpair(xre_pr, kc)) for kc in range(NCI)])
                tQ = mm3([(prtile(zr_pr, kc, mc), xrpair(xiN_pr, kc)) for kc in range(NCI)])
                psub(tP, tQ, T,
                     lambda v, pP, pQ, sl=sl: v.tensor_tensor(u_im[:, sl], pP[:, 0:T], pQ[:, 0:T],
                                                              ALU.subtract))
            asx["uv"] = p.sems["dve"].n
            if k == 0:
                tapdma("ure", u_re); tapdma("uim", u_im)
        out = [(6, s_wd), (40, s_xln), (48, s_xtp), (65, s_pa), (98, s_px),
               (116, s_xr), (130, s_u)]
        scpts = [140, 145, 150, 155, 160, 165, 170, 175]
        nsl = len(scpts)
        for si in range(nsl):
            ta = (LC * si) // nsl
            tb = (LC * (si + 1)) // nsl
            out.append((scpts[si], lambda ta=ta, tb=tb: scan_inner(ta, tb, asx)))
        out.append((181, lambda: scan_chain(1, NJ // 2 + 1, asx)))
        out.append((186, lambda: scan_chain(NJ // 2 + 1, NJ + 1, asx)))
        out.append((191, lambda: scan_app(asx)))
        return out

    def stages_B(k):
        x_tm = xt_of(k)
        bs = {}
        def s_h():
            p.wait("dve", "gps", wst["scanv"])
            s_reh, s_rel = bpair(a_re)
            s_imh, s_iml = bpair(a_im)
            p.do("dve", lambda v: v.tensor_copy(s_reh, u_re))
            p.do("dve", lambda v: v.tensor_copy(s_imh, u_im))
            p.guard("dve")
            p.do("dve", lambda v: v.tensor_sub(s_rel, u_re, s_reh))
            p.do("dve", lambda v: v.tensor_sub(s_iml, u_im, s_imh))
            spv = p.bump("dve", "dve")
            p.wait("te", "dve", spv)
            vtr_pr = (vtr_h, vtr_l); vti_pr = (vti_h, vti_l)
            sre_pr = (s_reh, s_rel); sim_pr = (s_imh, s_iml)
            for mc in range(NCI):
                sl = slice(mc * T, (mc + 1) * T)
                tP = mm3([(prtile(vtr_pr, kc, mc), xrpair(sre_pr, kc)) for kc in range(NCI)])
                tQ = mm3([(prtile(vti_pr, kc, mc), xrpair(sim_pr, kc)) for kc in range(NCI)])
                def hevac(v, pP, pQ, sl=sl, mc=mc):
                    return v.tensor_tensor(xn_tm[:, mc * T:(mc + 1) * T], pP[:, 0:T], pQ[:, 0:T],
                                           ALU.subtract)
                psub(tP, tQ, T, hevac)
                p.do("dve", lambda v, sl=sl, mc=mc: v.tensor_add(xn_tm[:, mc * T:(mc + 1) * T],
                                                                 xn_tm[:, mc * T:(mc + 1) * T],
                                                                 xc_re[:, sl]))
                p.guard("dve")
                p.do("dve", lambda v, sl=sl, mc=mc: v.tensor_copy(hreh[:, sl], xn_tm[:, mc * T:(mc + 1) * T]))
                p.guard("dve")
                p.do("dve", lambda v, sl=sl, mc=mc: v.tensor_tensor(hrel[:, sl],
                                                                    xn_tm[:, mc * T:(mc + 1) * T],
                                                                    hreh[:, sl], ALU.subtract))
                p.bump("dve", "dve")
            bs["hpv"] = p.sems["dve"].n
            if k == 0:
                tapdma("sre", u_re, "gps"); tapdma("sim", u_im, "gps")
        def s_op():
            p.wait("te", "dve", bs["hpv"])
            p.wait("te", *ws[("wo", 0)])
            p.wait("te", *ws[("wo", 1)])
            wov = watb.bitcast(mybir.dt.bfloat16)
            for j in range(2):
                for n in range(2):
                    b = p.take_banks(1)[0]
                    terms = []
                    for cc in range(NCI):
                        lh = hreh[:, cc * T + j * 128: cc * T + (j + 1) * 128]
                        ll = hrel[:, cc * T + j * 128: cc * T + (j + 1) * 128]
                        rh = wov[:, cc * 2048 + n * 512: cc * 2048 + (n + 1) * 512]
                        rl = wov[:, cc * 2048 + 1024 + n * 512: cc * 2048 + 1024 + (n + 1) * 512]
                        terms += [(lh, rh), (lh, rl), (ll, rh)]
                    mm_group(b, 512, terms)
                    ov = p.bump("te", "te")
                    p.wait("dve", "te", ov)
                    sl = slice(j * D + n * 512, j * D + (n + 1) * 512)
                    p.do("dve", lambda v, b=b, sl=sl: v.tensor_add(x_tm[:, sl], x_tm[:, sl], pb[b][:, 0:512]))
                    p.bump("dve", "dve")
                    p.release_bank(b, "dve")
            wst["wout_cons"] = p.sems["te"].n
            if k == 0:
                tapdma("x1", x_tm)
        def s_l2():
            bs["xn2v"] = ln(xn2_of(k), "ln2", k, x_tm)
        def s_l2tp():
            bs["x2v"] = ln_tp(xn2_of(k), bs["xn2v"])
        def s_g():
            gate_topk(k, xn2_of(k), bs["x2v"])
            bs["rwv"] = p.sems["dve"].n
            if k == 0:
                tapdma("rww", rww[:, 0:2 * E])
        def s_rb():
            p.wait("te", "dve", bs["rwv"])
            rwo = (k % 2) * 2 * E
            for j in range(2):
                b = p.take_banks(1)[0]
                p.do("te", lambda t, b=b, j=j, rwo=rwo: t.transpose(pb[b][0:E, 0:128],
                                                                    rww[:, rwo + j * E: rwo + (j + 1) * E],
                                                                    i128))
                tv = p.bump("te", "te")
                p.wait("dve", "te", tv)
                p.do("dve", lambda v, b=b, j=j: v.tensor_copy(rwf9[0:E, j * TOK:(j + 1) * TOK],
                                                              pb[b][0:E, 0:128]))
                p.bump("dve", "dve")
                p.release_bank(b, "dve")
            p.wait("te", "dve")
            for j in range(2):
                for n in range(2):
                    b = p.take_banks(1)[0]
                    p.do("te", lambda t, b=b, j=j, n=n:
                         t.matmul(pb[b][:, 0:512], rwf9[0:9, j * TOK:(j + 1) * TOK],
                                  b2sb[0:9, n * 512:(n + 1) * 512], start=True, stop=True))
                    tv = p.bump("te", "te")
                    p.wait("dve", "te", tv)
                    sl = slice(j * D + n * 512, j * D + (n + 1) * 512)
                    p.do("dve", lambda v, b=b, sl=sl: v.tensor_add(x_tm[:, sl], x_tm[:, sl],
                                                                   pb[b][:, 0:512]))
                    p.bump("dve", "dve")
                    p.release_bank(b, "dve")


        return [(208, s_h), (228, s_op), (242, s_l2), (250, s_l2tp),
                (260, s_g), (275, s_rb)]

    for _pt, f in stages_A(0):
        f()
    for _pt, f in stages_B(0):
        f()
    pend = {}
    for k in range(NCH):
        x_tm = xt_of(k)
        rwo = (k % 2) * 2 * E
        if k == 0:
            p.wait("sync", "dst", repack_done)
        units = []
        for e in range(9):
            W1s = W1[e] if e < E else SW1
            W3s = W3[e] if e < E else SW3
            W2s = W2[e] if e < E else SW2
            hstate = {}
            for hf in range(NH):
                def dma_fn(slot, sem, e=e, hf=hf):
                    return p.dma("sync", slot.bitcast(mybir.dt.bfloat16), W1P[e, hf], sem=sem)
                def comp_fn(slot, ld, hf=hf, e=e):
                    p.wait("te", ld[0], ld[1])
                    b = p.take_banks(1)[0]
                    sv = slot.bitcast(mybir.dt.bfloat16)
                    x2h, x2l = bpair(xn2_of(k))
                    terms = []
                    for kc in range(ND):
                        wh = sv[:, kc * 128:(kc + 1) * 128]
                        wl = sv[:, 1024 + kc * 128: 1024 + (kc + 1) * 128]
                        xh = x2h[:, kc * T:(kc + 1) * T]
                        xl = x2l[:, kc * T:(kc + 1) * T]
                        terms += [(wh, xh), (wh, xl), (wl, xh)]
                    mm_group(b, T, terms)
                    tv = p.bump("te", "te")
                    p.wait("act", "te", tv)
                    p.do("act", lambda s, b=b, hf=hf, e=e:
                         s.activation(hffn[:, hf * T:(hf + 1) * T], pb[b][:, 0:T], AF.Identity,
                                      bias=b1s[:, e * NH + hf: e * NH + hf + 1]))
                    p.bump("act", "act")
                    p.release_bank(b, "act")
                    return tv
                units.append((dma_fn, comp_fn))
            for hf in range(NH):
                def dma_fn(slot, sem, e=e, hf=hf):
                    return p.dma("sync", slot.bitcast(mybir.dt.bfloat16), W3P[e, hf], sem=sem)
                def comp_fn(slot, ld, hf=hf, e=e, hstate=hstate):
                    p.wait("te", ld[0], ld[1])
                    b = p.take_banks(1)[0]
                    sv = slot.bitcast(mybir.dt.bfloat16)
                    x2h, x2l = bpair(xn2_of(k))
                    terms = []
                    for kc in range(ND):
                        wh = sv[:, kc * 128:(kc + 1) * 128]
                        wl = sv[:, 1024 + kc * 128: 1024 + (kc + 1) * 128]
                        xh = x2h[:, kc * T:(kc + 1) * T]
                        xl = x2l[:, kc * T:(kc + 1) * T]
                        terms += [(wh, xh), (wh, xl), (wl, xh)]
                    mm_group(b, T, terms)
                    tv = p.bump("te", "te")
                    p.wait("dve", "te", tv)
                    p.wait("dve", "act", p.sems["act"].n)
                    hs = hmul[:, (hf % 4) * T:((hf % 4) + 1) * T]
                    p.do("dve", lambda v, b=b, hf=hf, e=e, hs=hs:
                         v.scalar_tensor_tensor(hs, pb[b][:, 0:T],
                                                b3s[:, e * NH + hf: e * NH + hf + 1],
                                                hffn[:, hf * T:(hf + 1) * T], ALU.add, ALU.mult))
                    mv = p.bump("dve", "dve")
                    p.release_bank(b, "dve")
                    hh, hl = hpair(hf)
                    p.wait("act", "dve", mv)
                    p.do("act", lambda s, hs=hs: s.activation(hs, hs, AF.Silu))
                    p.guard("act")
                    p.do("act", lambda s, hh=hh, hs=hs: s.copy(hh, hs))
                    cv = p.bump("act", "act")
                    p.wait("dve", "act", cv)
                    p.do("dve", lambda v, hh=hh, hl=hl, hs=hs: v.tensor_sub(hl, hs, hh))
                    hstate[hf] = p.bump("dve", "dve")
                    return tv
                units.append((dma_fn, comp_fn))
            ystate = {}
            for hf in range(NH):
                def dma_fn(slot, sem, e=e, hf=hf):
                    return p.dma("sync", slot.bitcast(mybir.dt.bfloat16), W2P[e, hf], sem=sem)
                def comp_fn(slot, ld, hf=hf, e=e, ystate=ystate, hstate=hstate):
                    hh, hl = hpair(hf)
                    if hf == 0:
                        ystate["banks"] = p.take_banks(4)
                        p.hold_banks(ystate["banks"])
                    p.wait("te", "dve", hstate[hf])
                    p.wait("te", ld[0], ld[1])
                    sv = slot.bitcast(mybir.dt.bfloat16)
                    for j in range(2):
                        for n in range(2):
                            lh = hh[:, j * 128:(j + 1) * 128]
                            ll = hl[:, j * 128:(j + 1) * 128]
                            rh = sv[:, n * 512:(n + 1) * 512]
                            rl = sv[:, 1024 + n * 512: 1024 + (n + 1) * 512]
                            for ti, (lt, rr) in enumerate(((lh, rh), (lh, rl), (ll, rh))):
                                p.do("te", lambda t, b=ystate["banks"][2 * j + n], lt=lt, rr=rr,
                                     st=(hf == 0 and ti == 0), sp=(hf == NH - 1 and ti == 2):
                                     t.matmul(pb[b][:, 0:512], lt, rr, start=st, stop=sp))
                    tv = p.bump("te", "te")
                    if hf == NH - 1:
                        p.wait("dve", "te", tv)
                        for j in range(2):
                            for n in range(2):
                                b = ystate["banks"][2 * j + n]
                                sl = slice(j * D + n * 512, j * D + (n + 1) * 512)
                                sc = rww[:, rwo + j * E + e: rwo + j * E + e + 1] if e < E else 1.0
                                p.do("dve", lambda v, b=b, sl=sl, sc=sc, x_tm=x_tm:
                                     v.scalar_tensor_tensor(x_tm[:, sl], pb[b][:, 0:512], sc,
                                                            x_tm[:, sl], ALU.mult, ALU.add))
                                p.bump("dve", "dve")
                                p.release_bank(b, "dve")
                        p.unhold_banks(ystate["banks"])
                    return tv
                units.append((dma_fn, comp_fn))
        G = 4
        DEPTH = 6
        if k == 0:
            anchor = p.sems["te"].n
            for _i in range(DEPTH):
                wslot_free[_i] = max(wslot_free[_i], anchor)
        NU = len(units)
        if k + 1 < NCH:
            pend = {}
            for pt, f in stages_A(k + 1) + stages_B(k + 1):
                pend.setdefault(pt, []).append(f)
        else:
            pend = {}
        ngrp = (NU + G - 1) // G
        gsem = ["dwa" if (gctr[0] + m) % 2 == 0 else "dwb" for m in range(ngrp)]
        gctr[0] += ngrp
        gend = {}

        def slot_ap(i):
            return wbuf[:, (i % DEPTH) * 1024:(i % DEPTH) * 1024 + 1024]

        def issue(i):
            p.wait("sync", "te", wslot_free[i % DEPTH], force=True)
            m = i // G
            v = units[i][0](slot_ap(i), gsem[m])
            if i == NU - 1:
                dn = 0
                while (i + 1 + dn) % G != 0:
                    p.dma("sync", dumm[0:1, dn * 8:dn * 8 + 8], ONES[0:1, 0:8], sem=gsem[m])
                    v = p.sems[gsem[m]].n
                    dn += 1
                gend[m] = v
            elif i % G == G - 1:
                gend[m] = v

        for i in range(min(DEPTH, NU)):
            issue(i)
        for i in range(NU):
            m = i // G
            wslot_free[i % DEPTH] = units[i][1](slot_ap(i), (gsem[m], gend[m]))
            if i in pend:
                for f in pend.pop(i):
                    f()
            ip = i + DEPTH
            if ip < NU:
                issue(ip)

        if k == 0:
            tapdma("xout", x_tm)
        p.wait("gps", "dve", p.sems["dve"].n)
        p.dma("gps", OUT.rearrange("(s n) d -> n s d", s=2)[k * TOK:(k + 1) * TOK],
              x_tm.rearrange("p (s d) -> p s d", s=2), sem="dst")

    p.wait("gps", "dst", p.sems["dst"].n)

    import contextlib
    with contextlib.ExitStack() as st:
        for s in p.sems.values():
            s.h = st.enter_context(nc.semaphore("s_" + s.name))
        blk = st.enter_context(nc.Block())
        blk.tensor(lambda e: [f(e) for f in p.ops["te"]])
        blk.vector(lambda e: [f(e) for f in p.ops["dve"]])
        blk.scalar(lambda e: [f(e) for f in p.ops["act"]])
        blk.gpsimd(lambda e: [f(e) for f in p.ops["gps"]])
        blk.sync(lambda e: [f(e) for f in p.ops["sync"]])
    ninstr = {k: len(v) for k, v in p.ops.items()}
    return nc, ninstr


def host_inputs(inp, core, n_cores=8, rows_per_core=4096):
    seqs_per_core = 16 // n_cores
    x = np.ascontiguousarray(
        inp["L0_attn_x"].reshape(16, 2048, 1024)[core * seqs_per_core:(core + 1) * seqs_per_core]
    ).reshape(-1, D)[:rows_per_core]
    m = {
        "x": x,
        "w_a": inp["L0_attn_w_a"], "w_x": inp["L0_attn_w_x"], "w_out": inp["L0_attn_w_out"],
        "mat_v_real": inp["L0_attn_mat_v_real"], "mat_v_imag": inp["L0_attn_mat_v_imag"],
        "h0_re": inp["L0_attn_last_hidden_real"], "h0_im": inp["L0_attn_last_hidden_imag"],
        "gate_w": inp["L0_ffn_gate_w"],
        "gate_bb": np.ascontiguousarray(np.broadcast_to(inp["L0_ffn_gate_b"], (128, E))),
        "w1": inp["L0_ffn_w1"], "b1": inp["L0_ffn_b1"],
        "w3": inp["L0_ffn_w3"], "b3": inp["L0_ffn_b3"],
        "w2": inp["L0_ffn_w2"], "b2": inp["L0_ffn_b2"],
        "sw1": inp["L0_ffn_sw1"], "sb1": inp["L0_ffn_sb1"],
        "sw3": inp["L0_ffn_sw3"], "sb3": inp["L0_ffn_sb3"],
        "sw2": inp["L0_ffn_sw2"], "sb2": inp["L0_ffn_sb2"],
        "eye512": np.eye(INNER, dtype=np.float32),
        "i128": np.eye(128, dtype=np.float32),
        "iota8": np.ascontiguousarray(np.broadcast_to(np.arange(E, dtype=np.float32), (128, E))),
        "ones": np.ones((1, 128), dtype=np.float32),
    }
    return {k: np.ascontiguousarray(np.asarray(v, dtype=np.float32)) for k, v in m.items()}


from concourse.bass_utils import run_bass_kernel_spmd

_NC_CACHE = {}

def kernel(**inputs):
    n_cores = 8
    rows = 4096
    if "nc" not in _NC_CACHE:
        _NC_CACHE["nc"], _ = build(rows_per_core=rows, n_chunks=None, tap_names=(),
                                   newton_iters=4)
    nc = _NC_CACHE["nc"]
    in_maps = [host_inputs(inputs, core=c, n_cores=n_cores, rows_per_core=rows)
               for c in range(n_cores)]
    res = run_bass_kernel_spmd(nc, in_maps, core_ids=list(range(n_cores)))
    out = np.stack([res.results[c]["out"].reshape(2, 2048, D) for c in range(n_cores)])
    return out.reshape(16, 2048, D).astype(np.float32)
